# version 04
# speedup vs baseline: 1.4034x; 1.2266x over previous
import contextlib
import numpy as np
import ml_dtypes
import concourse.bass as bass
import concourse.mybir as mybir
from concourse.bass_utils import run_bass_kernel_spmd

F32 = mybir.dt.float32
BF16 = mybir.dt.bfloat16
AF = mybir.ActivationFunctionType
ALU = mybir.AluOpType
AX = mybir.AxisListType
BF = ml_dtypes.bfloat16

B, S, D = 2, 8192, 1024
NCORES = 8
TPC = 2048
NTT = TPC // 128
NDC = D // 128

C_ONE, C_NEG1, C_EPS, C_WGEPS, C_LEAK, C_ZERO = 0, 1, 2, 3, 4, 5
NCONST = 8


def seq_segments(c):
    j = c % 4
    return [(512 * (j + 4 * k), 512 * (j + 4 * k) + 512) for k in range(4)]


def host_consts():
    row = np.zeros(NCONST, dtype=np.float32)
    row[C_ONE], row[C_NEG1], row[C_EPS] = 1.0, -1.0, 1e-5
    row[C_WGEPS], row[C_LEAK], row[C_ZERO] = 1e-4, 0.01, 0.0
    return np.tile(row[None, :], (128, 1))


class P:
    ENGS = ("t", "v", "s", "sp", "g")

    def __init__(self, name="mk"):
        self.nc = bass.Bass("TRN2", target_bir_lowering=False, name=name,
                            dynamic_dma_scratch_size=16384)
        self.stack = contextlib.ExitStack()
        self.q = {e: [] for e in self.ENGS}
        self.sems = {}
        self.cnt = {}
        self.sb = {}
        self.pbank = []
        self.n_ops = 0

    def sem(self, name):
        if name not in self.sems:
            self.sems[name] = self.stack.enter_context(self.nc.semaphore(name))
            self.cnt[name] = 0
        return self.sems[name]

    def sem_pool_begin(self, saved=None):
        self._pool_on = True
        if saved is not None:
            self._pool_map, self._pool_idx = saved
        else:
            self._pool_map, self._pool_idx = {}, {}
        return (self._pool_map, self._pool_idx)

    def _physname(self, name, eng=None):
        if not getattr(self, "_pool_on", False):
            return name
        if name.startswith("cc"):
            return "ZCC"
        if name in self.sems:
            return name
        if name in self._pool_map:
            return self._pool_map[name]
        assert eng is not None, f"sem {name} first used in a wait (pooling)"
        k = self._pool_idx.get(eng, 0)
        self._pool_idx[eng] = k + 1
        phys = f"Z{eng}{k}"
        self._pool_map[name] = phys
        return phys

    def sbuf(self, name, shape, dt):
        if name in self.sb:
            return self.sb[name]
        ovl = getattr(self, "overlay", None)
        if ovl is not None and name in ovl:
            h = self.nc.alloc_sbuf_tensor_at("sb_" + name, shape, dt,
                                             offset=ovl[name])
        else:
            h = self.stack.enter_context(self.nc.sbuf_tensor("sb_" + name, shape, dt))
        self.sb[name] = h
        return h

    def alloc_pbanks(self, n=8):
        for i in range(n):
            self.pbank.append(
                self.stack.enter_context(
                    self.nc.psum_tensor(f"pb{i}", [128, 512], F32)))

    def op(self, eng, fn, waits=(), inc=None, inc_by=1, incs=()):
        if inc is not None:
            inc = self._physname(inc, eng)
        incs = [(self._physname(n, eng), by) for (n, by) in incs]
        wl = [(self.sem(self._physname(n)), v) for (n, v) in waits if v > 0]
        il = []
        if not hasattr(self, "_semeng"):
            self._semeng = {}
        if inc is not None:
            il.append((self.sem(inc), inc_by))
            self.cnt[inc] += inc_by
        for n, by in incs:
            il.append((self.sem(n), by))
            self.cnt[n] += by
        for n in ([inc] if inc else []) + [n for n, _ in incs]:
            own = self._semeng.setdefault(n, eng)
            assert own == eng, f"sem {n} inc'd from {eng} and {own}"
        ret = self.cnt[inc] if inc is not None else None

        def g(e, fn=fn, wl=wl, il=il):
            for h, v in wl:
                e.wait_ge(h, v)
            r = fn(e)
            for h, by in il:
                r = r.then_inc(h, by)
        self.q[eng].append(g)
        self.n_ops += 1
        return ret

    def wait(self, eng, *wpairs):
        wl = [(self.sem(self._physname(n)), v) for (n, v) in wpairs if v > 0]

        def g(e, wl=wl):
            for h, v in wl:
                e.wait_ge(h, v)
        self.q[eng].append(g)

    def cur(self, name):
        if getattr(self, "_pool_on", False):
            if name.startswith("cc"):
                name = "ZCC"
            else:
                name = self._pool_map.get(name, name)
        return self.cnt.get(name, 0)

    def curb(self, name, eng):
        return self.cnt.get(self._physname(name, eng), 0)

    def mark(self, name):
        if not hasattr(self, "_marks"):
            self._marks = []
        self._marks.append((name, {e: len(self.q[e]) for e in self.ENGS}))

    def finish(self):
        marks = getattr(self, "_marks", [])
        with self.nc.Block() as b:
            for ename, deco in (("t", b.tensor), ("v", b.vector), ("s", b.scalar),
                                ("sp", b.sync), ("g", b.gpsimd)):
                ops = self.q[ename]
                if not ops:
                    continue
                segs = []
                prev_name, prev_idx = None, 0
                for name, idxs in marks:
                    i = idxs[ename]
                    if i > prev_idx:
                        segs.append((prev_name, prev_idx, i))
                    prev_name, prev_idx = name, i
                segs.append((prev_name, prev_idx, len(ops)))

                def runner(e, ops=ops, segs=segs):
                    for name, s, t in segs:
                        if name is None:
                            for f in ops[s:t]:
                                f(e)
                        else:
                            with self.nc.named_scope(name):
                                for f in ops[s:t]:
                                    f(e)
                deco(runner)
        self.stack.close()
        return self.nc



def emit_load_basics(P_, x_dram, consts_dram, ident_dram):
    x_sb = P_.sbuf("x", [128, NTT * 1024], F32)
    consts = P_.sbuf("consts", [128, NCONST], F32)
    ident = P_.sbuf("ident", [128, 128], F32)
    xr = x_dram.rearrange("(n p) d -> n p d", p=128)
    for n in range(NTT):
        P_.op("sp", lambda e, n=n: e.dma_start(
            out=x_sb[:, n * 1024:(n + 1) * 1024], in_=xr[n]), inc="ldx", inc_by=16)
    P_.op("sp", lambda e: e.dma_start(out=consts[:], in_=consts_dram), inc="ldx", inc_by=16)
    P_.op("sp", lambda e: e.dma_start(out=ident[:], in_=ident_dram), inc="ldx", inc_by=16)
    return x_sb, consts, ident


LDX_DONE = (NTT + 2) * 16


def emit_store_x(P_, x_sb, xout_dram, waits):
    xr = xout_dram.rearrange("(n p) d -> n p d", p=128)
    for n in range(NTT):
        P_.op("sp", lambda e, n=n: e.dma_start(
            out=xr[n], in_=x_sb[:, n * 1024:(n + 1) * 1024]),
            waits=(waits if n == 0 else ()), inc="stx", inc_by=16)
    P_.wait("sp", ("stx", P_.cur("stx")))



def emit_ln_tr(P_, tag, x_sb, xn_fm, consts, ident, first_waits=(),
               gamma_col=None, beta_col=None, tile_cb=None, fp32_cb=None,
               banks=(0, 1)):
    sv, ss, st_, se = "lnv" + tag, "lns" + tag, "ttr" + tag, "tev" + tag
    ses = "tevs" + tag
    sxn = "xn" + tag
    se_b, ses_b = P_.curb(se, "v"), P_.curb(ses, "s")
    xn2 = P_.sbuf("xn2", [128, 1024], F32)
    cols = P_.sbuf("cols", [128, 16], F32)
    csum, cm, cq, cvar, cm2, cr = (cols[:, i:i + 1] for i in range(6))
    gate_guard = [None]

    for tt in range(NTT):
        x_t = x_sb[:, tt * 1024:(tt + 1) * 1024]
        w0 = [("ldx", LDX_DONE)] + (list(first_waits) if tt == 0 else [])
        vw = [(sv, P_.cur(sv))]
        P_.op("v", lambda e, x_t=x_t: e.reduce_sum(csum, x_t, AX.X), waits=w0 + vw, inc=sv)
        P_.op("s", lambda e, x_t=x_t: e.activation(xn2[:], x_t, AF.Square, accum_out=cq),
              waits=w0 + [(ss, P_.cur(ss)), (st_, P_.cur(st_))], inc=ss)
        P_.op("v", lambda e: e.tensor_scalar(cm, csum, 1.0 / 1024, None, ALU.mult),
              waits=[(sv, P_.cur(sv))], inc=sv)
        P_.op("v", lambda e: e.tensor_mul(cm2, cm, cm), waits=[(sv, P_.cur(sv))], inc=sv)
        P_.op("v", lambda e: e.tensor_scalar(cvar, cq, 1.0 / 1024, None, ALU.mult),
              waits=[(ss, P_.cur(ss)), (sv, P_.cur(sv))], inc=sv)
        P_.op("v", lambda e: e.tensor_sub(cvar, cvar, cm2), waits=[(sv, P_.cur(sv))], inc=sv)
        P_.op("s", lambda e: e.activation(cvar, cvar, AF.Ln, bias=consts[:, C_EPS:C_EPS + 1]),
              waits=[(sv, P_.cur(sv))], inc=ss)
        P_.op("s", lambda e: e.activation(cr, cvar, AF.Exp, scale=-0.5),
              waits=[(ss, P_.cur(ss))], inc=ss)
        xn_t = xn2[:, 0:1024]
        tw = [(st_, P_.cur(st_))] if tt >= 1 else []
        P_.op("v", lambda e, x_t=x_t, xn_t=xn_t: e.tensor_scalar(
            xn_t, x_t, cm, cr, ALU.subtract, ALU.mult),
            waits=[(sv, P_.cur(sv)), (ss, P_.cur(ss))] + tw, inc=sv)
        xn_ready = P_.cur(sv)
        for dc in range(NDC):
            bview = P_.pbank[banks[dc % 2]][:, 0:128]
            idx = tt * NDC + dc
            tw2 = [(sv, xn_ready)]
            if idx >= 2:
                tw2.append((se, se_b + idx - 1))
                if fp32_cb is not None:
                    tw2.append((ses, ses_b + idx - 1))
            else:
                tw2.append((se, se_b))
                if fp32_cb is not None:
                    tw2.append((ses, ses_b))
            P_.op("t", lambda e, xn_t=xn_t, dc=dc, bview=bview: e.transpose(
                bview, xn_t[:, dc * 128:(dc + 1) * 128], ident[:]),
                waits=tw2, inc=st_)
            trc = P_.cur(st_)
            dst = xn_fm[:, dc * TPC + tt * 128: dc * TPC + (tt + 1) * 128]
            if gamma_col is None:
                P_.op("v", lambda e, dst=dst, bview=bview: e.tensor_copy(dst, bview),
                      waits=[(st_, trc)], inc=se)
            else:
                gc, bc = gamma_col(dc), beta_col(dc)
                P_.op("v", lambda e, dst=dst, bview=bview, gc=gc, bc=bc: e.tensor_scalar(
                    dst, bview, gc, bc, ALU.mult, ALU.add),
                    waits=[(st_, trc)], inc=se)
            if fp32_cb is not None:
                fp32_cb(tt, dc, bview, trc, gate_guard[0])
        w_end = [(se, P_.cur(se))] + ([(ses, P_.cur(ses))] if fp32_cb is not None else [])
        P_.op("v", lambda e: e.engine_nop(), waits=w_end, inc=sxn)
        if tile_cb is not None:
            gate_guard[0] = tile_cb(tt, P_.cur(ses))
    return sxn, P_.cur(sxn)



def emit_moe(P_, tag, x_sb, consts, ident, w1_d, w2_d, wred_d, wg_d, b1t_d, b2_d,
             first_waits=()):
    xn_fm = P_.sbuf("xn_fm", [128, NDC * TPC], BF16)
    gx = P_.sbuf("gx", [128, NDC * 128], F32)
    gcols = P_.sbuf("gcols", [128, 32], F32)
    selw = P_.sbuf("selw", [128, NTT * 8], F32)
    selwT = P_.sbuf("selwT", [8, NTT * 128], F32)
    gate8 = P_.sbuf("gate8", [128, 8], F32)
    eq8 = P_.sbuf("eq8", [128, 8], F32)
    wred_sb = P_.sbuf("wred_sb", [128, 128], F32)
    wgq = P_.sbuf("wgq", [16, 64], F32)
    redT_sb = P_.sbuf("redT_sb", [16, 128], F32)
    b1_sb = P_.sbuf("b1_sb", [128, 8 * 32], F32)
    b2_sb = P_.sbuf("b2_sb", [8, 1024], F32)
    w1buf = P_.sbuf("w1buf", [128, 2 * 8 * 512], BF16)
    w2buf = P_.sbuf("w2buf", [128, 2 * 4 * 1024], BF16)
    h_fm = P_.sbuf("h_fm", [128, 32 * 512], BF16)

    ldw = "ml" + tag
    P_.op("sp", lambda e: e.dma_start(out=wred_sb[:], in_=wred_d),
          waits=list(first_waits), inc=ldw, inc_by=16)
    P_.op("sp", lambda e: e.dma_start(out=wgq[0:8, 0:16], in_=wg_d), inc=ldw, inc_by=16)
    P_.op("sp", lambda e: e.dma_start(out=b1_sb[:], in_=b1t_d), inc=ldw, inc_by=16)
    P_.op("sp", lambda e: e.dma_start(out=b2_sb[:], in_=b2_d), inc=ldw, inc_by=16)
    ldw_done = P_.cur(ldw)

    sv, ss, st_, se = "mv" + tag, "ms" + tag, "ttr" + tag, "tev" + tag
    wgs, wgn = wgq[0:8, 0:16], wgq[0:8, 16:32]
    n2, sd, rc = wgq[0:8, 40:41], wgq[0:8, 41:42], wgq[0:8, 42:43]
    wgnT = wgq[0:16, 32:40]
    P_.op("s", lambda e: e.activation(wgq[0:8, 48:64], wgs, AF.Square, accum_out=n2),
          waits=[(ldw, ldw_done), ("ldx", LDX_DONE)], inc=ss)
    P_.op("s", lambda e: e.activation(sd, n2, AF.Ln), waits=[(ss, P_.cur(ss))], inc=ss)
    P_.op("s", lambda e: e.activation(sd, sd, AF.Exp, scale=0.5),
          waits=[(ss, P_.cur(ss))], inc=ss)
    P_.op("v", lambda e: e.tensor_scalar(sd, sd, 1e-4, None, ALU.max),
          waits=[(ss, P_.cur(ss))], inc=sv)
    P_.op("v", lambda e: e.reciprocal(rc, sd), waits=[(sv, P_.cur(sv))], inc=sv)
    P_.op("v", lambda e: e.tensor_scalar(wgn, wgs, rc, None, ALU.mult),
          waits=[(sv, P_.cur(sv))], inc=sv)
    P_.op("t", lambda e: e.transpose(P_.pbank[2][0:16, 0:8], wgn, ident[0:8, 0:8]),
          waits=[(sv, P_.cur(sv))], inc=st_)
    P_.op("v", lambda e: e.tensor_copy(wgnT, P_.pbank[2][0:16, 0:8]),
          waits=[(st_, P_.cur(st_))], inc=sv)
    wgnT_ready = P_.cur(sv)

    red_ps = P_.pbank[3]
    gps = P_.pbank[4]
    sg, gv, gs = "gt" + tag, "gv" + tag, "gs" + tag
    cgmax, cgden, cgrden, cnf, cneg = (gcols[:, i:i + 1] for i in range(5))
    red_sb = gcols[:, 8:24]

    def fp32_cb(tt, dc, bview, trc, guard):
        dst = gx[:, dc * 128:(dc + 1) * 128]
        w = [(st_, trc), ("tev" + tag, P_.cur("tev" + tag))]
        if guard is not None:
            w.append(guard)
        P_.op("s", lambda e, dst=dst, bview=bview: e.copy(dst, bview),
              waits=w, inc="tevs" + tag)

    import os as _os
    _G = 6

    def tile_cb(tt, evac_count):
        if _G < 1:
            return None
        for dc in range(NDC):
            w = [("tevs" + tag, evac_count), (ldw, ldw_done)]
            P_.op("t", lambda e, dc=dc: e.matmul(
                red_ps[:, 0:16], gx[:, dc * 128:(dc + 1) * 128],
                wred_sb[:, dc * 16:(dc + 1) * 16],
                start=(dc == 0), stop=(dc == NDC - 1)), waits=w, inc=st_)
        P_.op("v", lambda e: e.tensor_copy(red_sb, red_ps[:, 0:16]),
              waits=[(st_, P_.cur(st_)), (gv, P_.cur(gv))], inc=gv)
        if _G < 2:
            return (gv, P_.cur(gv))
        P_.op("t", lambda e: e.transpose(gps[0:16, 0:128], red_sb, ident[:]),
              waits=[(gv, P_.cur(gv))], inc=st_)
        P_.op("v", lambda e: e.tensor_copy(redT_sb[:], gps[0:16, 0:128]),
              waits=[(st_, P_.cur(st_))], inc=gv)
        if _G < 3:
            return (gv, P_.cur(gv))
        P_.op("t", lambda e: e.matmul(P_.pbank[5][:, 0:8], redT_sb[:], wgnT,
                                      start=True, stop=True),
              waits=[(gv, P_.cur(gv)), (sv, wgnT_ready)], inc=st_)
        P_.op("v", lambda e: e.tensor_copy(gate8[:], P_.pbank[5][:, 0:8]),
              waits=[(st_, P_.cur(st_))], inc=gv)
        if _G < 4:
            return (gv, P_.cur(gv))
        P_.op("v", lambda e: e.tensor_reduce(cgmax, gate8[:], AX.X, ALU.max),
              waits=[(gv, P_.cur(gv))], inc=gv)
        P_.op("v", lambda e: e.tensor_scalar(cneg, cgmax, -1.0, None, ALU.mult),
              waits=[(gv, P_.cur(gv))], inc=gv)
        P_.op("s", lambda e: e.activation(gate8[:], gate8[:], AF.Exp, bias=cneg,
                                          accum_out=cgden),
              waits=[(gv, P_.cur(gv))], inc=gs)
        P_.op("v", lambda e: e.reciprocal(cgrden, cgden), waits=[(gs, P_.cur(gs))], inc=gv)
        P_.op("v", lambda e: e.tensor_scalar(gate8[:], gate8[:], cgrden, None, ALU.mult),
              waits=[(gv, P_.cur(gv))], inc=gv)
        P_.op("v", lambda e: e.tensor_reduce(cgmax, gate8[:], AX.X, ALU.max),
              waits=[(gv, P_.cur(gv))], inc=gv)
        P_.op("v", lambda e: e.memset(cnf, 1.0), waits=[(gv, P_.cur(gv))], inc=gv)
        if _G < 5:
            return (gv, P_.cur(gv))
        for e_i in range(8):
            P_.op("v", lambda e, e_i=e_i: e.tensor_single_scalar(
                eq8[:, e_i:e_i + 1], gate8[:, e_i:e_i + 1], cgmax, ALU.is_equal),
                waits=[(gv, P_.cur(gv))], inc=gv)
        for e_i in range(8):
            sc = selw[:, tt * 8 + e_i: tt * 8 + e_i + 1]
            P_.op("v", lambda e, e_i=e_i, sc=sc: e.tensor_mul(
                sc, eq8[:, e_i:e_i + 1], cnf), waits=[(gv, P_.cur(gv))], inc=gv)
            P_.op("v", lambda e, sc=sc: e.tensor_sub(cnf, cnf, sc),
                  waits=[(gv, P_.cur(gv))], inc=gv)
        for e_i in range(8):
            sc = selw[:, tt * 8 + e_i: tt * 8 + e_i + 1]
            P_.op("v", lambda e, sc=sc: e.tensor_scalar(
                sc, sc, cgrden, None, ALU.mult), waits=[(gv, P_.cur(gv))], inc=gv)
        if _G < 6:
            return (gv, P_.cur(gv))
        P_.op("t", lambda e, tt=tt: e.transpose(
            P_.pbank[6][0:8, 0:128], selw[:, tt * 8:(tt + 1) * 8], ident[:]),
            waits=[(gv, P_.cur(gv))], inc=st_)
        P_.op("v", lambda e, tt=tt: e.tensor_copy(
            selwT[:, tt * 128:(tt + 1) * 128], P_.pbank[6][0:8, 0:128]),
            waits=[(st_, P_.cur(st_))], inc=sg)
        return (sg, P_.cur(sg))

    import os
    phase = 4
    if phase == 1:
        sxn, sxn_tgt = emit_ln_tr(P_, tag, x_sb, xn_fm, consts, ident,
                                  first_waits=first_waits, banks=(0, 1))
        P_.op("v", lambda e: e.engine_nop(), waits=[(sxn, sxn_tgt)], inc="done" + tag)
        return "done" + tag
    sxn, sxn_tgt = emit_ln_tr(P_, tag, x_sb, xn_fm, consts, ident,
                              first_waits=first_waits,
                              tile_cb=tile_cb, fp32_cb=fp32_cb, banks=(0, 1))
    if phase == 2:
        P_.op("v", lambda e: e.engine_nop(),
              waits=[(sxn, sxn_tgt), (sg, P_.cur(sg)), (gv, P_.cur(gv))],
              inc="done" + tag)
        return "done" + tag

    gw1, gw2 = "w1" + tag, "w2" + tag
    sh, sy = "hh" + tag, "yy" + tag
    w1_re = w1_d.rearrange("ei (dc p) f -> ei p dc f", p=128)
    w2_re = w2_d.rearrange("ei (fc p) d -> ei p fc d", p=128)
    nw1, nw2 = [0], [0]
    w1_done_st = []
    w2_done_st = []

    def load_w1(e_i, fb):
        k = nw1[0]
        nw1[0] += 1
        slot = k % 2
        w = ([(st_, w1_done_st[k - 2])] if k >= 2
             else list(first_waits) + [(st_, P_.cur(st_))])
        sem = gw1 + str(slot)
        for dc in range(8):
            dst = w1buf[:, slot * 4096 + dc * 512: slot * 4096 + (dc + 1) * 512]
            src = w1_re[e_i, :, dc, fb * 512:(fb + 1) * 512]
            P_.op("g", lambda e, dst=dst, src=src: e.dma_start(out=dst, in_=src),
                  waits=(w if dc == 0 else ()), inc=sem, inc_by=16)
        return slot, (sem, P_.cur(sem))

    def load_w2(e_i, fg):
        k = nw2[0]
        nw2[0] += 1
        slot = k % 2
        w = ([(st_, w2_done_st[k - 2])] if k >= 2
             else list(first_waits) + [(st_, P_.cur(st_))])
        sem = gw2 + str(slot)
        for fq in range(4):
            dst = w2buf[:, slot * 4096 + fq * 1024: slot * 4096 + (fq + 1) * 1024]
            src = w2_re[e_i, :, fg * 4 + fq, :]
            P_.op("g", lambda e, dst=dst, src=src: e.dma_start(out=dst, in_=src),
                  waits=(w if fq == 0 else ()), inc=sem, inc_by=16)
        return slot, (sem, P_.cur(sem))

    import os
    _tbs = 4
    _nes = 8
    for tb in range(_tbs):
        for e_i in range(_nes):
            for fb in range(8):
                slot, rdy = load_w1(e_i, fb)
                for fq in range(4):
                    fc = fb * 4 + fq
                    bank = P_.pbank[fc % 2]
                    for dc in range(NDC):
                        w = []
                        if dc == 0:
                            w = [rdy]
                            if fc == 0:
                                w += [(sy, P_.cur(sy)), (sxn, sxn_tgt)]
                            gelu_cnt = P_.cur(sh)
                            if gelu_cnt >= 2:
                                w.append((sh, gelu_cnt - 1))
                        P_.op("t", lambda e, bank=bank, slot=slot, dc=dc, fq=fq, tb=tb: e.matmul(
                            bank[:],
                            w1buf[:, slot * 4096 + dc * 512 + fq * 128:
                                  slot * 4096 + dc * 512 + fq * 128 + 128],
                            xn_fm[:, dc * TPC + tb * 512: dc * TPC + (tb + 1) * 512],
                            start=(dc == 0), stop=(dc == NDC - 1)),
                            waits=w, inc=st_)
                    mmc = P_.cur(st_)
                    b1c = b1_sb[:, e_i * 32 + fc: e_i * 32 + fc + 1]
                    P_.op("s", lambda e, fc=fc, bank=bank, b1c=b1c: e.activation(
                        h_fm[:, fc * 512:(fc + 1) * 512], bank[:], AF.Gelu, bias=b1c),
                        waits=[(st_, mmc)], inc=sh)
                w1_done_st.append(P_.cur(st_))
            h_done = P_.cur(sh)
            for fg in range(8):
                slot, rdy = load_w2(e_i, fg)
                for fq in range(4):
                    fc = fg * 4 + fq
                    for tt4 in range(4):
                        for do in range(2):
                            bank = P_.pbank[tt4 * 2 + do]
                            w = []
                            if fq == 0 and tt4 == 0 and do == 0:
                                w = [rdy]
                                if fg == 0:
                                    w += [(sh, h_done), (sy, P_.cur(sy))]
                                    if tb == 0 and e_i == 0:
                                        w.append((sg, P_.cur(sg)))
                            P_.op("t", lambda e, bank=bank, slot=slot, fq=fq, fc=fc,
                                  tt4=tt4, do=do: e.matmul(
                                      bank[:],
                                      h_fm[:, fc * 512 + tt4 * 128:
                                           fc * 512 + tt4 * 128 + 128],
                                      w2buf[:, slot * 4096 + fq * 1024 + do * 512:
                                            slot * 4096 + fq * 1024 + do * 512 + 512],
                                      start=(fc == 0), stop=(fc == 31)),
                                  waits=w, inc=st_)
                w2_done_st.append(P_.cur(st_))
            w2done = P_.cur(st_)
            for tt4 in range(4):
                tt = tb * 4 + tt4
                wcol = selw[:, tt * 8 + e_i: tt * 8 + e_i + 1]
                for do in range(2):
                    bank = P_.pbank[tt4 * 2 + do]
                    xsl = x_sb[:, tt * 1024 + do * 512: tt * 1024 + do * 512 + 512]
                    P_.op("v", lambda e, bank=bank, wcol=wcol, xsl=xsl: e.scalar_tensor_tensor(
                        xsl, bank[:], wcol, xsl, ALU.mult, ALU.add),
                        waits=[(st_, w2done)], inc=sy)
    if phase == 3:
        P_.op("v", lambda e: e.engine_nop(), waits=[(sy, P_.cur(sy))], inc="done" + tag)
        return "done" + tag
    b2e = "b2e" + tag
    yb = P_.pbank[0]
    for tt in range(NTT):
        for do in range(2):
            w = [(sy, P_.cur(sy)), (ldw, ldw_done), (sg, P_.cur(sg))]
            if tt > 0 or do > 0:
                w.append((b2e, P_.cur(b2e)))
            P_.op("t", lambda e, tt=tt, do=do: e.matmul(
                yb[:], selwT[:, tt * 128:(tt + 1) * 128],
                b2_sb[:, do * 512:(do + 1) * 512], start=True, stop=True),
                waits=w, inc=st_)
            xsl = x_sb[:, tt * 1024 + do * 512: tt * 1024 + do * 512 + 512]
            P_.op("v", lambda e, xsl=xsl: e.tensor_add(xsl, xsl, yb[:]),
                  waits=[(st_, P_.cur(st_))], inc=b2e)
    done = P_.op("v", lambda e: e.engine_nop(),
                 waits=[(b2e, P_.cur(b2e)), (sv, P_.cur(sv)), (ss, P_.cur(ss)),
                        (sh, P_.cur(sh)), (sy, P_.cur(sy)), (sg, P_.cur(sg)),
                        (gv, P_.cur(gv)), (gs, P_.cur(gs))],
                 inc="done" + tag)
    return "done" + tag



def build_moe_launch():
    pr = P("moe")
    nc = pr.nc
    x_d = nc.dram_tensor("x_in", [TPC, D], F32, kind="ExternalInput").ap()
    consts_d = nc.dram_tensor("consts", [128, NCONST], F32, kind="ExternalInput").ap()
    ident_d = nc.dram_tensor("ident", [128, 128], F32, kind="ExternalInput").ap()
    w1_d = nc.dram_tensor("w1", [8, 1024, 4096], F32, kind="ExternalInput").ap()
    w2_d = nc.dram_tensor("w2", [8, 4096, 1024], F32, kind="ExternalInput").ap()
    wred_d = nc.dram_tensor("wred", [128, 128], F32, kind="ExternalInput").ap()
    wg_d = nc.dram_tensor("wg", [8, 16], F32, kind="ExternalInput").ap()
    b1t_d = nc.dram_tensor("b1t", [128, 256], F32, kind="ExternalInput").ap()
    b2_d = nc.dram_tensor("b2", [8, 1024], F32, kind="ExternalInput").ap()
    xo_d = nc.dram_tensor("x_out", [TPC, D], F32, kind="ExternalOutput").ap()
    pr.alloc_pbanks(8)
    x_sb, consts, ident = emit_load_basics(pr, x_d, consts_d, ident_d)
    dsem = emit_moe(pr, "A", x_sb, consts, ident, w1_d, w2_d, wred_d, wg_d,
                    b1t_d, b2_d)
    emit_store_x(pr, x_sb, xo_d, waits=[(dsem, 1)])
    return pr.finish(), pr.n_ops


def moe_host_inputs(inp):
    b1 = np.asarray(inp["L0_ffn_b1"], np.float32)
    b1t = np.ascontiguousarray(
        b1.reshape(8, 32, 128).transpose(2, 0, 1).reshape(128, 256))
    wredt = np.ascontiguousarray(
        np.asarray(inp["L0_ffn_w_red"], np.float32).reshape(8, 128, 16)
        .transpose(1, 0, 2).reshape(128, 128))
    return {
        "consts": host_consts(),
        "ident": np.eye(128, dtype=np.float32),
        "w1": np.asarray(inp["L0_ffn_w1"], np.float32),
        "w2": np.asarray(inp["L0_ffn_w2"], np.float32),
        "wred": wredt,
        "wg": np.asarray(inp["L0_ffn_wg"], np.float32),
        "b1t": b1t,
        "b2": np.asarray(inp["L0_ffn_b2"], np.float32),
    }


def shard_rows(full):
    out = []
    for c in range(NCORES):
        b = c // 4
        parts = [full[b * S + a: b * S + bb] for (a, bb) in seq_segments(c)]
        out.append(np.ascontiguousarray(np.concatenate(parts, 0)))
    return out


def unshard_rows(shards):
    full = np.empty((B * S,) + shards[0].shape[1:], shards[0].dtype)
    for c in range(NCORES):
        b = c // 4
        for i, (a, bb) in enumerate(seq_segments(c)):
            full[b * S + a: b * S + bb] = shards[c][i * 512:(i + 1) * 512]
    return full



def emit_l1ffn(P_, tag, x_sb, consts, ident, win_d, bin_d, w1_d, b1_d, w2_d,
               b2_d, wout_d, bout_d, first_waits=()):
    xn_fm = P_.sbuf("xn_fm", [128, NDC * TPC], BF16)
    win_sb = P_.sbuf("win_sb", [128, NDC * 64], BF16)
    sm_sb = P_.sbuf("sm_sb", [64, TPC], F32)
    hbuf = P_.sbuf("hbuf", [64, 2 * TPC], F32)
    g_sb = P_.sbuf("g_sb", [64, TPC], F32)
    gbf = P_.sbuf("gbf", [64, TPC], BF16)
    small = P_.sbuf("ffn_small", [64, 64], F32)
    wout_sb = P_.sbuf("wout_sb", [64, 1024], BF16)
    bout_sb = P_.sbuf("bout_sb", [1, 1024], BF16)
    onesrow = P_.sbuf("onesrow", [1, 128], BF16)

    ld = "fl" + tag
    P_.op("g", lambda e: e.dma_start(out=win_sb[:], in_=win_d),
          waits=list(first_waits), inc=ld, inc_by=16)
    P_.op("g", lambda e: e.dma_start(out=wout_sb[:], in_=wout_d), inc=ld, inc_by=16)
    P_.op("g", lambda e: e.dma_start(out=bout_sb[:], in_=bout_d), inc=ld, inc_by=16)
    lds = "fls" + tag
    P_.op("sp", lambda e: e.dma_start(out=small[:, 0:16], in_=w1_d),
          waits=list(first_waits), inc=lds, inc_by=16)
    P_.op("sp", lambda e: e.dma_start(out=small[:, 16:32], in_=b1_d), inc=lds, inc_by=16)
    P_.op("sp", lambda e: e.dma_start(out=small[:, 32:48], in_=w2_d), inc=lds, inc_by=16)
    P_.op("sp", lambda e: e.dma_start(out=small[:, 48:49], in_=b2_d), inc=lds, inc_by=16)
    P_.op("sp", lambda e: e.dma_start(out=small[:, 49:50], in_=bin_d), inc=lds, inc_by=16)
    ld_done = P_.cur(ld)
    lds_done = P_.cur(lds)
    sv, ss, st_ = "fv" + tag, "fs" + tag, "ttr" + tag
    P_.op("v", lambda e: e.memset(onesrow[:], 1.0), waits=list(first_waits), inc=sv)

    sxn, sxn_tgt = emit_ln_tr(P_, tag, x_sb, xn_fm, consts, ident,
                              first_waits=first_waits, banks=(0, 1))

    for tb in range(4):
        for dc in range(NDC):
            w = [(ld, ld_done), (lds, lds_done), (sxn, sxn_tgt)] if dc == 0 else []
            if dc == 0 and tb > 0:
                w.append((sv, P_.cur(sv)))
            P_.op("t", lambda e, tb=tb, dc=dc: e.matmul(
                P_.pbank[2][0:64, 0:512],
                win_sb[:, dc * 64:(dc + 1) * 64],
                xn_fm[:, dc * TPC + tb * 512: dc * TPC + (tb + 1) * 512],
                start=(dc == 0), stop=(dc == NDC - 1)), waits=w, inc=st_)
        P_.op("v", lambda e, tb=tb: e.tensor_scalar(
            sm_sb[0:64, tb * 512:(tb + 1) * 512], P_.pbank[2][0:64, 0:512],
            small[0:64, 49:50], None, ALU.add),
            waits=[(st_, P_.cur(st_))], inc=sv)
    s_done = P_.cur(sv)

    for i in range(16):
        hsl = hbuf[0:64, (i % 2) * TPC:(i % 2 + 1) * TPC]
        w = [(sv, s_done)]
        if i >= 2:
            w.append((sv, P_.cur(sv)))
        P_.op("s", lambda e, hsl=hsl, i=i: e.activation(
            hsl, sm_sb[:], AF.Relu, bias=small[0:64, 16 + i:17 + i],
            scale=small[0:64, i:i + 1]), waits=w, inc=ss)
        if i == 0:
            P_.op("v", lambda e, hsl=hsl: e.tensor_scalar(
                g_sb[:], hsl, small[0:64, 32:33], None, ALU.mult),
                waits=[(ss, P_.cur(ss))], inc=sv)
        else:
            P_.op("v", lambda e, hsl=hsl, i=i: e.scalar_tensor_tensor(
                g_sb[:], hsl, small[0:64, 32 + i:33 + i], g_sb[:], ALU.mult, ALU.add),
                waits=[(ss, P_.cur(ss)), (sv, P_.cur(sv))], inc=sv)
    P_.op("v", lambda e: e.tensor_scalar(gbf[:], g_sb[:], small[0:64, 48:49],
                                         None, ALU.add),
          waits=[(sv, P_.cur(sv))], inc=sv)
    g_done = P_.cur(sv)

    ye = "ye" + tag
    for tt in range(NTT):
        for do in range(2):
            bank = P_.pbank[3 + (tt * 2 + do) % 4]
            w = [(sv, g_done)]
            if tt * 2 + do >= 4:
                w.append((ye, P_.cur(ye) - 3))
            P_.op("t", lambda e, tt=tt, do=do, bank=bank: e.matmul(
                bank[:], gbf[0:64, tt * 128:(tt + 1) * 128],
                wout_sb[0:64, do * 512:(do + 1) * 512],
                start=True, stop=False), waits=w, inc=st_)
            P_.op("t", lambda e, do=do, bank=bank: e.matmul(
                bank[:], onesrow[:], bout_sb[0:1, do * 512:(do + 1) * 512],
                start=False, stop=True), inc=st_)
            xsl = x_sb[:, tt * 1024 + do * 512: tt * 1024 + do * 512 + 512]
            P_.op("v", lambda e, xsl=xsl, bank=bank: e.tensor_add(xsl, xsl, bank[:]),
                  waits=[(st_, P_.cur(st_))], inc=ye)
    P_.op("v", lambda e: e.engine_nop(),
          waits=[(ye, P_.cur(ye)), (sv, P_.cur(sv)), (ss, P_.cur(ss))],
          inc="done" + tag)
    return "done" + tag


def build_l1ffn_launch():
    pr = P("l1ffn")
    nc = pr.nc
    x_d = nc.dram_tensor("x_in", [TPC, D], F32, kind="ExternalInput").ap()
    consts_d = nc.dram_tensor("consts_in", [128, NCONST], F32, kind="ExternalInput").ap()
    ident_d = nc.dram_tensor("ident_in", [128, 128], F32, kind="ExternalInput").ap()
    win_d = nc.dram_tensor("win", [128, NDC * 64], F32, kind="ExternalInput").ap()
    bin_d = nc.dram_tensor("binp", [64, 1], F32, kind="ExternalInput").ap()
    w1_d = nc.dram_tensor("w1p", [64, 16], F32, kind="ExternalInput").ap()
    b1_d = nc.dram_tensor("b1p", [64, 16], F32, kind="ExternalInput").ap()
    w2_d = nc.dram_tensor("w2p", [64, 16], F32, kind="ExternalInput").ap()
    b2_d = nc.dram_tensor("b2p", [64, 1], F32, kind="ExternalInput").ap()
    wout_d = nc.dram_tensor("woutp", [64, 1024], F32, kind="ExternalInput").ap()
    bout_d = nc.dram_tensor("boutp", [1, 1024], F32, kind="ExternalInput").ap()
    xo_d = nc.dram_tensor("x_out", [TPC, D], F32, kind="ExternalOutput").ap()
    pr.alloc_pbanks(8)
    x_sb, consts, ident = emit_load_basics(pr, x_d, consts_d, ident_d)
    dsem = emit_l1ffn(pr, "F", x_sb, consts, ident, win_d, bin_d, w1_d, b1_d,
                      w2_d, b2_d, wout_d, bout_d)
    emit_store_x(pr, x_sb, xo_d, waits=[(dsem, 1)])
    return pr.finish(), pr.n_ops


def l1ffn_host_inputs(inp):
    win = np.asarray(inp["L1_ffn_Win"], np.float32)
    wint = np.ascontiguousarray(
        win.reshape(8, 128, 64).transpose(1, 0, 2).reshape(128, 8 * 64))
    return {
        "consts_in": host_consts(),
        "ident_in": np.eye(128, dtype=np.float32),
        "win": wint,
        "binp": np.asarray(inp["L1_ffn_bin_"], np.float32)[:, None],
        "w1p": np.asarray(inp["L1_ffn_W1"], np.float32),
        "b1p": np.asarray(inp["L1_ffn_b1"], np.float32),
        "w2p": np.asarray(inp["L1_ffn_W2"], np.float32),
        "b2p": np.asarray(inp["L1_ffn_b2"], np.float32)[:, None],
        "woutp": np.asarray(inp["L1_ffn_Wout"], np.float32),
        "boutp": np.asarray(inp["L1_ffn_bout"], np.float32)[None, :],
    }






def emit_l0a(P_, tag, x_sb, consts, ident, io, first_waits=()):
    nc = P_.nc
    xn_fm = P_.sbuf("xn_fm", [128, NDC * TPC], BF16)
    attn = P_.sbuf("attn", [128, 8 * TPC], BF16)
    arena1 = P_.sbuf("arena1", [128, 8192], BF16)
    arena2 = P_.sbuf("arena2", [128, 8192], BF16)
    arena3 = P_.sbuf("arena3", [128, 8448], BF16)
    qw_sb = P_.sbuf("qw_sb", [128, 1024], BF16)
    q_hp = P_.sbuf("q_hp", [128, 2048], BF16)
    pt_sb = P_.sbuf("pt_sb", [128, 4 * 512], BF16)
    sq = P_.sbuf("sq", [128, 512], F32)
    qkb_sb = P_.sbuf("qkb_sb", [128, 16], F32)
    hmask_sb = P_.sbuf("hmask_sb", [128, 2], F32)
    negmc = P_.sbuf("negmc", [128, 2], F32)
    rows_sb = P_.sbuf("rows_sb", [1, 3 * 128], F32)
    obvb_sb = P_.sbuf("obvb_sb", [1, 1024], BF16)
    onesbf = P_.sbuf("onesbf", [1, 128], BF16)
    mxrow = P_.sbuf("mxrow", [1, 192], F32)
    den_sb = P_.sbuf("den_sb", [1, 1024], F32)
    kg_sb = P_.sbuf("kg_sb", [1, 64], F32)

    kag_i, vag_i, kag_o, vag_o = io["kag_i"], io["vag_i"], io["kag_o"], io["vag_o"]
    groups = [[0, 1, 2, 3], [4, 5, 6, 7]]

    ld, ldg = "al" + tag, "ag" + tag
    sv, ss, st_ = "av" + tag, "as" + tag, "ttr" + tag
    P_.op("sp", lambda e: e.dma_start(out=qkb_sb[:], in_=io["qkb"]),
          waits=list(first_waits), inc=ld, inc_by=16)
    P_.op("sp", lambda e: e.dma_start(out=hmask_sb[:], in_=io["hmask"]), inc=ld, inc_by=16)
    P_.op("sp", lambda e: e.dma_start(out=rows_sb[0:1, 0:128], in_=io["rowA"]), inc=ld, inc_by=16)
    P_.op("sp", lambda e: e.dma_start(out=rows_sb[0:1, 128:256], in_=io["rowB"]), inc=ld, inc_by=16)
    P_.op("sp", lambda e: e.dma_start(out=rows_sb[0:1, 256:384], in_=io["rowOnes"]), inc=ld, inc_by=16)
    P_.op("g", lambda e: e.dma_start(out=obvb_sb[0:1, 0:1024], in_=io["vb_row"]),
          waits=list(first_waits), inc="vbr" + tag, inc_by=16)
    ld_done, ldg_done = P_.cur(ld), P_.cur("vbr" + tag)
    P_.op("v", lambda e: e.memset(onesbf[:], 1.0), waits=list(first_waits), inc=sv)
    P_.op("v", lambda e: e.memset(arena1[:, 5120:7200], 1.0), inc=sv)
    ones_done = P_.cur(sv)

    sxn, sxn_tgt = emit_ln_tr(P_, tag, x_sb, xn_fm, consts, ident,
                              first_waits=first_waits, banks=(0, 1))

    kw_re = io["kw2d"].rearrange("(dc p) n -> dc p n", p=128)
    vw_re = io["vw2d"].rearrange("(dc p) n -> dc p n", p=128)
    qw_re = io["qw2d"].rearrange("(dc p) n -> dc p n", p=128)
    ow_re = io["ow2d"].rearrange("(dc p) n -> dc p n", p=128)

    kwslot_done = []
    norm_reads = []
    kev_done = []
    for hp in range(8):
        slot = hp % 2
        wsem = "kw" + tag + str(slot)
        w = ([(st_, kwslot_done[hp - 2])] if hp >= 2
             else list(first_waits) + [(st_, P_.cur(st_))])
        for dc in range(NDC):
            dst = arena1[:, slot * 1024 + dc * 128: slot * 1024 + dc * 128 + 128]
            P_.op("g", lambda e, dst=dst, dc=dc, hp=hp: e.dma_start(
                out=dst, in_=kw_re[dc, :, hp * 128:(hp + 1) * 128]),
                waits=(w if dc == 0 else ()), inc=wsem, inc_by=16)
        kw_rdy = (wsem, P_.cur(wsem))
        for tb in range(4):
            it = hp * 4 + tb
            bank = P_.pbank[2 + tb % 2]
            spk = "ks" + tag + str(tb % 2)
            for dc in range(NDC):
                mw = []
                if dc == 0:
                    mw = [kw_rdy, (sxn, sxn_tgt)]
                    if it >= 2:
                        mw.append((sv, kev_done[it - 2]))
                P_.op("t", lambda e, bank=bank, slot=slot, dc=dc, tb=tb: e.matmul(
                    bank[:], arena1[:, slot * 1024 + dc * 128: slot * 1024 + dc * 128 + 128],
                    xn_fm[:, dc * TPC + tb * 512: dc * TPC + (tb + 1) * 512],
                    start=(dc == 0), stop=(dc == NDC - 1)), waits=mw, inc=st_)
            mmc = P_.cur(st_)
            kst = arena1[:, 4096 + (tb % 2) * 512: 4096 + (tb % 2) * 512 + 512]
            kev_w = [(st_, mmc), (ld, ld_done)]
            if it >= 2:
                kev_w.append((spk, P_.cur(spk)))
            P_.op("v", lambda e, kst=kst, bank=bank, hp=hp: e.tensor_scalar(
                kst, bank[:], qkb_sb[:, 8 + hp:9 + hp], None, ALU.add),
                waits=kev_w, inc=sv)
            kev_done.append(P_.cur(sv))
            P_.op("s", lambda e, bank=bank: e.activation(sq[:, 0:512], bank[:],
                                                         AF.Square),
                  waits=[(sv, P_.cur(sv)), (st_, P_.cur(st_))], inc=ss)
            nw = [(ss, P_.cur(ss)), (ld, ld_done)]
            if norm_reads:
                nw.append((sv, norm_reads[-1]))
            P_.op("t", lambda e: e.matmul(P_.pbank[6][0:1, 0:512], hmask_sb[:, 0:1],
                                          sq[:, 0:512], start=True, stop=True),
                  waits=nw, inc=st_)
            P_.op("t", lambda e: e.matmul(P_.pbank[7][0:1, 0:512], hmask_sb[:, 1:2],
                                          sq[:, 0:512], start=True, stop=True),
                  inc=st_)
            nc_ = P_.cur(st_)
            c0 = 64 + (2 * hp) * 4 + tb
            c1 = 64 + (2 * hp + 1) * 4 + tb
            P_.op("v", lambda e, c0=c0: e.tensor_reduce(
                mxrow[0:1, c0:c0 + 1], P_.pbank[6][0:1, 0:512], AX.X, ALU.max),
                waits=[(st_, nc_)], inc=sv)
            P_.op("v", lambda e, c1=c1: e.tensor_reduce(
                mxrow[0:1, c1:c1 + 1], P_.pbank[7][0:1, 0:512], AX.X, ALU.max),
                inc=sv)
            norm_reads.append(P_.cur(sv))
            P_.op("sp", lambda e, hp=hp, tb=tb, kst=kst: e.dma_start(
                out=kag_i.ap()[hp * 128:(hp + 1) * 128, tb * 512:(tb + 1) * 512],
                in_=kst), waits=[(sv, P_.cur(sv) - 2)], inc=spk, inc_by=16)
        kwslot_done.append(P_.cur(st_))
    for h in range(16):
        src = mxrow[0:1, 64 + h * 4: 64 + h * 4 + 4]
        P_.op("v", lambda e, src=src, h=h: e.tensor_reduce(
            mxrow[0:1, 128 + h:129 + h], src, AX.X, ALU.max),
            waits=([(sv, P_.cur(sv))] if h == 0 else []), inc=sv)
    P_.op("v", lambda e: e.memset(den_sb[:], 0.0), waits=[(sv, P_.cur(sv))], inc=sv)
    P_.op("v", lambda e: e.tensor_copy(den_sb[0:1, 0:16], mxrow[0:1, 128:144]),
          waits=[(sv, P_.cur(sv))], inc=sv)
    P_.op("g", lambda e: e.dma_start(out=kag_i.ap()[1024:1025, 0:1024],
                                     in_=den_sb[0:1, 0:1024]),
          waits=[(sv, P_.cur(sv))], inc=ldg, inc_by=16)
    P_.op("g", lambda e: e.dma_start(out=kag_i.ap()[1024:1025, 1024:2048],
                                     in_=den_sb[0:1, 0:1024]), inc=ldg, inc_by=16)

    vstage = 5120
    vev_done = []
    for vb in range(2):
        wsem = "vw" + tag + str(vb)
        vww = [(st_, kwslot_done[-1]), (sv, ones_done)]
        if vb == 1:
            vww = [(st_, vb1_wait_st)]
        for dc in range(NDC):
            dst = arena1[:, dc * 512:(dc + 1) * 512]
            P_.op("g", lambda e, dst=dst, dc=dc, vb=vb: e.dma_start(
                out=dst, in_=vw_re[dc, :, vb * 512:(vb + 1) * 512]),
                waits=(vww if dc == 0 else ()), inc=wsem, inc_by=16)
        vw_rdy = (wsem, P_.cur(wsem))
        for tt in range(NTT):
            it = vb * NTT + tt
            bank = P_.pbank[4 + tt % 2]
            spv = "vs" + tag + str(tt % 2)
            for dc in range(NDC):
                mw = []
                if dc == 0:
                    mw = [vw_rdy, ("vbr" + tag, ldg_done)]
                    if it >= 2:
                        mw.append((sv, vev_done[it - 2]))
                P_.op("t", lambda e, bank=bank, dc=dc, tt=tt: e.matmul(
                    bank[:], xn_fm[:, dc * TPC + tt * 128: dc * TPC + (tt + 1) * 128],
                    arena1[:, dc * 512:(dc + 1) * 512],
                    start=(dc == 0), stop=False), waits=mw, inc=st_)
            P_.op("t", lambda e, bank=bank, vb=vb: e.matmul(
                bank[:], onesbf[:], obvb_sb[0:1, vb * 512:(vb + 1) * 512],
                start=False, stop=True), inc=st_)
            mmc = P_.cur(st_)
            off = vstage + (tt % 2) * 1040
            dst3 = arena1[:, off: off + 1040].rearrange(
                "p (h c) -> p h c", h=8)[:, :, 0:64]
            vev_w = [(st_, mmc)]
            if it >= 2:
                vev_w.append((spv, P_.cur(spv)))
            P_.op("v", lambda e, dst3=dst3, bank=bank: e.tensor_copy(
                dst3, bank[:].rearrange("p (h c) -> p h c", h=8)),
                waits=vev_w, inc=sv)
            vev_done.append(P_.cur(sv))
            if tt % 2 == 1:
                for h in range(8):
                    src_even = arena1[:, vstage + h * 130: vstage + h * 130 + 65]
                    src_odd = arena1[:, vstage + 1040 + h * 130: vstage + 1040 + h * 130 + 65]
                    gh = vb * 8 + h
                    ktp = tt // 2
                    P_.op("sp", lambda e, src_even=src_even, gh=gh, ktp=ktp: e.dma_start(
                        out=vag_i.ap()[gh * 128:(gh + 1) * 128, ktp * 130: ktp * 130 + 65],
                        in_=src_even),
                        waits=([("vs" + tag + "0", P_.cur(sv) * 0 + 0)] if False else
                               [(sv, P_.cur(sv))] if h == 0 else ()),
                        inc="vs" + tag + "0", inc_by=16)
                    P_.op("sp", lambda e, src_odd=src_odd, gh=gh, ktp=ktp: e.dma_start(
                        out=vag_i.ap()[gh * 128:(gh + 1) * 128, ktp * 130 + 65: ktp * 130 + 130],
                        in_=src_odd), inc="vs" + tag + "1", inc_by=16)
        if vb == 0:
            vb1_wait_st = P_.cur(st_)
    vmm_done = P_.cur(st_)
    vag_writes_done = [("vs" + tag + "0", P_.cur("vs" + tag + "0")),
                      ("vs" + tag + "1", P_.cur("vs" + tag + "1"))]
    kag_writes_done = [("ks" + tag + "0", P_.cur("ks" + tag + "0")),
                       ("ks" + tag + "1", P_.cur("ks" + tag + "1")),
                       (ldg, P_.cur(ldg))]

    cc = "cc" + tag
    P_.op("g", lambda e: e.collective_compute(
        "AllGather", ALU.bypass, replica_groups=groups,
        ins=[kag_i.ap().opt()], outs=[kag_o.ap().opt()]),
        waits=kag_writes_done + [(sv, P_.cur(sv))], inc=cc)
    P_.op("g", lambda e: e.collective_compute(
        "AllGather", ALU.bypass, replica_groups=groups,
        ins=[vag_i.ap().opt()], outs=[vag_o.ap().opt()]),
        waits=vag_writes_done, inc=cc)
    cc_done = P_.cur(cc)

    cm = "cm" + tag
    P_.op("g", lambda e: e.dma_start(out=arena1[:], in_=io["cmask16"]),
          waits=[(sv, P_.cur(sv)), ("vs" + tag + "0", P_.cur("vs" + tag + "0")),
                 ("vs" + tag + "1", P_.cur("vs" + tag + "1")),
                 ("ks" + tag + "0", P_.cur("ks" + tag + "0")),
                 ("ks" + tag + "1", P_.cur("ks" + tag + "1")),
                 (st_, vmm_done)],
          inc=cm, inc_by=16)
    cm_done = P_.cur(cm)
    kx = "kx" + tag
    for r in range(4):
        P_.op("g", lambda e, r=r: e.dma_start(
            out=kg_sb[0:1, r * 16:(r + 1) * 16],
            in_=kag_o.ap()[r * 1025 + 1024: r * 1025 + 1025, 0:16]),
            waits=([(cc, cc_done)] if r == 0 else ()), inc=kx, inc_by=16)
    kmg = mxrow[0:1, 144:160]
    P_.op("v", lambda e: e.tensor_reduce(
        kmg, kg_sb[0:1, 0:64].rearrange("o (r h) -> o h r", r=4), AX.X, ALU.max),
        waits=[(kx, P_.cur(kx))], inc=sv)
    kmg_done = P_.cur(sv)

    sp2 = "st2" + tag
    sxp = "xp" + tag
    smk = "mk" + tag
    sde = "de" + tag
    P_.sem(sde); P_.sem(sxp); P_.sem(smk)
    ql = "ql" + tag + "0"
    qflash_done_st = []
    for hp in range(8):
        wsem = "qw" + tag + "0"
        qlw = [(st_, qmm_done_st[-1])] if hp >= 1 else [(st_, P_.cur(st_))]
        for dc in range(NDC):
            dst = qw_sb[:, dc * 128: dc * 128 + 128]
            P_.op("g", lambda e, dst=dst, dc=dc, hp=hp: e.dma_start(
                out=dst, in_=qw_re[dc, :, hp * 128:(hp + 1) * 128]),
                waits=(qlw if dc == 0 else ()), inc=wsem, inc_by=16)
        qw_rdy = (wsem, P_.cur(wsem))
        if hp == 0:
            qmm_done_st = []
        qev_q = []
        for tb in range(4):
            bank = P_.pbank[tb % 2]
            for dc in range(NDC):
                mw = []
                if dc == 0:
                    mw = [qw_rdy]
                    if tb < 2:
                        mw.append((sxp, P_.cur(sxp)))
                    else:
                        mw.append((sv, qev_q[tb - 2]))
                P_.op("t", lambda e, bank=bank, dc=dc, tb=tb: e.matmul(
                    bank[:], qw_sb[:, dc * 128: dc * 128 + 128],
                    xn_fm[:, dc * TPC + tb * 512: dc * TPC + (tb + 1) * 512],
                    start=(dc == 0), stop=(dc == NDC - 1)), waits=mw, inc=st_)
            mmc = P_.cur(st_)
            P_.op("v", lambda e, bank=bank, hp=hp, tb=tb: e.tensor_scalar(
                q_hp[:, tb * 512:(tb + 1) * 512], bank[:],
                qkb_sb[:, hp:hp + 1], 0.125, ALU.add, ALU.mult),
                waits=[(st_, mmc), (ld, ld_done)], inc=sv)
            qev_q.append(P_.cur(sv))
            P_.op("s", lambda e, bank=bank: e.activation(
                sq[:, 0:512], bank[:], AF.Square),
                waits=[(sv, P_.cur(sv)), (st_, P_.cur(st_))], inc=ss)
            nw = [(ss, P_.cur(ss)), (sde, P_.cur(sde))]
            if norm_reads:
                nw.append((sv, norm_reads[-1]))
            P_.op("t", lambda e: e.matmul(P_.pbank[6][0:1, 0:512], hmask_sb[:, 0:1],
                                          sq[:, 0:512], start=True, stop=True),
                  waits=nw, inc=st_)
            P_.op("t", lambda e: e.matmul(P_.pbank[7][0:1, 0:512], hmask_sb[:, 1:2],
                                          sq[:, 0:512], start=True, stop=True),
                  inc=st_)
            nc_ = P_.cur(st_)
            P_.op("v", lambda e, tb=tb: e.tensor_reduce(
                mxrow[0:1, tb:tb + 1], P_.pbank[6][0:1, 0:512], AX.X, ALU.max),
                waits=[(st_, nc_)], inc=sv)
            P_.op("v", lambda e, tb=tb: e.tensor_reduce(
                mxrow[0:1, 4 + tb:5 + tb], P_.pbank[7][0:1, 0:512], AX.X, ALU.max),
                inc=sv)
            norm_reads.append(P_.cur(sv))
        qmm_done_st.append(P_.cur(st_))
        P_.op("v", lambda e: e.tensor_reduce(
            mxrow[0:1, 8:9], mxrow[0:1, 0:4], AX.X, ALU.max),
            waits=[(sv, P_.cur(sv)), (sv, kmg_done)], inc=sv)
        P_.op("v", lambda e: e.tensor_reduce(
            mxrow[0:1, 9:10], mxrow[0:1, 4:8], AX.X, ALU.max), inc=sv)
        P_.op("v", lambda e, hp=hp: e.tensor_mul(
            mxrow[0:1, 10:12], mxrow[0:1, 8:10], kmg[0:1, 2 * hp:2 * hp + 2]),
            waits=[(sv, P_.cur(sv))], inc=sv)
        P_.op("s", lambda e: e.activation(mxrow[0:1, 12:14], mxrow[0:1, 10:12],
                                          AF.Ln), waits=[(sv, P_.cur(sv))], inc=ss)
        P_.op("s", lambda e: e.activation(mxrow[0:1, 14:16], mxrow[0:1, 12:14],
                                          AF.Exp, scale=0.5),
              waits=[(ss, P_.cur(ss))], inc=ss)
        P_.op("v", lambda e: e.tensor_scalar(mxrow[0:1, 16:18], mxrow[0:1, 14:16],
                                             -0.125, None, ALU.mult),
              waits=[(ss, P_.cur(ss))], inc=sv)
        P_.op("t", lambda e: e.matmul(P_.pbank[6][:, 0:2], rows_sb[0:1, 256:384],
                                      mxrow[0:1, 16:18], start=True, stop=True),
              waits=[(sv, P_.cur(sv))], inc=st_)
        P_.op("v", lambda e: e.tensor_copy(negmc[:], P_.pbank[6][:, 0:2]),
              waits=[(st_, P_.cur(st_))], inc=sv)
        negm_done = P_.cur(sv)

        stw = [(st_, qflash_done_st[-1])] if qflash_done_st else [(cc, cc_done)]
        nk = 0
        for r in range(4):
            for lb in range(4):
                gq = r + 4 * lb
                P_.op("sp", lambda e, r=r, lb=lb, gq=gq, hp=hp: e.dma_start(
                    out=arena2[:, gq * 512:(gq + 1) * 512],
                    in_=kag_o.ap()[r * 1025 + hp * 128: r * 1025 + (hp + 1) * 128,
                                   lb * 512:(lb + 1) * 512]),
                    waits=(stw if nk == 0 else ()), inc=sp2, inc_by=16)
                nk += 1
        for r in range(4):
            for lb in range(4):
                gq = r + 4 * lb
                for hh in range(2):
                    h = 2 * hp + hh
                    P_.op("sp", lambda e, r=r, lb=lb, gq=gq, h=h, hh=hh: e.dma_start(
                        out=arena3[:, hh * 4224 + gq * 260: hh * 4224 + gq * 260 + 260],
                        in_=vag_o.ap()[r * 2048 + h * 128: r * 2048 + (h + 1) * 128,
                                       lb * 260:(lb + 1) * 260]),
                        inc=sp2, inc_by=16)
        strips_rdy = (sp2, P_.cur(sp2))

        for qb in range(4):
            nkt = 16 * (qb + 1)
            pvb_e, pvb_o = P_.pbank[4], P_.pbank[5]
            exps_d = {}

            def emit_pv(kt, qb=qb, nkt=nkt, pvb_e=pvb_e, pvb_o=pvb_o):
                ptA = pt_sb[:, (kt % 2) * 512:(kt % 2) * 512 + 512]
                ptB = pt_sb[:, 1024 + (kt % 2) * 512: 1024 + (kt % 2) * 512 + 512]
                wloc = kt - 16 * qb
                masked = wloc >= 0
                if masked:
                    cmt = arena1[:, wloc * 512:(wloc + 1) * 512]
                    P_.op("v", lambda e, ptA=ptA, cmt=cmt: e.tensor_mul(
                        ptA, ptA, cmt), waits=[(sxp, exps_d[kt])], inc=smk)
                    P_.op("v", lambda e, ptB=ptB, cmt=cmt: e.tensor_mul(
                        ptB, ptB, cmt), inc=smk)
                pvw = [(smk, P_.cur(smk))] if masked else [(sxp, exps_d[kt])]
                if kt == 0:
                    pvw.append((sde, P_.cur(sde)))
                P_.op("t", lambda e, kt=kt, ptA=ptA, nkt=nkt: e.matmul(
                    pvb_e[0:65, 0:512],
                    arena3[:, kt * 65:(kt + 1) * 65], ptA,
                    start=(kt == 0), stop=(kt == nkt - 1)), waits=pvw, inc=st_)
                P_.op("t", lambda e, kt=kt, ptB=ptB, nkt=nkt: e.matmul(
                    pvb_o[0:65, 0:512],
                    arena3[:, 4224 + kt * 65: 4224 + (kt + 1) * 65], ptB,
                    start=(kt == 0), stop=(kt == nkt - 1)), inc=st_)

            for kt in range(nkt):
                sA = P_.pbank[kt % 2]
                sB = P_.pbank[2 + kt % 2]
                mw = [strips_rdy, (sv, negm_done), (cm, cm_done)]
                if kt >= 2:
                    mw = [(sxp, P_.cur(sxp) - 2)]
                elif qb > 0:
                    mw.append((sxp, P_.cur(sxp)))
                P_.op("t", lambda e, sA=sA, kt=kt, qb=qb: e.matmul(
                    sA[:], arena2[0:64, kt * 128:(kt + 1) * 128],
                    q_hp[0:64, qb * 512:(qb + 1) * 512],
                    start=True, stop=True), waits=mw, inc=st_)
                P_.op("t", lambda e, sB=sB, kt=kt, qb=qb: e.matmul(
                    sB[:], arena2[64:128, kt * 128:(kt + 1) * 128],
                    q_hp[64:128, qb * 512:(qb + 1) * 512],
                    start=True, stop=True, tile_position=(64, 0)), inc=st_)
                smm = P_.cur(st_)
                ptA = pt_sb[:, (kt % 2) * 512:(kt % 2) * 512 + 512]
                ptB = pt_sb[:, 1024 + (kt % 2) * 512: 1024 + (kt % 2) * 512 + 512]
                P_.op("s", lambda e, ptA=ptA, sA=sA: e.activation(
                    ptA, sA[:], AF.Exp, bias=negmc[:, 0:1]),
                    waits=[(st_, smm)], inc=sxp)
                P_.op("s", lambda e, ptB=ptB, sB=sB: e.activation(
                    ptB, sB[:], AF.Exp, bias=negmc[:, 1:2]), inc=sxp)
                exps_d[kt] = P_.cur(sxp)
                if kt >= 1:
                    emit_pv(kt - 1)
            emit_pv(nkt - 1)
            sweep_done = P_.cur(st_)
            denw = [(st_, sweep_done)]
            if hp == 0 and qb == 0:
                denw.append((ldg, P_.cur(ldg)))
            P_.op("v", lambda e: e.tensor_copy(den_sb[0:1, 0:512],
                                               pvb_e[64:65, 0:512]),
                  waits=denw, inc=sde)
            P_.op("v", lambda e: e.tensor_copy(den_sb[0:1, 512:1024],
                                               pvb_o[64:65, 0:512]), inc=sde)
            P_.op("v", lambda e: e.reciprocal(den_sb[0:1, 0:512],
                                              den_sb[0:1, 0:512]),
                  waits=[(sde, P_.cur(sde))], inc=sde)
            P_.op("v", lambda e: e.reciprocal(den_sb[0:1, 512:1024],
                                              den_sb[0:1, 512:1024]), inc=sde)
            rd = P_.cur(sde)
            P_.op("t", lambda e: e.matmul(P_.pbank[6][:], rows_sb[0:1, 0:128],
                                          den_sb[0:1, 0:512],
                                          start=True, stop=False),
                  waits=[(sde, rd), (sv, P_.cur(sv))], inc=st_)
            P_.op("t", lambda e: e.matmul(P_.pbank[6][:], rows_sb[0:1, 128:256],
                                          den_sb[0:1, 512:1024],
                                          start=False, stop=True), inc=st_)
            exm = P_.cur(st_)
            asl = attn[:, hp * 2048 + qb * 512: hp * 2048 + qb * 512 + 512]
            P_.op("v", lambda e, asl=asl: e.tensor_copy(asl[0:64, :],
                                                        pvb_e[0:64, 0:512]),
                  waits=[(st_, exm)], inc=sde)
            P_.op("v", lambda e, asl=asl: e.tensor_copy(asl[64:128, :],
                                                        pvb_o[0:64, 0:512]),
                  inc=sde)
            P_.op("v", lambda e, asl=asl: e.tensor_mul(asl, asl, P_.pbank[6][:]),
                  waits=[(sde, P_.cur(sde))], inc=sde)
        qflash_done_st.append(P_.cur(st_))

    ow = "ow" + tag
    P_.op("g", lambda e: e.dma_start(out=obvb_sb[0:1, 0:1024], in_=io["ob_row"]),
          waits=[(st_, vmm_done)], inc=ow, inc_by=16)
    for hp in range(8):
        for dc_half in range(2):
            P_.op("g", lambda e, hp=hp, dc_half=dc_half: e.dma_start(
                out=arena2[:, hp * 1024 + dc_half * 512: hp * 1024 + dc_half * 512 + 512],
                in_=ow_re[hp, :, dc_half * 512:(dc_half + 1) * 512]),
                waits=([(st_, qflash_done_st[-1])] if hp == 0 and dc_half == 0 else ()),
                inc=ow, inc_by=16)
    ow_rdy = (ow, P_.cur(ow))
    oe = "oe" + tag
    oe_b = P_.curb(oe, "v")
    for qb in range(4):
        for tq in range(4):
            tt = qb * 4 + tq
            for mb in range(2):
                gi = (qb * 4 + tq) * 2 + mb
                bank = P_.pbank[6 + gi % 2]
                for hp in range(8):
                    mw = []
                    if hp == 0:
                        mw = [ow_rdy, (sde, P_.cur(sde))]
                        if gi >= 2:
                            mw.append((oe, oe_b + gi - 1))
                    P_.op("t", lambda e, bank=bank, hp=hp, qb=qb, tq=tq, mb=mb: e.matmul(
                        bank[:], attn[:, hp * 2048 + qb * 512 + tq * 128:
                                      hp * 2048 + qb * 512 + tq * 128 + 128],
                        arena2[:, hp * 1024 + mb * 512: hp * 1024 + mb * 512 + 512],
                        start=(hp == 0), stop=False), waits=mw, inc=st_)
                P_.op("t", lambda e, bank=bank, mb=mb: e.matmul(
                    bank[:], onesbf[:], obvb_sb[0:1, mb * 512:(mb + 1) * 512],
                    start=False, stop=True), inc=st_)
                xsl = x_sb[:, tt * 1024 + mb * 512: tt * 1024 + mb * 512 + 512]
                P_.op("v", lambda e, xsl=xsl, bank=bank: e.tensor_add(
                    xsl, xsl, bank[:]), waits=[(st_, P_.cur(st_))], inc=oe)
    P_.op("v", lambda e: e.engine_nop(),
          waits=[(oe, P_.cur(oe)), (sv, P_.cur(sv)), (ss, P_.cur(ss))],
          inc="done" + tag)
    return "done" + tag


def build_l0a_launch():
    pr = P("l0a")
    nc = pr.nc
    x_d = nc.dram_tensor("x_in", [TPC, D], F32, kind="ExternalInput").ap()
    consts_d = nc.dram_tensor("consts_in", [128, NCONST], F32, kind="ExternalInput").ap()
    ident_d = nc.dram_tensor("ident_in", [128, 128], F32, kind="ExternalInput").ap()
    io = {}
    for nm, shp in [("qw2d", [1024, 1024]), ("kw2d", [1024, 1024]),
                    ("vw2d", [1024, 1024]), ("ow2d", [1024, 1024]),
                    ("qkb", [128, 16]), ("vb_row", [1, 1024]),
                    ("ob_row", [1, 1024]), ("hmask", [128, 2]),
                    ("rowA", [1, 128]), ("rowB", [1, 128]),
                    ("rowOnes", [1, 128]), ("cmask16", [128, 16 * 512])]:
        io[nm] = nc.dram_tensor(nm, shp, F32, kind="ExternalInput").ap()
    io["kag_i"] = nc.dram_tensor("kag_i", [1025, 2048], BF16, kind="Internal")
    io["vag_i"] = nc.dram_tensor("vag_i", [2048, 1040], BF16, kind="Internal")
    io["kag_o"] = nc.dram_tensor("kag_o", [4100, 2048], BF16, kind="Internal")
    io["vag_o"] = nc.dram_tensor("vag_o", [8192, 1040], BF16, kind="Internal")
    xo_d = nc.dram_tensor("x_out", [TPC, D], F32, kind="ExternalOutput").ap()
    pr.alloc_pbanks(8)
    x_sb, consts, ident = emit_load_basics(pr, x_d, consts_d, ident_d)
    dsem = emit_l0a(pr, "Q", x_sb, consts, ident, io)
    emit_store_x(pr, x_sb, xo_d, waits=[(dsem, 1)])
    return pr.finish(), pr.n_ops


def l0a_host_inputs(inp, core):
    j = core % 4
    qw = np.asarray(inp["L0_attn_a_Q_w"], np.float32)
    kw = np.asarray(inp["L0_attn_a_K_w"], np.float32)
    vw = np.asarray(inp["L0_attn_a_V_w"], np.float32)
    owt = np.asarray(inp["L0_attn_a_O_w"], np.float32)
    qb = np.asarray(inp["L0_attn_a_Q_b"], np.float32)
    kb = np.asarray(inp["L0_attn_a_K_b"], np.float32)
    vb = np.asarray(inp["L0_attn_a_V_b"], np.float32)
    ob = np.asarray(inp["L0_attn_a_O_b"], np.float32)
    qkb = np.zeros((128, 16), np.float32)
    for hp in range(8):
        qkb[0:64, hp] = qb[2 * hp]
        qkb[64:128, hp] = qb[2 * hp + 1]
        qkb[0:64, 8 + hp] = kb[2 * hp]
        qkb[64:128, 8 + hp] = kb[2 * hp + 1]
    hm = np.zeros((128, 2), np.float32)
    hm[0:64, 0] = 1.0
    hm[64:128, 1] = 1.0
    rowA = np.zeros((1, 128), np.float32); rowA[0, 0:64] = 1.0
    rowB = np.zeros((1, 128), np.float32); rowB[0, 64:128] = 1.0
    rowOnes = np.ones((1, 128), np.float32)
    cm = np.zeros((128, 16 * 512), np.float32)
    jj = np.arange(128)[:, None]
    ii = np.arange(512)[None, :]
    for w in range(16):
        cm[:, w * 512:(w + 1) * 512] = (128 * w + jj <= 512 * j + ii)
    return {
        "consts_in": host_consts(),
        "ident_in": np.eye(128, dtype=np.float32),
        "qw2d": np.ascontiguousarray(qw.transpose(1, 0, 2).reshape(1024, 1024)),
        "kw2d": np.ascontiguousarray(kw.transpose(1, 0, 2).reshape(1024, 1024)),
        "vw2d": np.ascontiguousarray(vw.transpose(1, 0, 2).reshape(1024, 1024)),
        "ow2d": np.ascontiguousarray(owt.reshape(1024, 1024)),
        "qkb": qkb, "vb_row": vb.reshape(1, 1024), "ob_row": ob.reshape(1, 1024),
        "hmask": hm, "rowA": rowA, "rowB": rowB, "rowOnes": rowOnes,
        "cmask16": cm,
    }




def slopes16():
    start = 2.0 ** (-2.0 ** (-(np.log2(16) - 3.0)))
    return np.float64(start) ** np.arange(1, 17)


def emit_l0b(P_, tag, x_sb, consts, ident, io, first_waits=()):
    nc = P_.nc
    xn_fm = P_.sbuf("xn_fm", [128, NDC * TPC], BF16)
    attn = P_.sbuf("attn", [128, 8 * TPC], BF16)
    arena1 = P_.sbuf("arena1", [128, 8192], BF16)
    arena2 = P_.sbuf("arena2", [128, 8192], BF16)
    arena3 = P_.sbuf("arena3", [128, 8448], BF16)
    qw_sb = P_.sbuf("qw_sb", [128, 1024], BF16)
    q_hp = P_.sbuf("q_hp", [128, 2048], BF16)
    pt_sb = P_.sbuf("pt_sb", [128, 4 * 512], BF16)
    sq = P_.sbuf("sq", [128, 512], F32)
    hmask_sb = P_.sbuf("hmask_sb", [128, 2], F32)
    negmc = P_.sbuf("negmc", [128, 2], F32)
    rows_sb = P_.sbuf("rows_sb", [1, 3 * 128], F32)
    obvb_sb = P_.sbuf("obvb_sb", [1, 1024], BF16)
    onesbf = P_.sbuf("onesbf", [1, 128], BF16)
    mxrow = P_.sbuf("mxrow", [1, 192], F32)
    den_sb = P_.sbuf("den_sb", [1, 1024], F32)
    kg_sb = P_.sbuf("kg_sb", [1, 64], F32)
    bcols = P_.sbuf("bcols", [128, 65], F32)
    gjc = P_.sbuf("gjc", [128, 64], F32)
    iirow = P_.sbuf("iirow", [1, 512], F32)
    qpos = P_.sbuf("qpos", [1, 8], F32)

    kag_i, vag_i, kag_o, vag_o = io["kag_i"], io["vag_i"], io["kag_o"], io["vag_o"]
    groups = [[0, 1, 2, 3], [4, 5, 6, 7]]
    SL = slopes16()

    ld, sv, ss, st_ = "bl" + tag, "bv" + tag, "bs" + tag, "ttr" + tag
    sde, sxp, smk = "de" + tag, "xp" + tag, "mk" + tag
    P_.op("sp", lambda e: e.dma_start(out=hmask_sb[:], in_=io["hmask"]),
          waits=list(first_waits), inc=ld, inc_by=16)
    P_.op("sp", lambda e: e.dma_start(out=rows_sb[0:1, 0:128], in_=io["rowA"]), inc=ld, inc_by=16)
    P_.op("sp", lambda e: e.dma_start(out=rows_sb[0:1, 128:256], in_=io["rowB"]), inc=ld, inc_by=16)
    P_.op("sp", lambda e: e.dma_start(out=rows_sb[0:1, 256:384], in_=io["rowOnes"]), inc=ld, inc_by=16)
    P_.op("sp", lambda e: e.dma_start(out=gjc[:], in_=io["gjcols"]), inc=ld, inc_by=16)
    P_.op("sp", lambda e: e.dma_start(out=iirow[:], in_=io["iirow"]), inc=ld, inc_by=16)
    P_.op("sp", lambda e: e.dma_start(out=qpos[0:1, 0:4], in_=io["qpos"]), inc=ld, inc_by=16)
    P_.op("g", lambda e: e.dma_start(out=obvb_sb[:], in_=io["bproj_row"]),
          waits=list(first_waits), inc="vbr" + tag, inc_by=16)
    ld_done = P_.cur(ld)
    vbr_done = P_.cur("vbr" + tag)
    P_.op("v", lambda e: e.memset(onesbf[:], 1.0), waits=list(first_waits), inc=sv)
    P_.op("v", lambda e: e.memset(arena1[:, 5120:6160], 1.0), inc=sv)
    ones_done = P_.cur(sv)

    sxn, sxn_tgt = emit_ln_tr(P_, tag, x_sb, xn_fm, consts, ident,
                              first_waits=first_waits, banks=(0, 1))

    kvw_re = io["kvw2d"].rearrange("(dc p) n -> dc p n", p=128)
    qw_re = io["qw2d"].rearrange("(dc p) n -> dc p n", p=128)
    pw_re = io["wproj2d"].rearrange("(dc p) n -> dc p n", p=128)

    kvw = "kvw" + tag
    for dc in range(NDC):
        P_.op("g", lambda e, dc=dc: e.dma_start(
            out=arena1[:, dc * 512:(dc + 1) * 512], in_=kvw_re[dc]),
            waits=(list(first_waits) + [(st_, P_.cur(st_))] if dc == 0 else ()),
            inc=kvw, inc_by=16)
    kvw_done = P_.cur(kvw)

    norm_reads = []
    kev_done = []
    for kvp in range(2):
        for tb in range(4):
            it = kvp * 4 + tb
            bank = P_.pbank[2 + tb % 2]
            spk = "ks" + tag + str(tb % 2)
            for dc in range(NDC):
                mw = []
                if dc == 0:
                    mw = [(kvw, kvw_done), (sxn, sxn_tgt)]
                    if it >= 2:
                        mw.append((sv, kev_done[it - 2]))
                P_.op("t", lambda e, bank=bank, kvp=kvp, dc=dc, tb=tb: e.matmul(
                    bank[:], arena1[:, dc * 512 + kvp * 128: dc * 512 + kvp * 128 + 128],
                    xn_fm[:, dc * TPC + tb * 512: dc * TPC + (tb + 1) * 512],
                    start=(dc == 0), stop=(dc == NDC - 1)), waits=mw, inc=st_)
            mmc = P_.cur(st_)
            kst = arena1[:, 4096 + (tb % 2) * 512: 4096 + (tb % 2) * 512 + 512]
            kev_w = [(st_, mmc)]
            if it >= 2:
                kev_w.append((spk, P_.cur(spk)))
            P_.op("v", lambda e, kst=kst, bank=bank: e.tensor_copy(kst, bank[:]),
                  waits=kev_w, inc=sv)
            kev_done.append(P_.cur(sv))
            P_.op("s", lambda e, bank=bank: e.activation(sq[:, 0:512], bank[:],
                                                         AF.Square),
                  waits=[(sv, P_.cur(sv)), (st_, P_.cur(st_))], inc=ss)
            nw = [(ss, P_.cur(ss)), (ld, ld_done)]
            if norm_reads:
                nw.append((sv, norm_reads[-1]))
            P_.op("t", lambda e: e.matmul(P_.pbank[6][0:1, 0:512], hmask_sb[:, 0:1],
                                          sq[:, 0:512], start=True, stop=True),
                  waits=nw, inc=st_)
            P_.op("t", lambda e: e.matmul(P_.pbank[7][0:1, 0:512], hmask_sb[:, 1:2],
                                          sq[:, 0:512], start=True, stop=True),
                  inc=st_)
            nc_ = P_.cur(st_)
            c0 = 64 + (2 * kvp) * 4 + tb
            c1 = 64 + (2 * kvp + 1) * 4 + tb
            P_.op("v", lambda e, c0=c0: e.tensor_reduce(
                mxrow[0:1, c0:c0 + 1], P_.pbank[6][0:1, 0:512], AX.X, ALU.max),
                waits=[(st_, nc_)], inc=sv)
            P_.op("v", lambda e, c1=c1: e.tensor_reduce(
                mxrow[0:1, c1:c1 + 1], P_.pbank[7][0:1, 0:512], AX.X, ALU.max),
                inc=sv)
            norm_reads.append(P_.cur(sv))
            P_.op("sp", lambda e, kvp=kvp, tb=tb, kst=kst: e.dma_start(
                out=kag_i.ap()[kvp * 128:(kvp + 1) * 128, tb * 512:(tb + 1) * 512],
                in_=kst), waits=[(sv, P_.cur(sv) - 2)], inc=spk, inc_by=16)
    for kv4 in range(4):
        src = mxrow[0:1, 64 + kv4 * 4: 64 + kv4 * 4 + 4]
        P_.op("v", lambda e, src=src, kv4=kv4: e.tensor_reduce(
            mxrow[0:1, 128 + kv4:129 + kv4], src, AX.X, ALU.max),
            waits=([(sv, P_.cur(sv))] if kv4 == 0 else []), inc=sv)
    P_.op("v", lambda e: e.memset(den_sb[:], 0.0), waits=[(sv, P_.cur(sv))], inc=sv)
    P_.op("v", lambda e: e.tensor_copy(den_sb[0:1, 0:4], mxrow[0:1, 128:132]),
          waits=[(sv, P_.cur(sv))], inc=sv)
    ldg = "ag" + tag
    P_.op("g", lambda e: e.dma_start(out=kag_i.ap()[256:257, 0:1024],
                                     in_=den_sb[0:1, 0:1024]),
          waits=[(sv, P_.cur(sv))], inc=ldg, inc_by=16)
    P_.op("g", lambda e: e.dma_start(out=kag_i.ap()[256:257, 1024:2048],
                                     in_=den_sb[0:1, 0:1024]), inc=ldg, inc_by=16)

    vstage = 5120
    vev_done = []
    kproj_done = P_.cur(st_)
    for tt in range(NTT):
        bank = P_.pbank[4 + tt % 2]
        spv = "vs" + tag + str(tt % 2)
        for dc in range(NDC):
            mw = []
            if dc == 0:
                mw = [(kvw, kvw_done), (sv, ones_done)]
                if tt >= 2:
                    mw.append((sv, vev_done[tt - 2]))
            P_.op("t", lambda e, bank=bank, dc=dc, tt=tt: e.matmul(
                bank[0:128, 0:256],
                xn_fm[:, dc * TPC + tt * 128: dc * TPC + (tt + 1) * 128],
                arena1[:, dc * 512 + 256: dc * 512 + 512],
                start=(dc == 0), stop=(dc == NDC - 1)), waits=mw, inc=st_)
        mmc = P_.cur(st_)
        off = vstage + (tt % 2) * 520
        dst3 = arena1[:, off: off + 520].rearrange("p (h c) -> p h c", h=4)[:, :, 0:64]
        vev_w = [(st_, mmc)]
        if tt >= 2:
            vev_w.append((spv, P_.cur(spv)))
        P_.op("v", lambda e, dst3=dst3, bank=bank: e.tensor_copy(
            dst3, bank[0:128, 0:256].rearrange("p (h c) -> p h c", h=4)),
            waits=vev_w, inc=sv)
        vev_done.append(P_.cur(sv))
        if tt % 2 == 1:
            for h in range(4):
                se_ = arena1[:, vstage + h * 130: vstage + h * 130 + 65]
                so_ = arena1[:, vstage + 520 + h * 130: vstage + 520 + h * 130 + 65]
                ktp = tt // 2
                P_.op("sp", lambda e, se_=se_, h=h, ktp=ktp: e.dma_start(
                    out=vag_i.ap()[h * 128:(h + 1) * 128, ktp * 130: ktp * 130 + 65],
                    in_=se_), waits=([(sv, P_.cur(sv))] if h == 0 else ()),
                    inc="vs" + tag + "0", inc_by=16)
                P_.op("sp", lambda e, so_=so_, h=h, ktp=ktp: e.dma_start(
                    out=vag_i.ap()[h * 128:(h + 1) * 128, ktp * 130 + 65: ktp * 130 + 130],
                    in_=so_), inc="vs" + tag + "1", inc_by=16)
    vmm_done = P_.cur(st_)

    cc = "cc" + tag
    P_.op("g", lambda e: e.collective_compute(
        "AllGather", ALU.bypass, replica_groups=groups,
        ins=[kag_i.ap().opt()], outs=[kag_o.ap().opt()]),
        waits=[("ks" + tag + "0", P_.cur("ks" + tag + "0")),
               ("ks" + tag + "1", P_.cur("ks" + tag + "1")),
               (ldg, P_.cur(ldg))], inc=cc)
    P_.op("g", lambda e: e.collective_compute(
        "AllGather", ALU.bypass, replica_groups=groups,
        ins=[vag_i.ap().opt()], outs=[vag_o.ap().opt()]),
        waits=[("vs" + tag + "0", P_.cur("vs" + tag + "0")),
               ("vs" + tag + "1", P_.cur("vs" + tag + "1"))], inc=cc)
    cc_done = P_.cur(cc)

    kx = "kx" + tag
    for r in range(4):
        P_.op("g", lambda e, r=r: e.dma_start(
            out=kg_sb[0:1, r * 16:r * 16 + 4],
            in_=kag_o.ap()[r * 257 + 256: r * 257 + 257, 0:4]),
            waits=([(cc, cc_done)] if r == 0 else ()), inc=kx, inc_by=16)
    kmg = mxrow[0:1, 144:148]
    P_.op("v", lambda e: e.tensor_reduce(
        kmg, kg_sb[0:1, 0:64].rearrange("o (r h) -> o h r", r=4)[:, 0:4, :],
        AX.X, ALU.max), waits=[(kx, P_.cur(kx))], inc=sv)
    kmg_done = P_.cur(sv)

    sp2 = "st2" + tag
    at_ = "at" + tag
    flash_done_st = []
    for kvh in range(4):
        stw = [(st_, flash_done_st[-1])] if flash_done_st else [(cc, cc_done), (st_, vmm_done)]
        P_.op("v", lambda e: e.memset(arena2[64:65, 0:8192], 1.0),
              waits=([(sv, kmg_done)] if kvh == 0 else [(sxp, P_.cur(sxp))]), inc=sv)
        ones_k = P_.cur(sv)
        nk = 0
        for r in range(4):
            for lb in range(4):
                gq = r + 4 * lb
                P_.op("sp", lambda e, r=r, lb=lb, gq=gq, kvh=kvh: e.dma_start(
                    out=arena2[0:64, gq * 512:(gq + 1) * 512],
                    in_=kag_o.ap()[r * 257 + kvh * 64: r * 257 + (kvh + 1) * 64,
                                   lb * 512:(lb + 1) * 512]),
                    waits=(stw if nk == 0 else ()), inc=sp2, inc_by=16)
                nk += 1
                P_.op("sp", lambda e, r=r, lb=lb, gq=gq, kvh=kvh: e.dma_start(
                    out=arena3[:, gq * 260:(gq + 1) * 260],
                    in_=vag_o.ap()[r * 512 + kvh * 128: r * 512 + (kvh + 1) * 128,
                                   lb * 260:(lb + 1) * 260]),
                    inc=sp2, inc_by=16)
        strips_rdy = (sp2, P_.cur(sp2))

        for hh in range(2):
            hA = kvh + 8 * hh
            hB = kvh + 4 + 8 * hh
            wsem = "qw" + tag + "0"
            qlw = [(st_, P_.cur(st_))]
            for dc in range(NDC):
                P_.op("g", lambda e, dc=dc, hA=hA, hB=hB: e.dma_start(
                    out=qw_sb[:, dc * 128: dc * 128 + 64],
                    in_=qw_re[dc, :, hA * 64:(hA + 1) * 64]),
                    waits=(qlw if dc == 0 else ()), inc=wsem, inc_by=16)
                P_.op("g", lambda e, dc=dc, hA=hA, hB=hB: e.dma_start(
                    out=qw_sb[:, dc * 128 + 64: dc * 128 + 128],
                    in_=qw_re[dc, :, hB * 64:(hB + 1) * 64]), inc=wsem, inc_by=16)
            qw_rdy = (wsem, P_.cur(wsem))
            qev = []
            for tb in range(4):
                bank = P_.pbank[tb % 2]
                for dc in range(NDC):
                    mw = []
                    if dc == 0:
                        mw = [qw_rdy, (sxp, P_.cur(sxp))]
                        if tb >= 2:
                            mw.append((sv, qev[tb - 2]))
                    P_.op("t", lambda e, bank=bank, dc=dc, tb=tb: e.matmul(
                        bank[:], qw_sb[:, dc * 128: dc * 128 + 128],
                        xn_fm[:, dc * TPC + tb * 512: dc * TPC + (tb + 1) * 512],
                        start=(dc == 0), stop=(dc == NDC - 1)), waits=mw, inc=st_)
                mmc = P_.cur(st_)
                P_.op("v", lambda e, bank=bank, tb=tb: e.tensor_scalar(
                    q_hp[0:64, tb * 512:(tb + 1) * 512], bank[0:64, :],
                    0.125, None, ALU.mult), waits=[(st_, mmc)], inc=sv)
                P_.op("v", lambda e, bank=bank, tb=tb: e.tensor_scalar(
                    arena3[0:64, 4224 + tb * 512: 4224 + (tb + 1) * 512],
                    bank[64:128, :], 0.125, None, ALU.mult), inc=sv)
                P_.op("s", lambda e, bank=bank: e.activation(
                    sq[:, 0:512], bank[:], AF.Square), waits=[(sv, P_.cur(sv))], inc=ss)
                nw = [(ss, P_.cur(ss)), (sde, P_.cur(sde))]
                if norm_reads:
                    nw.append((sv, norm_reads[-1]))
                P_.op("t", lambda e: e.matmul(P_.pbank[6][0:1, 0:512], hmask_sb[:, 0:1],
                                              sq[:, 0:512], start=True, stop=True),
                      waits=nw, inc=st_)
                P_.op("t", lambda e: e.matmul(P_.pbank[7][0:1, 0:512], hmask_sb[:, 1:2],
                                              sq[:, 0:512], start=True, stop=True),
                      inc=st_)
                nc_ = P_.cur(st_)
                P_.op("v", lambda e, tb=tb: e.tensor_reduce(
                    mxrow[0:1, tb:tb + 1], P_.pbank[6][0:1, 0:512], AX.X, ALU.max),
                    waits=[(st_, nc_)], inc=sv)
                P_.op("v", lambda e, tb=tb: e.tensor_reduce(
                    mxrow[0:1, 4 + tb:5 + tb], P_.pbank[7][0:1, 0:512], AX.X, ALU.max),
                    inc=sv)
                norm_reads.append(P_.cur(sv))
                qev.append(P_.cur(sv))
            P_.op("v", lambda e: e.tensor_reduce(
                mxrow[0:1, 8:9], mxrow[0:1, 0:4], AX.X, ALU.max),
                waits=[(sv, P_.cur(sv)), (sv, kmg_done)], inc=sv)
            P_.op("v", lambda e: e.tensor_reduce(
                mxrow[0:1, 9:10], mxrow[0:1, 4:8], AX.X, ALU.max), inc=sv)
            P_.op("v", lambda e, kvh=kvh: e.tensor_scalar(
                mxrow[0:1, 10:12], mxrow[0:1, 8:10], kmg[0:1, kvh:kvh + 1],
                None, ALU.mult), waits=[(sv, P_.cur(sv))], inc=sv)
            P_.op("s", lambda e: e.activation(mxrow[0:1, 12:14], mxrow[0:1, 10:12],
                                              AF.Ln), waits=[(sv, P_.cur(sv))], inc=ss)
            P_.op("s", lambda e: e.activation(mxrow[0:1, 14:16], mxrow[0:1, 12:14],
                                              AF.Exp, scale=0.5),
                  waits=[(ss, P_.cur(ss))], inc=ss)
            P_.op("v", lambda e: e.tensor_scalar(mxrow[0:1, 16:18], mxrow[0:1, 14:16],
                                                 -0.125, None, ALU.mult),
                  waits=[(ss, P_.cur(ss))], inc=sv)
            P_.op("t", lambda e: e.matmul(P_.pbank[6][:, 0:2], rows_sb[0:1, 256:384],
                                          mxrow[0:1, 16:18], start=True, stop=True),
                  waits=[(sv, P_.cur(sv)), (sde, P_.cur(sde))], inc=st_)
            P_.op("v", lambda e: e.tensor_copy(negmc[:], P_.pbank[6][:, 0:2]),
                  waits=[(st_, P_.cur(st_))], inc=sv)
            negm_done = P_.cur(sv)

            for hx, h in ((0, hA), (1, hB)):
                qbuf = q_hp if hx == 0 else None
                slope8 = float(SL[h] / 8.0)
                if hx == 0:
                    qrow = q_hp[64:65, 0:2048]
                else:
                    qrow = arena3[64:65, 4224:6272]
                for r4 in range(4):
                    P_.op("v", lambda e, qrow=qrow, r4=r4, slope8=slope8: e.tensor_scalar(
                        qrow[0:1, r4 * 512:(r4 + 1) * 512], iirow[0:1, 0:512],
                        -slope8, None, ALU.mult),
                        waits=([(sv, negm_done), (ld, ld_done)] if r4 == 0 else []),
                        inc=sv)
                qrow_done = P_.cur(sv)
                P_.op("g", lambda e, h=h: e.dma_start(
                    out=arena1[:, 0:2944], in_=io["alibiT"][h]),
                    waits=[(smk, P_.cur(smk))] +
                          ([(st_, vmm_done),
                            ("ks" + tag + "0", P_.cur("ks" + tag + "0")),
                            ("ks" + tag + "1", P_.cur("ks" + tag + "1"))]
                           if kvh == 0 and hh == 0 and hx == 0 else []),
                    inc=at_, inc_by=16)
                at_done = P_.cur(at_)
                for qb in range(4):
                    P_.op("v", lambda e, qb=qb, slope8=slope8: e.tensor_scalar(
                        mxrow[0:1, 20:21], qpos[0:1, qb:qb + 1], -slope8, None,
                        ALU.mult), waits=[(sv, P_.cur(sv)), (ld, ld_done)], inc=sv)
                    P_.op("v", lambda e, hx=hx: e.tensor_add(
                        mxrow[0:1, 20:21], mxrow[0:1, 20:21],
                        mxrow[0:1, 16 + hx:17 + hx]), waits=[(sv, P_.cur(sv))], inc=sv)
                    P_.op("t", lambda e: e.matmul(
                        P_.pbank[6][:, 4:5], rows_sb[0:1, 256:384],
                        mxrow[0:1, 20:21], start=True, stop=True),
                        waits=[(sv, P_.cur(sv)), (sde, P_.cur(sde))], inc=st_)
                    P_.op("v", lambda e: e.tensor_copy(bcols[:, 64:65],
                                                       P_.pbank[6][:, 4:5]),
                          waits=[(st_, P_.cur(st_))], inc=sv)
                    P_.op("v", lambda e, slope8=slope8: e.tensor_scalar(
                        bcols[:, 0:64], gjc[:], slope8, bcols[:, 64:65],
                        ALU.mult, ALU.add),
                        waits=[(sv, P_.cur(sv))], inc=sv)
                    bcols_done = P_.cur(sv)
                    k = qb
                    pvb = P_.pbank[4]
                    exps_done = {}

                    def emit_pv(kt, k=k, pvb=pvb, at_done=at_done, hx=hx):
                        win = 16 * k <= kt < 16 * k + 20
                        pt = pt_sb[:, (kt % 3) * 512:(kt % 3) * 512 + 512]
                        if win:
                            o = 2432 - 128 * (kt - 16 * k)
                            P_.op("v", lambda e, pt=pt, o=o: e.tensor_mul(
                                pt, pt, arena1[:, o:o + 512]),
                                waits=[(sxp, exps_done[kt]), (at_, at_done)],
                                inc=smk)
                        pvw = ([(smk, P_.cur(smk))] if win
                               else [(sxp, exps_done[kt])])
                        if kt == 0:
                            pvw.append((sde, P_.cur(sde)))
                        P_.op("t", lambda e, kt=kt, pt=pt, pvb=pvb: e.matmul(
                            pvb[0:65, 0:512], arena3[:, kt * 65:(kt + 1) * 65], pt,
                            start=(kt == 0), stop=(kt == 63)), waits=pvw, inc=st_)

                    for kt in range(64):
                        sbk = P_.pbank[kt % 2]
                        past = kt < 16 * k
                        mw = [(sv, bcols_done)]
                        if kt < 2:
                            if hh == 0 and hx == 0 and qb == 0:
                                mw.append(strips_rdy)
                            mw.append((sxp, P_.cur(sxp)))
                        else:
                            mw = [(sxp, P_.cur(sxp) - 1), (sv, bcols_done)]
                        KK = 65 if past else 64
                        if hx == 0:
                            qap = q_hp[0:KK, qb * 512:(qb + 1) * 512]
                        else:
                            qap = arena3[0:KK, 4224 + qb * 512: 4224 + (qb + 1) * 512]
                        P_.op("t", lambda e, sbk=sbk, kt=kt, KK=KK, qap=qap: e.matmul(
                            sbk[:], arena2[0:KK, kt * 128:(kt + 1) * 128], qap,
                            start=True, stop=True), waits=mw, inc=st_)
                        smm = P_.cur(st_)
                        pt = pt_sb[:, (kt % 3) * 512:(kt % 3) * 512 + 512]
                        bias = bcols[:, kt:kt + 1] if past else negmc[:, hx:hx + 1]
                        P_.op("s", lambda e, pt=pt, sbk=sbk, bias=bias: e.activation(
                            pt, sbk[:], AF.Exp, bias=bias),
                            waits=[(st_, smm)], inc=sxp)
                        exps_done[kt] = P_.cur(sxp)
                        if kt >= 1:
                            emit_pv(kt - 1)
                    emit_pv(63)
                    sweep_done = P_.cur(st_)
                    denw = [(st_, sweep_done)]
                    if kvh == 0 and hh == 0 and hx == 0 and qb == 0:
                        denw.append((ldg, P_.cur(ldg)))
                    P_.op("v", lambda e: e.tensor_copy(den_sb[0:1, 0:512],
                                                       pvb[64:65, 0:512]),
                          waits=denw, inc=sde)
                    P_.op("v", lambda e: e.reciprocal(den_sb[0:1, 0:512],
                                                      den_sb[0:1, 0:512]),
                          waits=[(sde, P_.cur(sde))], inc=sde)
                    rd = P_.cur(sde)
                    rowm = rows_sb[0:1, 0:128] if h % 2 == 0 else rows_sb[0:1, 128:256]
                    P_.op("t", lambda e, rowm=rowm: e.matmul(
                        P_.pbank[6][:], rowm, den_sb[0:1, 0:512],
                        start=True, stop=True),
                        waits=[(sde, rd), (sv, P_.cur(sv))], inc=st_)
                    exm = P_.cur(st_)
                    hp = h // 2
                    r0 = (h % 2) * 64
                    asl = attn[r0:r0 + 64, hp * 2048 + qb * 512: hp * 2048 + qb * 512 + 512]
                    P_.op("v", lambda e, asl=asl: e.tensor_copy(asl, pvb[0:64, 0:512]),
                          waits=[(st_, exm)], inc=sde)
                    P_.op("v", lambda e, asl=asl, r0=r0: e.tensor_mul(
                        asl, asl, P_.pbank[6][r0:r0 + 64, 0:512]),
                        waits=[(sde, P_.cur(sde))], inc=sde)
        flash_done_st.append(P_.cur(st_))

    ow = "ow" + tag
    for hp in range(8):
        for half in range(2):
            P_.op("g", lambda e, hp=hp, half=half: e.dma_start(
                out=arena2[:, hp * 1024 + half * 512: hp * 1024 + half * 512 + 512],
                in_=pw_re[hp, :, half * 512:(half + 1) * 512]),
                waits=([(st_, flash_done_st[-1])] if hp == 0 and half == 0 else ()),
                inc=ow, inc_by=16)
    ow_rdy = (ow, P_.cur(ow))
    oe = "oe" + tag
    oe_b = P_.curb(oe, "v")
    for tt in range(NTT):
        qb, tq = tt // 4, tt % 4
        for mb in range(2):
            gi = tt * 2 + mb
            bank = P_.pbank[6 + gi % 2]
            for hp in range(8):
                mw = []
                if hp == 0:
                    mw = [ow_rdy, (sde, P_.cur(sde)), ("vbr" + tag, vbr_done)]
                    if gi >= 2:
                        mw.append((oe, oe_b + gi - 1))
                P_.op("t", lambda e, bank=bank, hp=hp, tt=tt, mb=mb: e.matmul(
                    bank[:], attn[:, hp * 2048 + tt * 128: hp * 2048 + tt * 128 + 128],
                    arena2[:, hp * 1024 + mb * 512: hp * 1024 + mb * 512 + 512],
                    start=(hp == 0), stop=False), waits=mw, inc=st_)
            P_.op("t", lambda e, bank=bank, mb=mb: e.matmul(
                bank[:], onesbf[:], obvb_sb[0:1, mb * 512:(mb + 1) * 512],
                start=False, stop=True), inc=st_)
            xsl = x_sb[:, tt * 1024 + mb * 512: tt * 1024 + mb * 512 + 512]
            P_.op("v", lambda e, xsl=xsl, bank=bank: e.tensor_add(
                xsl, xsl, bank[:]), waits=[(st_, P_.cur(st_))], inc=oe)
    P_.op("v", lambda e: e.engine_nop(),
          waits=[(oe, P_.cur(oe)), (sv, P_.cur(sv)), (ss, P_.cur(ss))],
          inc="done" + tag)
    return "done" + tag


def build_l0b_launch():
    pr = P("l0b")
    nc = pr.nc
    x_d = nc.dram_tensor("x_in", [TPC, D], F32, kind="ExternalInput").ap()
    consts_d = nc.dram_tensor("consts_in", [128, NCONST], F32, kind="ExternalInput").ap()
    ident_d = nc.dram_tensor("ident_in", [128, 128], F32, kind="ExternalInput").ap()
    io = {}
    for nm, shp in [("qw2d", [1024, 1024]), ("kvw2d", [1024, 512]),
                    ("wproj2d", [1024, 1024]), ("bproj_row", [1, 1024]),
                    ("hmask", [128, 2]), ("rowA", [1, 128]), ("rowB", [1, 128]),
                    ("rowOnes", [1, 128]), ("gjcols", [128, 64]),
                    ("iirow", [1, 512]), ("qpos", [1, 4])]:
        io[nm] = nc.dram_tensor(nm, shp, F32, kind="ExternalInput").ap()
    io["alibiT"] = nc.dram_tensor("alibiT", [16, 128, 2944], F32, kind="ExternalInput").ap()
    io["kag_i"] = nc.dram_tensor("kag_i", [257, 2048], BF16, kind="Internal")
    io["vag_i"] = nc.dram_tensor("vag_i", [512, 1040], BF16, kind="Internal")
    io["kag_o"] = nc.dram_tensor("kag_o", [1028, 2048], BF16, kind="Internal")
    io["vag_o"] = nc.dram_tensor("vag_o", [2048, 1040], BF16, kind="Internal")
    xo_d = nc.dram_tensor("x_out", [TPC, D], F32, kind="ExternalOutput").ap()
    pr.alloc_pbanks(8)
    x_sb, consts, ident = emit_load_basics(pr, x_d, consts_d, ident_d)
    dsem = emit_l0b(pr, "W", x_sb, consts, ident, io)
    emit_store_x(pr, x_sb, xo_d, waits=[(dsem, 1)])
    return pr.finish(), pr.n_ops


def l0b_host_inputs(inp, core):
    j = core % 4
    wq = np.asarray(inp["L0_attn_b_Wq"], np.float32)
    wkv = np.asarray(inp["L0_attn_b_Wkv"], np.float32)
    wproj = np.asarray(inp["L0_attn_b_Wproj"], np.float32)
    bproj = np.asarray(inp["L0_attn_b_bproj"], np.float32)
    hm = np.zeros((128, 2), np.float32)
    hm[0:64, 0] = 1.0
    hm[64:128, 1] = 1.0
    rowA = np.zeros((1, 128), np.float32); rowA[0, 0:64] = 1.0
    rowB = np.zeros((1, 128), np.float32); rowB[0, 64:128] = 1.0
    rowOnes = np.ones((1, 128), np.float32)
    jj = np.arange(128, dtype=np.float32)[:, None]
    ktv = np.arange(64, dtype=np.float32)[None, :]
    gj = 128.0 * ktv + jj
    iirow = np.arange(512, dtype=np.float32)[None, :]
    qpos = np.array([[512.0 * (j + 4 * k) for k in range(4)]], np.float32)
    SL = slopes16()
    dlt = np.arange(-2047, 2560, dtype=np.float64)
    aT = np.zeros((16, 128, 2944), np.float32)
    jjl = np.arange(128)[:, None]
    cc = np.arange(2944)[None, :]
    darg = jjl - cc - 512 * j + 2432
    for h in range(16):
        g = np.where(darg <= 0, np.exp(SL[h] * darg / 8.0), 1.0)
        aT[h] = g.astype(np.float32)
    return {
        "consts_in": host_consts(),
        "ident_in": np.eye(128, dtype=np.float32),
        "qw2d": np.ascontiguousarray(wq.T),
        "kvw2d": np.ascontiguousarray(wkv.T),
        "wproj2d": np.ascontiguousarray(wproj.T),
        "bproj_row": bproj.reshape(1, 1024),
        "hmask": hm, "rowA": rowA, "rowB": rowB, "rowOnes": rowOnes,
        "gjcols": gj.astype(np.float32), "iirow": iirow, "qpos": qpos,
        "alibiT": aT,
    }




def emit_l1a(P_, tag0, x_sb, consts, ident, io, first_waits=()):
    nc = P_.nc
    xn_fm = P_.sbuf("xn_fm", [128, NDC * TPC], BF16)
    attn = P_.sbuf("attn", [128, 8 * TPC], BF16)
    w1buf = P_.sbuf("w1buf", [128, 2 * 8 * 512], BF16)
    w2buf = P_.sbuf("w2buf", [128, 2 * 4 * 1024], BF16)
    agc = P_.sbuf("agc", [128, 4160], F32)
    kfm = P_.sbuf("kfm", [128, 2048], F32)
    vfm = P_.sbuf("vfm", [128, 2048], F32)
    numb = P_.sbuf("numb", [128, 8 * 128 + 16], F32)
    ctxb = P_.sbuf("ctxb", [128, 8 * 128], BF16)
    etile = P_.sbuf("etile", [128, 2 * 128], BF16)
    vtile = P_.sbuf("vtile", [128, 2 * 128], BF16)
    lnwb = P_.sbuf("lnwb", [128, 192], F32)
    borow = P_.sbuf("borow", [1, 1024], BF16)
    onesbf = P_.sbuf("onesbf", [1, 128], BF16)
    cols2 = P_.sbuf("cols2", [128, 24], F32)
    qs = P_.sbuf("xn2", [128, 1024], F32)

    ag_i, ag_o = io["l1a_ag_i"], io["l1a_ag_o"]
    groups = [[0, 1, 2, 3], [4, 5, 6, 7]]

    ld = "al" + tag0
    P_.op("sp", lambda e: e.dma_start(out=lnwb[:], in_=io["lnwb"]),
          waits=list(first_waits), inc=ld, inc_by=16)
    P_.op("v", lambda e: e.memset(onesbf[:], 1.0), waits=list(first_waits),
          inc="on" + tag0)
    ld_done = P_.cur(ld)
    ones_done = P_.cur("on" + tag0)

    wq_re = io["l1a_wq"].rearrange("i (dc p) n -> i dc p n", p=128)
    wk_re = io["l1a_wk"].rearrange("i (dc p) n -> i dc p n", p=128)
    wv_re = io["l1a_wv"].rearrange("i (dc p) n -> i dc p n", p=128)
    wo_re = io["l1a_wo"].rearrange("i (dc p) n -> i dc p n", p=128)
    w1_re = io["l1a_w1"].rearrange("i (dc p) f -> i p dc f", p=128)
    w2_re = io["l1a_w2"].rearrange("i (fc p) d -> i p fc d", p=128)

    st_ = "ttr" + tag0

    xsv = "xsv" + tag0
    P_.op("sp", lambda e: e.dma_start(out=io["l1a_xsave"].ap()[:], in_=x_sb[:]),
          waits=[("ldx", LDX_DONE)] + list(first_waits), inc=xsv, inc_by=16)
    xsv_done = P_.cur(xsv)
    cols = P_.sbuf("cols", [128, 16], F32)
    pv, ps = "plv" + tag0, "pls" + tag0
    for tt in range(NTT):
        x_t = x_sb[:, tt * 1024:(tt + 1) * 1024]
        w0 = [("ldx", LDX_DONE)] + (list(first_waits) if tt == 0 else [])
        P_.op("v", lambda e, x_t=x_t: e.reduce_sum(cols[:, 0:1], x_t, AX.X),
              waits=w0 + [(pv, P_.cur(pv))], inc=pv)
        P_.op("s", lambda e, x_t=x_t: e.activation(qs[:], x_t, AF.Square,
                                                   accum_out=cols[:, 2:3]),
              waits=w0 + [(ps, P_.cur(ps))], inc=ps)
        P_.op("v", lambda e: e.tensor_scalar(cols[:, 1:2], cols[:, 0:1],
                                             1.0 / 1024, None, ALU.mult),
              waits=[(pv, P_.cur(pv))], inc=pv)
        P_.op("v", lambda e: e.tensor_mul(cols[:, 3:4], cols[:, 1:2], cols[:, 1:2]),
              waits=[(pv, P_.cur(pv))], inc=pv)
        P_.op("v", lambda e: e.tensor_scalar(cols[:, 4:5], cols[:, 2:3],
                                             1.0 / 1024, None, ALU.mult),
              waits=[(ps, P_.cur(ps)), (pv, P_.cur(pv))], inc=pv)
        P_.op("v", lambda e: e.tensor_sub(cols[:, 4:5], cols[:, 4:5], cols[:, 3:4]),
              waits=[(pv, P_.cur(pv))], inc=pv)
        P_.op("s", lambda e: e.activation(cols[:, 4:5], cols[:, 4:5], AF.Ln,
                                          bias=consts[:, C_EPS:C_EPS + 1]),
              waits=[(pv, P_.cur(pv))], inc=ps)
        P_.op("s", lambda e: e.activation(cols[:, 5:6], cols[:, 4:5], AF.Exp,
                                          scale=-0.5), waits=[(ps, P_.cur(ps))], inc=ps)
        aw = [(pv, P_.cur(pv)), (ps, P_.cur(ps))]
        if tt == 0:
            aw.append((xsv, xsv_done))
        P_.op("v", lambda e, x_t=x_t: e.tensor_scalar(
            x_t, x_t, cols[:, 1:2], cols[:, 5:6], ALU.subtract, ALU.mult),
            waits=aw, inc=pv)
    prev_done = [(pv, P_.cur(pv))]

    st_kv5 = [0]
    ctx5 = [0]
    dn_b = P_.curb("done" + tag0, "v")
    sv, ss = "av" + tag0, "as" + tag0
    lntag = tag0 + "N"
    for li in range(6):
        tag = tag0 + str(li)
        st0 = P_.curb(st_, "t")
        g1 = lambda dc, li=li: lnwb[:, li * 32 + 0 * 8 + dc: li * 32 + 0 * 8 + dc + 1]
        b1 = lambda dc, li=li: lnwb[:, li * 32 + 1 * 8 + dc: li * 32 + 1 * 8 + dc + 1]
        g2 = lambda dc, li=li: lnwb[:, li * 32 + 2 * 8 + dc: li * 32 + 2 * 8 + dc + 1]
        b2 = lambda dc, li=li: lnwb[:, li * 32 + 3 * 8 + dc: li * 32 + 3 * 8 + dc + 1]

        sxn, sxn_tgt = emit_ln_tr(P_, lntag, x_sb, xn_fm, consts, ident,
                                  first_waits=prev_done + [
                                      (ld, ld_done), (sv, P_.cur(sv)),
                                      (ss, P_.cur(ss)),
                                      ("hh" + tag0, P_.cur("hh" + tag0)),
                                      ("yy" + tag0, P_.cur("yy" + tag0)),
                                      ("oe" + tag0, P_.cur("oe" + tag0))],
                                  gamma_col=g1, beta_col=b1, banks=(0, 1))

        kv_done_st = []
        for c in range(8):
            slot = c % 2
            wsem = "kw" + tag0 + str(slot)
            w = ([(st_, kv_done_st[c - 2])] if c >= 2
                 else [(st_, st0)] + (list(first_waits) if li == 0 else []))
            for dc in range(NDC):
                P_.op("g", lambda e, dc=dc, c=c, slot=slot, li=li: e.dma_start(
                    out=w1buf[:, slot * 4096 + dc * 256: slot * 4096 + dc * 256 + 128],
                    in_=wk_re[li, dc, :, c * 128:(c + 1) * 128]),
                    waits=(w if dc == 0 else ()), inc=wsem, inc_by=16)
                P_.op("g", lambda e, dc=dc, c=c, slot=slot, li=li: e.dma_start(
                    out=w1buf[:, slot * 4096 + dc * 256 + 128: slot * 4096 + dc * 256 + 256],
                    in_=wv_re[li, dc, :, c * 128:(c + 1) * 128]),
                    inc=wsem, inc_by=16)
            kv_rdy = (wsem, P_.cur(wsem))
            kev = []
            for tb in range(4):
                bank = P_.pbank[2 + tb % 2]
                for dc in range(NDC):
                    mw = []
                    if dc == 0:
                        mw = [kv_rdy, (sxn, sxn_tgt)]
                        if tb >= 2:
                            mw.append((sv, kev[tb - 2]))
                        elif c > 0 or li > 0:
                            mw.append((sv, P_.cur(sv)))
                    P_.op("t", lambda e, bank=bank, slot=slot, dc=dc, tb=tb: e.matmul(
                        bank[:], w1buf[:, slot * 4096 + dc * 256: slot * 4096 + dc * 256 + 128],
                        xn_fm[:, dc * TPC + tb * 512: dc * TPC + (tb + 1) * 512],
                        start=(dc == 0), stop=(dc == NDC - 1)), waits=mw, inc=st_)
                P_.op("v", lambda e, bank=bank, tb=tb: e.tensor_copy(
                    kfm[:, tb * 512:(tb + 1) * 512], bank[:]),
                    waits=[(st_, P_.cur(st_))], inc=sv)
                kev.append(P_.cur(sv))
            P_.op("v", lambda e: e.tensor_reduce(cols2[:, 0:1], kfm[:], AX.X, ALU.max),
                  waits=[(sv, P_.cur(sv))], inc=sv)
            P_.op("v", lambda e: e.tensor_scalar(cols2[:, 1:2], cols2[:, 0:1], -1.0,
                                                 None, ALU.mult),
                  waits=[(sv, P_.cur(sv))], inc=sv)
            P_.op("s", lambda e, c=c: e.activation(
                kfm[:], kfm[:], AF.Exp, bias=cols2[:, 1:2],
                accum_out=numb[:, 1032 + c: 1033 + c]),
                waits=[(sv, P_.cur(sv)), (ss, P_.cur(ss))], inc=ss)
            P_.op("v", lambda e, c=c: e.tensor_copy(numb[:, 1024 + c:1025 + c],
                                                    cols2[:, 0:1]),
                  waits=[(sv, P_.cur(sv))], inc=sv)
            vev = []
            for tb in range(4):
                bank = P_.pbank[2 + tb % 2]
                for dc in range(NDC):
                    mw = []
                    if dc == 0:
                        mw = [kv_rdy, (sv, P_.cur(sv)), (ss, P_.cur(ss))]
                        if tb >= 2:
                            mw.append((sv, vev[tb - 2]))
                    P_.op("t", lambda e, bank=bank, slot=slot, dc=dc, tb=tb: e.matmul(
                        bank[:], w1buf[:, slot * 4096 + dc * 256 + 128:
                                       slot * 4096 + dc * 256 + 256],
                        xn_fm[:, dc * TPC + tb * 512: dc * TPC + (tb + 1) * 512],
                        start=(dc == 0), stop=(dc == NDC - 1)), waits=mw, inc=st_)
                P_.op("v", lambda e, bank=bank, tb=tb: e.tensor_copy(
                    vfm[:, tb * 512:(tb + 1) * 512], bank[:]),
                    waits=[(st_, P_.cur(st_))], inc=sv)
                vev.append(P_.cur(sv))
            for tt in range(NTT):
                bke = P_.pbank[tt % 2]
                trw = [(sv, P_.cur(sv)), (ss, P_.cur(ss))]
                P_.op("t", lambda e, bke=bke, tt=tt: e.transpose(
                    bke[:, 0:128], kfm[:, tt * 128:(tt + 1) * 128], ident[:]),
                    waits=trw, inc=st_)
                P_.op("v", lambda e, bke=bke, tt=tt: e.tensor_copy(
                    etile[:, (tt % 2) * 128:(tt % 2) * 128 + 128], bke[:, 0:128]),
                    waits=[(st_, P_.cur(st_))], inc=sv)
                P_.op("t", lambda e, bke=bke, tt=tt: e.transpose(
                    bke[:, 128:256], vfm[:, tt * 128:(tt + 1) * 128], ident[:]),
                    inc=st_)
                P_.op("v", lambda e, bke=bke, tt=tt: e.tensor_copy(
                    vtile[:, (tt % 2) * 128:(tt % 2) * 128 + 128], bke[:, 128:256]),
                    waits=[(st_, P_.cur(st_))], inc=sv)
                P_.op("t", lambda e, tt=tt: e.matmul(
                    P_.pbank[5][:, 0:128],
                    etile[:, (tt % 2) * 128:(tt % 2) * 128 + 128],
                    vtile[:, (tt % 2) * 128:(tt % 2) * 128 + 128],
                    start=(tt == 0), stop=(tt == NTT - 1)),
                    waits=[(sv, P_.cur(sv))] +
                          ([("nv" + tag0, P_.cur("nv" + tag0))] if tt == 0 else []),
                    inc=st_)
            P_.op("v", lambda e, c=c: e.tensor_copy(numb[:, c * 128:(c + 1) * 128],
                                                    P_.pbank[5][:, 0:128]),
                  waits=[(st_, P_.cur(st_))], inc="nv" + tag0)
            kv_done_st.append(P_.cur(st_))
        if li == 5:
            st_kv5[0] = P_.cur(st_)
        num_done = P_.cur("nv" + tag0)

        spg = "ks" + tag0 + "0"
        P_.op("sp", lambda e: e.dma_start(out=ag_i.ap()[:, 0:1040],
                                          in_=numb[:, 0:1040]),
              waits=[("nv" + tag0, num_done), (sv, P_.cur(sv)), (ss, P_.cur(ss))],
              inc=spg, inc_by=16)
        cc = "cc" + tag0
        P_.op("g", lambda e: e.collective_compute(
            "AllGather", ALU.bypass, replica_groups=groups,
            ins=[ag_i.ap().opt()], outs=[ag_o.ap().opt()]),
            waits=[(spg, P_.cur(spg))], inc=cc)
        cc_done = P_.cur(cc)

        for nb in range(2):
            wsem = "vw" + tag0 + str(nb)
            for dc in range(NDC):
                P_.op("g", lambda e, dc=dc, nb=nb, li=li: e.dma_start(
                    out=w2buf[:, nb * 4096 + dc * 512: nb * 4096 + (dc + 1) * 512],
                    in_=wq_re[li, dc, :, nb * 512:(nb + 1) * 512]),
                    waits=([(st_, st0)] if dc == 0 else ()), inc=wsem, inc_by=16)
        qw_rdy = [("vw" + tag0 + "0", P_.cur("vw" + tag0 + "0")),
                  ("vw" + tag0 + "1", P_.cur("vw" + tag0 + "1"))]
        for tt in range(NTT):
            for nb in range(2):
                bank = P_.pbank[2 + nb]
                for dc in range(NDC):
                    mw = []
                    if dc == 0:
                        mw = list(qw_rdy)
                        mw.append((sv, P_.cur(sv)))
                    P_.op("t", lambda e, bank=bank, nb=nb, dc=dc, tt=tt: e.matmul(
                        bank[:], xn_fm[:, dc * TPC + tt * 128: dc * TPC + (tt + 1) * 128],
                        w2buf[:, nb * 4096 + dc * 512: nb * 4096 + (dc + 1) * 512],
                        start=(dc == 0), stop=(dc == NDC - 1)), waits=mw, inc=st_)
                P_.op("v", lambda e, bank=bank, nb=nb: e.tensor_copy(
                    qs[:, nb * 512:(nb + 1) * 512], bank[:]),
                    waits=[(st_, P_.cur(st_))], inc=sv)
            for h8 in range(8):
                P_.op("v", lambda e, h8=h8: e.tensor_reduce(
                    cols2[:, 8 + h8:9 + h8], qs[:, h8 * 128:(h8 + 1) * 128],
                    AX.X, ALU.max), waits=[(sv, P_.cur(sv))], inc=sv)
            P_.op("v", lambda e: e.tensor_scalar(cols2[:, 8:16], cols2[:, 8:16],
                                                 -1.0, None, ALU.mult),
                  waits=[(sv, P_.cur(sv))], inc=sv)
            for h8 in range(8):
                P_.op("v", lambda e, h8=h8: e.tensor_scalar(
                    qs[:, h8 * 128:(h8 + 1) * 128], qs[:, h8 * 128:(h8 + 1) * 128],
                    cols2[:, 8 + h8:9 + h8], None, ALU.add),
                    waits=[(sv, P_.cur(sv))], inc=sv)
            P_.op("s", lambda e: e.activation(qs[:], qs[:], AF.Exp),
                  waits=[(sv, P_.cur(sv)), (ss, P_.cur(ss))], inc=ss)
            for h8 in range(8):
                P_.op("v", lambda e, h8=h8: e.tensor_reduce(
                    cols2[:, 16 + h8:17 + h8], qs[:, h8 * 128:(h8 + 1) * 128],
                    AX.X, ALU.add),
                    waits=[(sv, P_.cur(sv))] +
                          ([(ss, P_.cur(ss))] if h8 == 0 else []), inc=sv)
            P_.op("v", lambda e: e.reciprocal(cols2[:, 16:24], cols2[:, 16:24]),
                  waits=[(sv, P_.cur(sv))], inc=sv)
            for h8 in range(8):
                P_.op("v", lambda e, h8=h8: e.tensor_scalar(
                    qs[:, h8 * 128:(h8 + 1) * 128], qs[:, h8 * 128:(h8 + 1) * 128],
                    cols2[:, 16 + h8:17 + h8], None, ALU.mult),
                    waits=[(sv, P_.cur(sv))], inc=sv)
            qn_ready = P_.cur(sv)
            for c in range(8):
                bank = P_.pbank[2 + c % 2]
                trw = [(sv, qn_ready)]
                if c >= 2 or tt > 0:
                    trw.append((sv, P_.cur(sv)))
                P_.op("t", lambda e, bank=bank, c=c, tt=tt: e.transpose(
                    bank[:, 0:128], qs[:, c * 128:(c + 1) * 128], ident[:]),
                    waits=trw, inc=st_)
                P_.op("v", lambda e, bank=bank, c=c, tt=tt: e.tensor_copy(
                    attn[:, c * TPC + tt * 128: c * TPC + (tt + 1) * 128],
                    bank[:, 0:128]), waits=[(st_, P_.cur(st_))], inc=sv)
        qn_done = P_.cur(sv)

        kx = "kx" + tag0
        for r in range(4):
            kw = []
            if r == 0:
                kw = [(cc, cc_done)]
                if li > 0:
                    kw.append(("done" + tag0, dn_b + li))
            P_.op("sp", lambda e, r=r: e.dma_start(
                out=agc[:, r * 1040:(r + 1) * 1040],
                in_=ag_o.ap()[r * 128:(r + 1) * 128, 0:1040]),
                waits=kw, inc=kx, inc_by=16)
        kx_done = P_.cur(kx)
        for c in range(8):
            P_.op("v", lambda e, c=c: e.tensor_tensor(
                cols2[:, 2:3], agc[:, 1024 + c:1025 + c],
                agc[:, 1040 + 1024 + c:1040 + 1025 + c], ALU.max),
                waits=[(kx, kx_done), (sv, P_.cur(sv))], inc=sv)
            for r in range(2, 4):
                P_.op("v", lambda e, r=r, c=c: e.tensor_tensor(
                    cols2[:, 2:3], cols2[:, 2:3],
                    agc[:, r * 1040 + 1024 + c: r * 1040 + 1025 + c], ALU.max),
                    waits=[(sv, P_.cur(sv))], inc=sv)
            P_.op("v", lambda e: e.tensor_scalar(cols2[:, 3:4], cols2[:, 2:3], -1.0,
                                                 None, ALU.mult),
                  waits=[(sv, P_.cur(sv))], inc=sv)
            for r in range(4):
                P_.op("s", lambda e, r=r, c=c: e.activation(
                    cols2[:, 4 + r:5 + r], agc[:, r * 1040 + 1024 + c: r * 1040 + 1025 + c],
                    AF.Exp, bias=cols2[:, 3:4]),
                    waits=[(sv, P_.cur(sv)), (ss, P_.cur(ss))], inc=ss)
            P_.op("v", lambda e, c=c: e.tensor_scalar(
                cols2[:, 20:21], agc[:, 1032 + c:1033 + c], cols2[:, 4:5], None,
                ALU.mult), waits=[(ss, P_.cur(ss))], inc=sv)
            for r in range(1, 4):
                P_.op("v", lambda e, r=r, c=c: e.scalar_tensor_tensor(
                    cols2[:, 20:21], agc[:, r * 1040 + 1032 + c: r * 1040 + 1033 + c],
                    cols2[:, 4 + r:5 + r], cols2[:, 20:21], ALU.mult, ALU.add),
                    waits=[(sv, P_.cur(sv))], inc=sv)
            P_.op("v", lambda e: e.reciprocal(cols2[:, 21:22], cols2[:, 20:21]),
                  waits=[(sv, P_.cur(sv))], inc=sv)
            P_.op("v", lambda e, c=c: e.tensor_scalar(
                qs[:, 0:128], agc[:, c * 128:(c + 1) * 128], cols2[:, 4:5], None,
                ALU.mult), waits=[(sv, P_.cur(sv))], inc=sv)
            for r in range(1, 4):
                P_.op("v", lambda e, r=r, c=c: e.scalar_tensor_tensor(
                    qs[:, 0:128], agc[:, r * 1040 + c * 128: r * 1040 + (c + 1) * 128],
                    cols2[:, 4 + r:5 + r], qs[:, 0:128], ALU.mult, ALU.add),
                    waits=[(sv, P_.cur(sv))], inc=sv)
            P_.op("v", lambda e, c=c: e.tensor_scalar(
                ctxb[:, c * 128:(c + 1) * 128], qs[:, 0:128], cols2[:, 21:22],
                None, ALU.mult), waits=[(sv, P_.cur(sv))], inc=sv)
        ctx_done = P_.cur(sv)
        if li == 5:
            ctx5[0] = ctx_done
        for c in range(8):
            for tb in range(4):
                bank = P_.pbank[2 + tb % 2]
                mw = [(sv, ctx_done)]
                if c * 4 + tb >= 2:
                    mw.append((sv, P_.cur(sv)))
                P_.op("t", lambda e, bank=bank, c=c, tb=tb: e.matmul(
                    bank[:], ctxb[:, c * 128:(c + 1) * 128],
                    attn[:, c * TPC + tb * 512: c * TPC + (tb + 1) * 512],
                    start=True, stop=True), waits=mw, inc=st_)
                P_.op("v", lambda e, bank=bank, c=c, tb=tb: e.tensor_copy(
                    xn_fm[:, c * TPC + tb * 512: c * TPC + (tb + 1) * 512], bank[:]),
                    waits=[(st_, P_.cur(st_))], inc=sv)
        attnfm_done = P_.cur(sv)

        for nb in range(2):
            wsem = "vw" + tag0 + str(nb)
            for dc in range(NDC):
                P_.op("g", lambda e, dc=dc, nb=nb, li=li: e.dma_start(
                    out=w2buf[:, nb * 4096 + dc * 512: nb * 4096 + (dc + 1) * 512],
                    in_=wo_re[li, dc, :, nb * 512:(nb + 1) * 512]),
                    waits=([(st_, P_.cur(st_))] if dc == 0 else ()), inc=wsem, inc_by=16)
        ow_rdy = [("vw" + tag0 + "0", P_.cur("vw" + tag0 + "0")),
                  ("vw" + tag0 + "1", P_.cur("vw" + tag0 + "1"))]
        P_.op("g", lambda e, li=li: e.dma_start(out=borow[:], in_=io["l1a_bo"][li]),
              waits=[(st_, st0)], inc="bo" + tag0, inc_by=16)
        bo_rdy = P_.cur("bo" + tag0)
        oe = "oe" + tag0
        oe_b = P_.curb(oe, "v")
        for tt in range(NTT):
            for nb in range(2):
                gi = tt * 2 + nb
                bank = P_.pbank[6 + gi % 2]
                for dc in range(NDC):
                    mw = []
                    if dc == 0:
                        mw = ow_rdy + [(sv, attnfm_done), ("bo" + tag0, bo_rdy),
                                       ("on" + tag0, ones_done)]
                        if gi >= 2:
                            mw.append((oe, oe_b + gi - 1))
                    P_.op("t", lambda e, bank=bank, nb=nb, dc=dc, tt=tt: e.matmul(
                        bank[:], xn_fm[:, dc * TPC + tt * 128: dc * TPC + (tt + 1) * 128],
                        w2buf[:, nb * 4096 + dc * 512: nb * 4096 + (dc + 1) * 512],
                        start=(dc == 0), stop=False), waits=mw, inc=st_)
                P_.op("t", lambda e, bank=bank, nb=nb: e.matmul(
                    bank[:], onesbf[:], borow[0:1, nb * 512:(nb + 1) * 512],
                    start=False, stop=True), inc=st_)
                xsl = x_sb[:, tt * 1024 + nb * 512: tt * 1024 + nb * 512 + 512]
                P_.op("v", lambda e, xsl=xsl, bank=bank: e.tensor_add(
                    xsl, xsl, bank[:]), waits=[(st_, P_.cur(st_))], inc=oe)
        oproj_done = P_.cur(oe)

        sxn2, sxn2_tgt = emit_ln_tr(P_, lntag, x_sb, xn_fm, consts, ident,
                                    first_waits=[
                                        (oe, oproj_done), (sv, P_.cur(sv)),
                                        (ss, P_.cur(ss)),
                                        ("hh" + tag0, P_.cur("hh" + tag0)),
                                        ("yy" + tag0, P_.cur("yy" + tag0))],
                                    gamma_col=g2, beta_col=b2, banks=(0, 1))
        nw1, nw2 = [0], [0]
        w1_done_st, w2_done_st = [], []
        sh, sy = "hh" + tag0, "yy" + tag0
        sh_b = P_.curb(sh, "v")
        for tb in range(4):
            for fb in range(8):
                k = nw1[0]; nw1[0] += 1
                slot = k % 2
                wsem = "kw" + tag0 + str(slot)
                w = [(st_, w1_done_st[k - 2])] if k >= 2 else [(st_, P_.cur(st_))]
                for dc in range(NDC):
                    P_.op("g", lambda e, dc=dc, fb=fb, slot=slot, li=li: e.dma_start(
                        out=w1buf[:, slot * 4096 + dc * 512: slot * 4096 + (dc + 1) * 512],
                        in_=w1_re[li, :, dc, fb * 512:(fb + 1) * 512]),
                        waits=(w if dc == 0 else ()), inc=wsem, inc_by=16)
                w1_rdy = (wsem, P_.cur(wsem))
                for fq in range(4):
                    fc = fb * 4 + fq
                    bank = P_.pbank[fc % 2]
                    for dc in range(NDC):
                        mw = []
                        if dc == 0:
                            mw = [w1_rdy, (sxn2, sxn2_tgt)]
                            gc = P_.cur(sh)
                            if gc - sh_b >= 2:
                                mw.append((sh, gc - 1))
                            if fc == 0 and fb == 0:
                                mw.append((sy, P_.cur(sy)))
                                mw.append((sv, P_.cur(sv)))
                        P_.op("t", lambda e, bank=bank, slot=slot, dc=dc, fq=fq, tb=tb: e.matmul(
                            bank[:], w1buf[:, slot * 4096 + dc * 512 + fq * 128:
                                           slot * 4096 + dc * 512 + fq * 128 + 128],
                            xn_fm[:, dc * TPC + tb * 512: dc * TPC + (tb + 1) * 512],
                            start=(dc == 0), stop=(dc == NDC - 1)), waits=mw, inc=st_)
                    mmc = P_.cur(st_)
                    hsl = attn[:, fc * 512:(fc + 1) * 512]
                    P_.op("v", lambda e, bank=bank: e.tensor_scalar(
                        qs[:, 0:512], bank[:], 0.0, 0.01, ALU.min, ALU.mult),
                        waits=[(st_, mmc), (sv, P_.cur(sv)), (sh, P_.cur(sh))],
                        inc=sv)
                    P_.op("v", lambda e, bank=bank, hsl=hsl: e.scalar_tensor_tensor(
                        hsl, bank[:], 0.0, qs[:, 0:512], ALU.max, ALU.add),
                        waits=[(sv, P_.cur(sv))], inc=sh)
                w1_done_st.append(P_.cur(st_))
            h_done = P_.cur(sh)
            for fg in range(8):
                k = nw2[0]; nw2[0] += 1
                slot = k % 2
                wsem = "vw" + tag0 + str(slot)
                w = [(st_, w2_done_st[k - 2])] if k >= 2 else [(st_, P_.cur(st_))]
                for fq in range(4):
                    P_.op("g", lambda e, fq=fq, fg=fg, slot=slot, li=li: e.dma_start(
                        out=w2buf[:, slot * 4096 + fq * 1024: slot * 4096 + (fq + 1) * 1024],
                        in_=w2_re[li, :, fg * 4 + fq, :]),
                        waits=(w if fq == 0 else ()), inc=wsem, inc_by=16)
                w2_rdy = (wsem, P_.cur(wsem))
                for fq in range(4):
                    fc = fg * 4 + fq
                    for tt4 in range(4):
                        for do in range(2):
                            bank = P_.pbank[tt4 * 2 + do]
                            mw = []
                            if fq == 0 and tt4 == 0 and do == 0:
                                mw = [w2_rdy]
                                if fg == 0:
                                    mw += [(sh, h_done), (sy, P_.cur(sy))]
                            P_.op("t", lambda e, bank=bank, slot=slot, fq=fq, fc=fc,
                                  tt4=tt4, do=do: e.matmul(
                                      bank[:],
                                      attn[:, fc * 512 + tt4 * 128: fc * 512 + tt4 * 128 + 128],
                                      w2buf[:, slot * 4096 + fq * 1024 + do * 512:
                                            slot * 4096 + fq * 1024 + do * 512 + 512],
                                      start=(fc == 0), stop=(fc == 31)),
                                  waits=mw, inc=st_)
                w2_done_st.append(P_.cur(st_))
            w2mm = P_.cur(st_)
            for tt4 in range(4):
                tt = tb * 4 + tt4
                for do in range(2):
                    bank = P_.pbank[tt4 * 2 + do]
                    xsl = x_sb[:, tt * 1024 + do * 512: tt * 1024 + do * 512 + 512]
                    P_.op("v", lambda e, bank=bank, xsl=xsl: e.tensor_add(
                        xsl, xsl, bank[:]), waits=[(st_, w2mm)], inc=sy)
        P_.op("v", lambda e: e.engine_nop(),
              waits=[(sy, P_.cur(sy)), (ss, P_.cur(ss)), (sv, P_.cur(sv)),
                     (sh, P_.cur(sh)), (oe, P_.cur(oe)),
                     ("nv" + tag0, P_.cur("nv" + tag0))], inc="done" + tag0)
        prev_done = [("done" + tag0, li + 1)]

    fav = "fav" + tag0
    xs_ap = io["l1a_xsave"].ap()
    for ci in range(8):
        sl = "xr" + tag0 + str(ci % 2)
        buf = [kfm, vfm][ci % 2]
        w = ([(st_, st_kv5[0]), (sv, ctx5[0])] if ci < 2 else [(fav, ci - 1)])
        P_.op("sp", lambda e, ci=ci, buf=buf: e.dma_start(
            out=buf[:], in_=xs_ap[:, ci * 2048:(ci + 1) * 2048]),
            waits=w, inc=sl, inc_by=16)
        aw = [(sl, P_.cur(sl)), ("done" + tag0, dn_b + 6)]
        P_.op("v", lambda e, ci=ci, buf=buf: e.tensor_add(
            x_sb[:, ci * 2048:(ci + 1) * 2048],
            x_sb[:, ci * 2048:(ci + 1) * 2048], buf[:]),
            waits=aw, inc=fav)
    return (fav, P_.cur(fav))


def build_l1a_launch():
    pr = P("l1a")
    nc = pr.nc
    x_d = nc.dram_tensor("x_in", [TPC, D], F32, kind="ExternalInput").ap()
    consts_d = nc.dram_tensor("consts_in", [128, NCONST], F32, kind="ExternalInput").ap()
    ident_d = nc.dram_tensor("ident_in", [128, 128], F32, kind="ExternalInput").ap()
    io = {}
    io["lnwb"] = nc.dram_tensor("lnwb", [128, 192], F32, kind="ExternalInput").ap()
    for nm, shp in [("l1a_wq", [6, 1024, 1024]), ("l1a_wk", [6, 1024, 1024]),
                    ("l1a_wv", [6, 1024, 1024]), ("l1a_wo", [6, 1024, 1024]),
                    ("l1a_w1", [6, 1024, 4096]), ("l1a_w2", [6, 4096, 1024]),
                    ("l1a_bo", [6, 1, 1024])]:
        io[nm] = nc.dram_tensor(nm, shp, F32, kind="ExternalInput").ap()
    io["l1a_ag_i"] = nc.dram_tensor("l1a_ag_i", [128, 1040], F32, kind="Internal")
    io["l1a_ag_o"] = nc.dram_tensor("l1a_ag_o", [512, 1040], F32, kind="Internal")
    io["l1a_xsave"] = nc.dram_tensor("l1a_xsave", [128, NTT * 1024], F32,
                                     kind="Internal")
    xo_d = nc.dram_tensor("x_out", [TPC, D], F32, kind="ExternalOutput").ap()
    pr.alloc_pbanks(8)
    x_sb, consts, ident = emit_load_basics(pr, x_d, consts_d, ident_d)
    dsem = emit_l1a(pr, "L", x_sb, consts, ident, io)
    emit_store_x(pr, x_sb, xo_d, waits=[dsem])
    return pr.finish(), pr.n_ops


def l1a_host_inputs(inp, core):
    f32 = np.float32
    lnwb = np.zeros((128, 192), f32)
    for li in range(6):
        for kind, nm in enumerate(["ln1_w", "ln1_b", "ln2_w", "ln2_b"]):
            v = np.asarray(inp[f"L1_attn_a_{nm}"], f32)[li]
            for dc in range(8):
                lnwb[:, li * 32 + kind * 8 + dc] = v[dc * 128:(dc + 1) * 128]
    return {
        "consts_in": host_consts(),
        "ident_in": np.eye(128, dtype=f32),
        "lnwb": lnwb,
        "l1a_wq": np.asarray(inp["L1_attn_a_wq"], f32),
        "l1a_wk": np.asarray(inp["L1_attn_a_wk"], f32),
        "l1a_wv": np.asarray(inp["L1_attn_a_wv"], f32),
        "l1a_wo": np.asarray(inp["L1_attn_a_wo"], f32),
        "l1a_w1": np.asarray(inp["L1_attn_a_w1"], f32),
        "l1a_w2": np.asarray(inp["L1_attn_a_w2"], f32),
        "l1a_bo": np.asarray(inp["L1_attn_a_bo"], f32).reshape(6, 1, 1024),
    }



RT32 = 0.17677669529663687


def emit_l1b(P_, tag, x_sb, consts, ident, io, first_waits=()):
    nc = P_.nc
    xn_fm = P_.sbuf("xn_fm", [128, NDC * TPC], BF16)
    attn = P_.sbuf("attn", [128, 8 * TPC], BF16)
    arena1 = P_.sbuf("arena1", [128, 8192], BF16)
    arena2 = P_.sbuf("arena2", [128, 8192], BF16)
    arena3 = P_.sbuf("arena3", [128, 8448], BF16)
    qw_sb = None
    q_hp = P_.sbuf("q_hp", [128, 2048], BF16)
    pt_sb = P_.sbuf("pt_sb", [128, 4 * 512], BF16)
    qw_sb = pt_sb[:, 0:1024]
    sq = P_.sbuf("sq", [128, 512], F32)
    lat_sb = P_.sbuf("lat_sb", [64, 2 * 2048], BF16)
    hmask_sb = P_.sbuf("hmask_sb", [128, 2], F32)
    negmc = P_.sbuf("negmc", [128, 2], F32)
    rows_sb = P_.sbuf("rows_sb", [1, 3 * 128], F32)

    onesbf = P_.sbuf("onesbf", [1, 128], BF16)
    mxrow = P_.sbuf("mxrow", [1, 256], F32)
    kg_sb = mxrow[0:1, 192:256]
    obvb_sb = None
    den_sb = P_.sbuf("den_sb", [1, 1024], F32)

    blk_sb = P_.sbuf("blk_sb", [128, 256], BF16)
    l2o_sb = P_.sbuf("l2o_sb", [64, 128], BF16)
    bias_sb = P_.sbuf("bias_sb", [128, 32], F32)

    kag_i, vag_i, kag_o, vag_o = io["kag_i"], io["vag_i"], io["kag_o"], io["vag_o"]
    groups = [[0, 1, 2, 3], [4, 5, 6, 7]]

    ld, ldg = "ll" + tag, "lg" + tag
    sv, ss, st_ = "lv" + tag, "ls" + tag, "ttr" + tag
    sde, sxp, smk = "de" + tag, "xp" + tag, "mk" + tag
    P_.op("sp", lambda e: e.dma_start(out=hmask_sb[:], in_=io["hmask32"]),
          waits=list(first_waits), inc=ld, inc_by=16)
    P_.op("sp", lambda e: e.dma_start(out=rows_sb[0:1, 0:128], in_=io["row32A"]), inc=ld, inc_by=16)
    P_.op("sp", lambda e: e.dma_start(out=rows_sb[0:1, 128:256], in_=io["row32B"]), inc=ld, inc_by=16)
    P_.op("sp", lambda e: e.dma_start(out=rows_sb[0:1, 256:384], in_=io["rowOnes"]), inc=ld, inc_by=16)
    P_.op("sp", lambda e: e.dma_start(out=bias_sb[:], in_=io["bias_cols"]), inc=ld, inc_by=16)
    P_.op("g", lambda e: e.dma_start(out=blk_sb[:], in_=io["blkw"]),
          waits=list(first_waits), inc=ldg, inc_by=16)
    P_.op("g", lambda e: e.dma_start(out=l2o_sb[:], in_=io["l2oT2"]), inc=ldg, inc_by=16)
    P_.op("g", lambda e: e.dma_start(out=arena3[:, 4224:8320], in_=io["ropeT"]),
          inc="rp" + tag, inc_by=16)
    ld_done, ldg_done = P_.cur(ld), P_.cur(ldg)
    rope_done = P_.cur("rp" + tag)
    P_.op("v", lambda e: e.memset(onesbf[:], 1.0), waits=list(first_waits), inc=sv)
    P_.op("v", lambda e: e.memset(arena3[:, 0:2112], 1.0), inc=sv)
    ones_done = P_.cur(sv)

    sxn, sxn_tgt = emit_ln_tr(P_, tag, x_sb, xn_fm, consts, ident,
                              first_waits=first_waits, banks=(0, 1))

    caw_re = io["cattn2d"].rearrange("(dc p) n -> dc p n", p=128)
    cpw_re = io["cproj2d"].rearrange("(dc p) n -> dc p n", p=128)
    cos4 = arena3[:, 4224:6272]
    sin4 = arena3[:, 6272:8320]

    q2lT2 = blk_sb[:, 0:64]
    kv2lT2 = blk_sb[:, 64:128]
    RT2 = blk_sb[0:64, 128:192]
    wqkT2 = blk_sb[0:64, 192:256]

    import os as _os0
    _npr = 8
    kev_done = []
    norm_reads = []
    kw_done_st = []
    for pr_ in range(_npr):
        wsem = "kw" + tag + str(pr_ % 2)
        w = [(st_, kw_done_st[pr_ - 2])] if pr_ >= 2 else []
        slot = pr_ % 2
        for dc in range(NDC):
            P_.op("g", lambda e, dc=dc, pr_=pr_, slot=slot: e.dma_start(
                out=arena2[:, slot * 2048 + dc * 256: slot * 2048 + dc * 256 + 128],
                in_=caw_re[dc, :, 1024 + pr_ * 128: 1024 + (pr_ + 1) * 128]),
                waits=(w if dc == 0 else ()), inc=wsem, inc_by=16)
            P_.op("g", lambda e, dc=dc, pr_=pr_, slot=slot: e.dma_start(
                out=arena2[:, slot * 2048 + dc * 256 + 128: slot * 2048 + dc * 256 + 256],
                in_=caw_re[dc, :, 2048 + pr_ * 128: 2048 + (pr_ + 1) * 128]),
                inc=wsem, inc_by=16)
        kw_rdy = (wsem, P_.cur(wsem))
        for kv in range(2):
            for tb in range(4):
                it = pr_ * 8 + kv * 4 + tb
                bank = P_.pbank[2 + tb % 2]
                for dc in range(NDC):
                    mw = []
                    if dc == 0:
                        mw = [kw_rdy, (sxn, sxn_tgt), (ldg, ldg_done)]
                        if kev_done[-2:] and it >= 2:
                            mw.append((sv, kev_done[-2]))
                    P_.op("t", lambda e, bank=bank, slot=slot, dc=dc, kv=kv, tb=tb: e.matmul(
                        bank[:], arena2[:, slot * 2048 + dc * 256 + kv * 128:
                                        slot * 2048 + dc * 256 + kv * 128 + 128],
                        xn_fm[:, dc * TPC + tb * 512: dc * TPC + (tb + 1) * 512],
                        start=(dc == 0), stop=(dc == NDC - 1)), waits=mw, inc=st_)
                mmc = P_.cur(st_)
                bc = bias_sb[:, 8 + kv * 8 + pr_: 9 + kv * 8 + pr_]
                evw = [(st_, mmc), (ld, ld_done), (sv, P_.cur(sv))]
                if kv == 1 and tb == 0:
                    evw.append(kst_sent)
                P_.op("v", lambda e, bank=bank, tb=tb, bc=bc: e.tensor_scalar(
                    q_hp[:, tb * 512:(tb + 1) * 512], bank[:], bc, None, ALU.add),
                    waits=evw, inc=sv)
                kev_done.append(P_.cur(sv))
            if kv == 0:
                for tb in range(4):
                    bank = P_.pbank[6]
                    mw = [(sv, P_.cur(sv))]
                    if tb == 0:
                        mw.append((sde, P_.cur(sde)))
                    P_.op("t", lambda e, bank=bank, tb=tb: e.matmul(
                        bank[0:64, 0:512], kv2lT2,
                        q_hp[:, tb * 512:(tb + 1) * 512],
                        start=True, stop=True), waits=mw, inc=st_)
                    P_.op("v", lambda e, bank=bank, tb=tb: e.tensor_scalar(
                        lat_sb[0:64, tb * 512:(tb + 1) * 512], bank[0:64, 0:512],
                        bias_sb[0:64, 25:26], None, ALU.add),
                        waits=[(st_, P_.cur(st_))], inc=sv)
                lat_done = P_.cur(sv)
            if kv == 0:
                for tb in range(4):
                    bank = P_.pbank[7]
                    P_.op("t", lambda e, bank=bank, tb=tb: e.matmul(
                        bank[0:64, 0:512], RT2,
                        lat_sb[0:64, tb * 512:(tb + 1) * 512],
                        start=True, stop=True),
                        waits=[(sv, P_.cur(sv))] + ([(sde, P_.cur(sde))] if tb == 0 else []),
                        inc=st_)
                    rmm = P_.cur(st_)
                    t1 = lat_sb[0:64, 2048 + tb * 512: 2048 + (tb + 1) * 512]
                    P_.op("v", lambda e, t1=t1, tb=tb: e.tensor_mul(
                        t1, lat_sb[0:64, tb * 512:(tb + 1) * 512],
                        cos4[0:64, tb * 512:(tb + 1) * 512]),
                        waits=[(st_, rmm), ("rp" + tag, rope_done)], inc=sv)
                    t2 = pt_sb[0:64, 512:1024]
                    P_.op("v", lambda e, t2=t2, bank=bank, tb=tb: e.tensor_mul(
                        t2, bank[0:64, 0:512],
                        sin4[0:64, tb * 512:(tb + 1) * 512]),
                        waits=[(sv, P_.cur(sv))], inc=sv)
                    P_.op("v", lambda e, t1=t1, t2=t2: e.tensor_add(t1, t1, t2),
                          waits=[(sv, P_.cur(sv))], inc=sv)
                krope_done = P_.cur(sv)
                for tb in range(4):
                    P_.op("s", lambda e, tb=tb: e.activation(
                        sq[0:64, 0:512], lat_sb[0:64, 2048 + tb * 512: 2048 + (tb + 1) * 512],
                        AF.Square), waits=[(sv, krope_done), (st_, P_.cur(st_))] +
                        ([(ss, P_.cur(ss))] if tb > 0 else []), inc=ss)
                    nw = [(ss, P_.cur(ss)), (ld, ld_done), (sv, P_.cur(sv))]
                    P_.op("t", lambda e: e.matmul(P_.pbank[6][0:1, 0:512],
                                                  hmask_sb[0:64, 0:1],
                                                  sq[0:64, 0:512],
                                                  start=True, stop=True),
                          waits=nw, inc=st_)
                    P_.op("t", lambda e: e.matmul(P_.pbank[7][0:1, 0:512],
                                                  hmask_sb[0:64, 1:2],
                                                  sq[0:64, 0:512],
                                                  start=True, stop=True), inc=st_)
                    nc_ = P_.cur(st_)
                    c0 = 64 + (2 * pr_) * 4 + tb
                    c1 = 64 + (2 * pr_ + 1) * 4 + tb
                    P_.op("v", lambda e, c0=c0: e.tensor_reduce(
                        mxrow[0:1, c0:c0 + 1], P_.pbank[6][0:1, 0:512], AX.X,
                        ALU.max), waits=[(st_, nc_)], inc=sv)
                    P_.op("v", lambda e, c1=c1: e.tensor_reduce(
                        mxrow[0:1, c1:c1 + 1], P_.pbank[7][0:1, 0:512], AX.X,
                        ALU.max), inc=sv)
                    norm_reads.append(P_.cur(sv))
                spk = "ks" + tag + str(pr_ % 2)
                kst = q_hp[0:64, 0:2048]
                kst_w = [(sv, P_.cur(sv))]
                if pr_ >= 2:
                    kst_w.append((spk, P_.cur(spk)))
                P_.op("v", lambda e, kst=kst: e.tensor_copy(
                    kst, lat_sb[0:64, 2048:4096]), waits=kst_w, inc=sv)
                P_.op("sp", lambda e, pr_=pr_, kst=kst: e.dma_start(
                    out=kag_i.ap()[pr_ * 64:(pr_ + 1) * 64, 0:2048], in_=kst),
                    waits=[(sv, P_.cur(sv))], inc=spk, inc_by=16)
                kst_sent = (spk, P_.cur(spk))
            else:
                spv = "vs" + tag + str(pr_ % 2)
                for tb in range(4):
                    bank6 = P_.pbank[6]
                    P_.op("t", lambda e, bank6=bank6, tb=tb: e.matmul(
                        bank6[0:64, 0:512], kv2lT2,
                        q_hp[:, tb * 512:(tb + 1) * 512],
                        start=True, stop=True),
                        waits=[(sv, P_.cur(sv))] + ([(sde, P_.cur(sde))] if tb == 0 else []),
                        inc=st_)
                    P_.op("v", lambda e, bank6=bank6: e.tensor_scalar(
                        sq[0:64, 0:512], bank6[0:64, 0:512],
                        bias_sb[0:64, 25:26], None, ALU.add),
                        waits=[(st_, P_.cur(st_))], inc=sv)
                    vsq = P_.cur(sv)
                    for tt4 in range(4):
                        tt = tb * 4 + tt4
                        for hx in range(2):
                            bank = P_.pbank[7]
                            tw = [(sv, vsq)]
                            if not (tb == 0 and tt4 == 0 and hx == 0):
                                tw.append((sv, P_.cur(sv)))
                            elif pr_ == 0:
                                tw.append((sde, P_.cur(sde)))
                            P_.op("t", lambda e, tt4=tt4, hx=hx, bank=bank: e.transpose(
                                bank[0:128, 0:32],
                                sq[hx * 32:(hx + 1) * 32, tt4 * 128:(tt4 + 1) * 128],
                                ident[hx * 32:(hx + 1) * 32, hx * 32:(hx + 1) * 32]),
                                waits=tw, inc=st_)
                            dst = arena3[:, hx * 1056 + tt * 33: hx * 1056 + tt * 33 + 32]
                            vtw = [(st_, P_.cur(st_))]
                            if tb == 0 and tt4 == 0 and hx == 0:
                                vtw.append((sv, ones_done))
                                if pr_ >= 1:
                                    vtw.append(("vs" + tag + "0", P_.cur("vs" + tag + "0")))
                                    vtw.append(("vs" + tag + "1", P_.cur("vs" + tag + "1")))
                            P_.op("v", lambda e, dst=dst, bank=bank: e.tensor_copy(
                                dst, bank[0:128, 0:32]), waits=vtw, inc=sv)
                for hx in range(2):
                    h = 2 * pr_ + hx
                    P_.op("sp", lambda e, h=h, hx=hx: e.dma_start(
                        out=vag_i.ap()[h * 128:(h + 1) * 128, 0:528],
                        in_=arena3[:, hx * 1056: hx * 1056 + 528]),
                        waits=([(sv, P_.cur(sv))] if hx == 0 else ()),
                        inc=spv, inc_by=16)
        kw_done_st.append(P_.cur(st_))
    if _npr < 8:
        P_.op("v", lambda e: e.engine_nop(),
              waits=[(sv, P_.cur(sv)), (ss, P_.cur(ss)), (st_, 0)], inc="done" + tag)
        return "done" + tag
    for h in range(16):
        src = mxrow[0:1, 64 + h * 4: 64 + h * 4 + 4]
        P_.op("v", lambda e, src=src, h=h: e.tensor_reduce(
            mxrow[0:1, 128 + h:129 + h], src, AX.X, ALU.max),
            waits=([(sv, P_.cur(sv))] if h == 0 else []), inc=sv)
    P_.op("v", lambda e: e.memset(den_sb[:], 0.0), waits=[(sv, P_.cur(sv))], inc=sv)
    P_.op("v", lambda e: e.tensor_copy(den_sb[0:1, 0:16], mxrow[0:1, 128:144]),
          waits=[(sv, P_.cur(sv))], inc=sv)
    P_.op("g", lambda e: e.dma_start(out=kag_i.ap()[512:513, 0:1024],
                                     in_=den_sb[0:1, 0:1024]),
          waits=[(sv, P_.cur(sv))], inc=ldg, inc_by=16)
    P_.op("g", lambda e: e.dma_start(out=kag_i.ap()[512:513, 1024:2048],
                                     in_=den_sb[0:1, 0:1024]), inc=ldg, inc_by=16)

    cc = "cc" + tag
    P_.op("g", lambda e: e.collective_compute(
        "AllGather", ALU.bypass, replica_groups=groups,
        ins=[kag_i.ap().opt()], outs=[kag_o.ap().opt()]),
        waits=[("ks" + tag + "0", P_.cur("ks" + tag + "0")),
               ("ks" + tag + "1", P_.cur("ks" + tag + "1")),
               (ldg, P_.cur(ldg))], inc=cc)
    P_.op("g", lambda e: e.collective_compute(
        "AllGather", ALU.bypass, replica_groups=groups,
        ins=[vag_i.ap().opt()], outs=[vag_o.ap().opt()]),
        waits=[("vs" + tag + "0", P_.cur("vs" + tag + "0")),
               ("vs" + tag + "1", P_.cur("vs" + tag + "1"))], inc=cc)
    cc_done = P_.cur(cc)

    cm = "cm" + tag
    P_.op("g", lambda e: e.dma_start(out=arena1[:], in_=io["cmask16"]),
          inc=cm, inc_by=16)
    cm_done = P_.cur(cm)
    kx = "kx" + tag
    for r in range(4):
        P_.op("g", lambda e, r=r: e.dma_start(
            out=kg_sb[0:1, r * 16:(r + 1) * 16],
            in_=kag_o.ap()[r * 513 + 512: r * 513 + 513, 0:16]),
            waits=([(cc, cc_done)] if r == 0 else ()), inc=kx, inc_by=16)
    kmg = mxrow[0:1, 144:160]
    P_.op("v", lambda e: e.tensor_reduce(
        kmg, kg_sb[0:1, 0:64].rearrange("o (r h) -> o h r", r=4), AX.X, ALU.max),
        waits=[(kx, P_.cur(kx))], inc=sv)
    kmg_done = P_.cur(sv)

    import os as _os1
    if False:
        P_.op("v", lambda e: e.engine_nop(), waits=[(sv, P_.cur(sv)), (cm, cm_done)],
              inc="done" + tag)
        return "done" + tag
    sp2 = "st2" + tag
    flash_done_st = []
    for pr_ in range(8):
        stw = [(st_, flash_done_st[-1])] if flash_done_st else [(cc, cc_done)]
        nk = 0
        for r in range(4):
            for lb in range(4):
                gq = r + 4 * lb
                P_.op("sp", lambda e, r=r, lb=lb, gq=gq, pr_=pr_: e.dma_start(
                    out=arena2[0:64, gq * 512:(gq + 1) * 512],
                    in_=kag_o.ap()[r * 513 + pr_ * 64: r * 513 + (pr_ + 1) * 64,
                                   lb * 512:(lb + 1) * 512]),
                    waits=(stw if nk == 0 else ()), inc=sp2, inc_by=16)
                nk += 1
                for hx in range(2):
                    h = 2 * pr_ + hx
                    P_.op("sp", lambda e, r=r, lb=lb, gq=gq, h=h, hx=hx: e.dma_start(
                        out=arena3[:, hx * 2112 + gq * 132: hx * 2112 + (gq + 1) * 132],
                        in_=vag_o.ap()[r * 2048 + h * 128: r * 2048 + (h + 1) * 128,
                                       lb * 132:(lb + 1) * 132]),
                        inc=sp2, inc_by=16)
        strips_rdy = (sp2, P_.cur(sp2))

        wsem = "qw" + tag + "0"
        qlw = [(st_, P_.cur(st_)), (sv, P_.cur(sv)), (sde, P_.cur(sde)),
               (sxp, P_.cur(sxp)), (smk, P_.cur(smk))]
        for dc in range(NDC):
            P_.op("g", lambda e, dc=dc, pr_=pr_: e.dma_start(
                out=qw_sb[:, dc * 128:(dc + 1) * 128],
                in_=caw_re[dc, :, pr_ * 128:(pr_ + 1) * 128]),
                waits=(qlw if dc == 0 else ()), inc=wsem, inc_by=16)
        qw_rdy = (wsem, P_.cur(wsem))
        qev = []
        for tb in range(4):
            bank = P_.pbank[tb % 2]
            for dc in range(NDC):
                mw = []
                if dc == 0:
                    mw = [qw_rdy, (sxp, P_.cur(sxp))]
                    if tb >= 2:
                        mw.append((sv, qev[tb - 2]))
                P_.op("t", lambda e, bank=bank, dc=dc, tb=tb: e.matmul(
                    bank[:], qw_sb[:, dc * 128:(dc + 1) * 128],
                    xn_fm[:, dc * TPC + tb * 512: dc * TPC + (tb + 1) * 512],
                    start=(dc == 0), stop=(dc == NDC - 1)), waits=mw, inc=st_)
            mmc = P_.cur(st_)
            bc = bias_sb[:, pr_:pr_ + 1]
            P_.op("v", lambda e, bank=bank, tb=tb, bc=bc: e.tensor_scalar(
                q_hp[:, tb * 512:(tb + 1) * 512], bank[:], bc, None, ALU.add),
                waits=[(st_, mmc)], inc=sv)
            qev.append(P_.cur(sv))
        qfm_done = P_.cur(sv)
        for tb in range(4):
            bank = P_.pbank[6]
            P_.op("t", lambda e, bank=bank, tb=tb: e.matmul(
                bank[0:64, 0:512], q2lT2, q_hp[:, tb * 512:(tb + 1) * 512],
                start=True, stop=True),
                waits=[(sv, P_.cur(sv))] + ([(sde, P_.cur(sde))] if tb == 0 else []),
                inc=st_)
            P_.op("v", lambda e, bank=bank, tb=tb: e.tensor_scalar(
                lat_sb[0:64, tb * 512:(tb + 1) * 512], bank[0:64, 0:512],
                bias_sb[0:64, 24:25], None, ALU.add),
                waits=[(st_, P_.cur(st_))], inc=sv)
        lat_done = P_.cur(sv)
        for tb in range(4):
            bank = P_.pbank[7]
            P_.op("t", lambda e, bank=bank, tb=tb: e.matmul(
                bank[0:64, 0:512], RT2, lat_sb[0:64, tb * 512:(tb + 1) * 512],
                start=True, stop=True),
                waits=[(sv, P_.cur(sv))] + ([(sde, P_.cur(sde))] if tb == 0 else []),
                inc=st_)
            rmm = P_.cur(st_)
            t1 = lat_sb[0:64, 2048 + tb * 512: 2048 + (tb + 1) * 512]
            P_.op("v", lambda e, t1=t1, tb=tb: e.tensor_mul(
                t1, lat_sb[0:64, tb * 512:(tb + 1) * 512],
                cos4[0:64, tb * 512:(tb + 1) * 512]),
                waits=[(st_, rmm)], inc=sv)
            t2 = pt_sb[0:64, 512:1024]
            P_.op("v", lambda e, t2=t2, bank=bank, tb=tb: e.tensor_mul(
                t2, bank[0:64, 0:512], sin4[0:64, tb * 512:(tb + 1) * 512]),
                waits=[(sv, P_.cur(sv))], inc=sv)
            P_.op("v", lambda e, t1=t1, t2=t2: e.tensor_add(t1, t1, t2),
                  waits=[(sv, P_.cur(sv))], inc=sv)
        qrope_done = P_.cur(sv)
        for tb in range(4):
            bank = P_.pbank[6]
            P_.op("t", lambda e, bank=bank, tb=tb: e.matmul(
                bank[0:64, 0:512], wqkT2,
                lat_sb[0:64, 2048 + tb * 512: 2048 + (tb + 1) * 512],
                start=True, stop=True),
                waits=[(sv, P_.cur(sv))] + ([(sde, P_.cur(sde))] if tb == 0 else []),
                inc=st_)
            P_.op("v", lambda e, bank=bank, tb=tb: e.tensor_scalar(
                lat_sb[0:64, 0 + tb * 512: 0 + (tb + 1) * 512],
                bank[0:64, 0:512], bias_sb[0:64, 26:27], RT32, ALU.add, ALU.mult),
                waits=[(st_, P_.cur(st_))], inc=sv)
            P_.op("s", lambda e, tb=tb: e.activation(
                sq[0:64, 0:512], lat_sb[0:64, 0 + tb * 512: 0 + (tb + 1) * 512],
                AF.Square), waits=[(sv, P_.cur(sv)), (ss, P_.cur(ss))], inc=ss)
            nw = [(ss, P_.cur(ss)), (sde, P_.cur(sde)), (sv, P_.cur(sv))]
            P_.op("t", lambda e: e.matmul(P_.pbank[6][0:1, 0:512],
                                          hmask_sb[0:64, 0:1], sq[0:64, 0:512],
                                          start=True, stop=True), waits=nw, inc=st_)
            P_.op("t", lambda e: e.matmul(P_.pbank[7][0:1, 0:512],
                                          hmask_sb[0:64, 1:2], sq[0:64, 0:512],
                                          start=True, stop=True), inc=st_)
            nc_ = P_.cur(st_)
            P_.op("v", lambda e, tb=tb: e.tensor_reduce(
                mxrow[0:1, tb:tb + 1], P_.pbank[6][0:1, 0:512], AX.X, ALU.max),
                waits=[(st_, nc_)], inc=sv)
            P_.op("v", lambda e, tb=tb: e.tensor_reduce(
                mxrow[0:1, 4 + tb:5 + tb], P_.pbank[7][0:1, 0:512], AX.X, ALU.max),
                inc=sv)
            norm_reads.append(P_.cur(sv))
        P_.op("v", lambda e: e.tensor_reduce(
            mxrow[0:1, 8:9], mxrow[0:1, 0:4], AX.X, ALU.max),
            waits=[(sv, P_.cur(sv)), (sv, kmg_done)], inc=sv)
        P_.op("v", lambda e: e.tensor_reduce(
            mxrow[0:1, 9:10], mxrow[0:1, 4:8], AX.X, ALU.max), inc=sv)
        P_.op("v", lambda e, pr_=pr_: e.tensor_mul(
            mxrow[0:1, 10:12], mxrow[0:1, 8:10], kmg[0:1, 2 * pr_:2 * pr_ + 2]),
            waits=[(sv, P_.cur(sv))], inc=sv)
        P_.op("s", lambda e: e.activation(mxrow[0:1, 12:14], mxrow[0:1, 10:12],
                                          AF.Ln), waits=[(sv, P_.cur(sv))], inc=ss)
        P_.op("s", lambda e: e.activation(mxrow[0:1, 14:16], mxrow[0:1, 12:14],
                                          AF.Exp, scale=0.5),
              waits=[(ss, P_.cur(ss))], inc=ss)
        P_.op("v", lambda e: e.tensor_scalar(mxrow[0:1, 16:18], mxrow[0:1, 14:16],
                                             -1.0, None, ALU.mult),
              waits=[(ss, P_.cur(ss))], inc=sv)
        P_.op("t", lambda e: e.matmul(P_.pbank[6][:, 0:2], rows_sb[0:1, 256:384],
                                      mxrow[0:1, 16:18], start=True, stop=True),
              waits=[(sv, P_.cur(sv)), (sde, P_.cur(sde))], inc=st_)
        P_.op("v", lambda e: e.tensor_copy(negmc[:], P_.pbank[6][:, 0:2]),
              waits=[(st_, P_.cur(st_))], inc=sv)
        negm_done = P_.cur(sv)

        for qb in range(4):
            nkt = 16 * (qb + 1)
            pvA, pvB = P_.pbank[4], P_.pbank[5]
            exps_d = {}

            def emit_pv(kt, qb=qb, nkt=nkt, pvA=pvA, pvB=pvB):
                ptA = pt_sb[:, (kt % 2) * 512:(kt % 2) * 512 + 512]
                ptB = pt_sb[:, 1024 + (kt % 2) * 512: 1024 + (kt % 2) * 512 + 512]
                wloc = kt - 16 * qb
                masked = wloc >= 0
                if masked:
                    cmt = arena1[:, wloc * 512:(wloc + 1) * 512]
                    P_.op("v", lambda e, ptA=ptA, cmt=cmt: e.tensor_mul(
                        ptA, ptA, cmt), waits=[(sxp, exps_d[kt])], inc=smk)
                    P_.op("v", lambda e, ptB=ptB, cmt=cmt: e.tensor_mul(
                        ptB, ptB, cmt), inc=smk)
                pvw = [(smk, P_.cur(smk))] if masked else [(sxp, exps_d[kt])]
                if kt == 0:
                    pvw.append((sde, P_.cur(sde)))
                P_.op("t", lambda e, kt=kt, ptA=ptA, nkt=nkt: e.matmul(
                    pvA[0:33, 0:512], arena3[:, kt * 33:(kt + 1) * 33], ptA,
                    start=(kt == 0), stop=(kt == nkt - 1)), waits=pvw, inc=st_)
                P_.op("t", lambda e, kt=kt, ptB=ptB, nkt=nkt: e.matmul(
                    pvB[0:33, 0:512], arena3[:, 2112 + kt * 33: 2112 + (kt + 1) * 33],
                    ptB, start=(kt == 0), stop=(kt == nkt - 1)), inc=st_)

            for kt in range(nkt):
                sA = P_.pbank[kt % 2]
                sB = P_.pbank[2 + kt % 2]
                mw = [strips_rdy, (sv, negm_done), (cm, cm_done)]
                if kt >= 2:
                    mw = [(sxp, P_.cur(sxp) - 2)]
                elif qb > 0:
                    mw.append((sxp, P_.cur(sxp)))
                P_.op("t", lambda e, sA=sA, kt=kt, qb=qb: e.matmul(
                    sA[:], arena2[0:32, kt * 128:(kt + 1) * 128],
                    lat_sb[0:32, qb * 512:(qb + 1) * 512],
                    start=True, stop=True), waits=mw, inc=st_)
                P_.op("t", lambda e, sB=sB, kt=kt, qb=qb: e.matmul(
                    sB[:], arena2[32:64, kt * 128:(kt + 1) * 128],
                    lat_sb[32:64, qb * 512:(qb + 1) * 512],
                    start=True, stop=True, tile_position=(32, 0)), inc=st_)
                smm = P_.cur(st_)
                ptA = pt_sb[:, (kt % 2) * 512:(kt % 2) * 512 + 512]
                ptB = pt_sb[:, 1024 + (kt % 2) * 512: 1024 + (kt % 2) * 512 + 512]
                ew = [(st_, smm)]
                if kt < 2:
                    ew.append((sde, P_.cur(sde)))
                P_.op("s", lambda e, ptA=ptA, sA=sA: e.activation(
                    ptA, sA[:], AF.Exp, bias=negmc[:, 0:1]),
                    waits=ew, inc=sxp)
                P_.op("s", lambda e, ptB=ptB, sB=sB: e.activation(
                    ptB, sB[:], AF.Exp, bias=negmc[:, 1:2]), inc=sxp)
                exps_d[kt] = P_.cur(sxp)
                if kt >= 1:
                    emit_pv(kt - 1)
            emit_pv(nkt - 1)
            sweep_done = P_.cur(st_)
            denw = [(st_, sweep_done)]
            if pr_ == 0 and qb == 0:
                denw.append((ldg, P_.cur(ldg)))
            P_.op("v", lambda e: e.tensor_copy(den_sb[0:1, 0:512],
                                               pvA[32:33, 0:512]),
                  waits=denw, inc=sde)
            P_.op("v", lambda e: e.tensor_copy(den_sb[0:1, 512:1024],
                                               pvB[32:33, 0:512]), inc=sde)
            P_.op("v", lambda e: e.reciprocal(den_sb[0:1, 0:512],
                                              den_sb[0:1, 0:512]),
                  waits=[(sde, P_.cur(sde))], inc=sde)
            P_.op("v", lambda e: e.reciprocal(den_sb[0:1, 512:1024],
                                              den_sb[0:1, 512:1024]), inc=sde)
            rd = P_.cur(sde)
            P_.op("t", lambda e: e.matmul(P_.pbank[6][0:64, 0:512],
                                          rows_sb[0:1, 0:64],
                                          den_sb[0:1, 0:512],
                                          start=True, stop=False),
                  waits=[(sde, rd), (sv, P_.cur(sv))], inc=st_)
            P_.op("t", lambda e: e.matmul(P_.pbank[6][0:64, 0:512],
                                          rows_sb[0:1, 128:192],
                                          den_sb[0:1, 512:1024],
                                          start=False, stop=True), inc=st_)
            exm = P_.cur(st_)
            yl = pt_sb[0:64, 512:1024]
            P_.op("v", lambda e, yl=yl: e.tensor_copy(yl[0:32, :], pvA[0:32, 0:512]),
                  waits=[(st_, exm)], inc=sde)
            P_.op("v", lambda e, yl=yl: e.tensor_copy(yl[32:64, :], pvB[0:32, 0:512]),
                  inc=sde)
            P_.op("v", lambda e, yl=yl: e.tensor_mul(yl, yl, P_.pbank[6][0:64, 0:512]),
                  waits=[(sde, P_.cur(sde))], inc=sde)
            yl_done = P_.cur(sde)
            P_.op("t", lambda e, yl=yl: e.matmul(
                P_.pbank[7][:, 0:512], l2o_sb[0:64, 0:128], yl,
                start=True, stop=True),
                waits=[(sde, yl_done), (ldg, ldg_done)], inc=st_)
            lmm = P_.cur(st_)
            asl = attn[:, pr_ * 2048 + qb * 512: pr_ * 2048 + qb * 512 + 512]
            P_.op("v", lambda e, asl=asl: e.tensor_scalar(
                asl, P_.pbank[7][:, 0:512], bias_sb[:, 27:28], None, ALU.add),
                waits=[(st_, lmm)], inc=sde)
        flash_done_st.append(P_.cur(st_))

    import os as _os2
    if False:
        P_.op("v", lambda e: e.engine_nop(), waits=[(sde, P_.cur(sde))],
              inc="done" + tag)
        return "done" + tag
    ow = "ow" + tag
    obvb_sb = arena3[0:1, 0:1024]
    P_.op("g", lambda e: e.dma_start(out=obvb_sb, in_=io["cprojb_row"]),
          waits=[(st_, flash_done_st[-1])], inc=ow, inc_by=16)
    for hp in range(8):
        for half in range(2):
            P_.op("g", lambda e, hp=hp, half=half: e.dma_start(
                out=arena2[:, hp * 1024 + half * 512: hp * 1024 + half * 512 + 512],
                in_=cpw_re[hp, :, half * 512:(half + 1) * 512]),
                waits=([(st_, flash_done_st[-1])] if hp == 0 and half == 0 else ()),
                inc=ow, inc_by=16)
    ow_rdy = (ow, P_.cur(ow))
    oe = "oe" + tag
    oe_b = P_.curb(oe, "v")
    for tt in range(NTT):
        for mb in range(2):
            gi = tt * 2 + mb
            bank = P_.pbank[6 + gi % 2]
            for hp in range(8):
                mw = []
                if hp == 0:
                    mw = [ow_rdy, (sde, P_.cur(sde))]
                    if gi >= 2:
                        mw.append((oe, oe_b + gi - 1))
                P_.op("t", lambda e, bank=bank, hp=hp, tt=tt, mb=mb: e.matmul(
                    bank[:], attn[:, hp * 2048 + tt * 128: hp * 2048 + tt * 128 + 128],
                    arena2[:, hp * 1024 + mb * 512: hp * 1024 + mb * 512 + 512],
                    start=(hp == 0), stop=False), waits=mw, inc=st_)
            P_.op("t", lambda e, bank=bank, mb=mb: e.matmul(
                bank[:], onesbf[:], obvb_sb[0:1, mb * 512:(mb + 1) * 512],
                start=False, stop=True), inc=st_)
            xsl = x_sb[:, tt * 1024 + mb * 512: tt * 1024 + mb * 512 + 512]
            P_.op("v", lambda e, xsl=xsl, bank=bank: e.tensor_add(
                xsl, xsl, bank[:]), waits=[(st_, P_.cur(st_))], inc=oe)
    import os as _os
    if False:
        P_.op("sp", lambda e: e.dma_start(out=io["dbg_attn"], in_=attn[:]),
              waits=[(oe, P_.cur(oe))], inc="dbg" + tag, inc_by=16)
        P_.op("sp", lambda e: e.dma_start(out=io["dbg_qfk"], in_=lat_sb[0:64, 0:2048]),
              inc="dbg" + tag, inc_by=16)
        P_.op("g", lambda e: e.dma_start(out=io["dbg_kag"], in_=kag_o.ap()),
              inc="dbgg" + tag, inc_by=16)
        P_.op("v", lambda e: e.engine_nop(),
              waits=[(oe, P_.cur(oe)), ("dbg" + tag, P_.cur("dbg" + tag)),
                     ("dbgg" + tag, P_.cur("dbgg" + tag))], inc="done" + tag)
        return "done" + tag
    P_.op("v", lambda e: e.engine_nop(),
          waits=[(oe, P_.cur(oe)), (sv, P_.cur(sv)), (ss, P_.cur(ss))],
          inc="done" + tag)
    return "done" + tag


def build_l1b_launch():
    pr = P("l1b")
    nc = pr.nc
    x_d = nc.dram_tensor("x_in", [TPC, D], F32, kind="ExternalInput").ap()
    consts_d = nc.dram_tensor("consts_in", [128, NCONST], F32, kind="ExternalInput").ap()
    ident_d = nc.dram_tensor("ident_in", [128, 128], F32, kind="ExternalInput").ap()
    io = {}
    for nm, shp in [("cattn2d", [1024, 3072]), ("cproj2d", [1024, 1024]),
                    ("cprojb_row", [1, 1024]), ("hmask32", [128, 2]),
                    ("row32A", [1, 128]), ("row32B", [1, 128]),
                    ("rowOnes", [1, 128]), ("cmask16", [128, 16 * 512]),
                    ("blkw", [128, 256]), ("l2oT2", [64, 128]),
                    ("bias_cols", [128, 32]), ("ropeT", [128, 4096])]:
        io[nm] = nc.dram_tensor(nm, shp, F32, kind="ExternalInput").ap()
    io["kag_i"] = nc.dram_tensor("kag_i", [513, 2048], BF16, kind="Internal")
    io["vag_i"] = nc.dram_tensor("vag_i", [2048, 528], BF16, kind="Internal")
    io["kag_o"] = nc.dram_tensor("kag_o", [2052, 2048], BF16, kind="Internal")
    io["vag_o"] = nc.dram_tensor("vag_o", [8192, 528], BF16, kind="Internal")
    xo_d = nc.dram_tensor("x_out", [TPC, D], F32, kind="ExternalOutput").ap()
    import os as _os
    if False:
        io["dbg_attn"] = nc.dram_tensor("dbg_attn", [128, 16384], BF16, kind="ExternalOutput").ap()
        io["dbg_qfk"] = nc.dram_tensor("dbg_qfk", [64, 2048], BF16, kind="ExternalOutput").ap()
        io["dbg_kag"] = nc.dram_tensor("dbg_kag", [2052, 2048], BF16, kind="ExternalOutput").ap()
    pr.alloc_pbanks(8)
    x_sb, consts, ident = emit_load_basics(pr, x_d, consts_d, ident_d)
    dsem = emit_l1b(pr, "M", x_sb, consts, ident, io)
    emit_store_x(pr, x_sb, xo_d, waits=[(dsem, 1)])
    return pr.finish(), pr.n_ops


def rope_tables(Tn=8192, L=32):
    inv = 1.0 / 10000.0 ** (np.arange(0, L, 2, dtype=np.float32) / L)
    t = np.arange(Tn, dtype=np.float32)
    fr = np.outer(t, inv)
    cos = np.repeat(np.cos(fr), 2, axis=-1)[:, :L]
    sin = np.repeat(np.sin(fr), 2, axis=-1)[:, :L]
    return cos, sin


def l1b_host_inputs(inp, core):
    f32 = np.float32
    caw = np.asarray(inp["L1_attn_b_c_attn_w"], f32)
    cab = np.asarray(inp["L1_attn_b_c_attn_b"], f32)
    q2l = np.asarray(inp["L1_attn_b_q2l_w"], f32)
    q2lb = np.asarray(inp["L1_attn_b_q2l_b"], f32)
    kv2l = np.asarray(inp["L1_attn_b_kv2l_w"], f32)
    kv2lb = np.asarray(inp["L1_attn_b_kv2l_b"], f32)
    l2o = np.asarray(inp["L1_attn_b_l2o_w"], f32)
    l2ob = np.asarray(inp["L1_attn_b_l2o_b"], f32)
    wqk = np.asarray(inp["L1_attn_b_wqk_w"], f32)
    wqkb = np.asarray(inp["L1_attn_b_wqk_b"], f32)
    cpw = np.asarray(inp["L1_attn_b_cproj_w"], f32)
    cpb = np.asarray(inp["L1_attn_b_cproj_b"], f32)
    hm = np.zeros((128, 2), f32)
    hm[0:32, 0] = 1.0
    hm[32:64, 1] = 1.0
    r32A = np.zeros((1, 128), f32); r32A[0, 0:32] = 1.0
    r32B = np.zeros((1, 128), f32); r32B[0, 32:64] = 1.0
    rowOnes = np.ones((1, 128), f32)
    j = core % 4
    jj = np.arange(128)[:, None]
    ii = np.arange(512)[None, :]
    cmv = np.zeros((128, 16 * 512), f32)
    for w in range(16):
        cmv[:, w * 512:(w + 1) * 512] = (128 * w + jj <= 512 * j + ii)
    blkw = np.zeros((128, 256), f32)
    q2lT = q2l.T
    kv2lT = kv2l.T
    blkw[0:64, 0:32] = q2lT
    blkw[64:128, 32:64] = q2lT
    blkw[0:64, 64:96] = kv2lT
    blkw[64:128, 96:128] = kv2lT
    R = np.zeros((32, 32), f32)
    for i_ in range(16):
        R[2 * i_, 2 * i_ + 1] = -1.0
        R[2 * i_ + 1, 2 * i_] = 1.0
    RT = R.T
    blkw[0:32, 128:160] = RT
    blkw[32:64, 160:192] = RT
    wqkT = wqk.T
    blkw[0:32, 192:224] = wqkT
    blkw[32:64, 224:256] = wqkT
    l2oT2 = np.zeros((64, 128), f32)
    l2oT = l2o.T
    l2oT2[0:32, 0:64] = l2oT
    l2oT2[32:64, 64:128] = l2oT
    bias_cols = np.zeros((128, 32), f32)
    for p in range(8):
        bias_cols[:, p] = cab[p * 128:(p + 1) * 128]
        bias_cols[:, 8 + p] = cab[1024 + p * 128: 1024 + (p + 1) * 128]
        bias_cols[:, 16 + p] = cab[2048 + p * 128: 2048 + (p + 1) * 128]
    bias_cols[0:32, 24] = q2lb; bias_cols[32:64, 24] = q2lb
    bias_cols[0:32, 25] = kv2lb; bias_cols[32:64, 25] = kv2lb
    bias_cols[0:32, 26] = wqkb; bias_cols[32:64, 26] = wqkb
    bias_cols[0:64, 27] = l2ob; bias_cols[64:128, 27] = l2ob
    cos, sin = rope_tables()
    pos = np.concatenate([np.arange(a, b) for a, b in seq_segments(core)])
    cosl = cos[pos].T
    sinl = sin[pos].T
    ropeT = np.zeros((128, 4096), f32)
    for r in range(4):
        ropeT[r * 32:(r + 1) * 32, 0:2048] = cosl
        ropeT[r * 32:(r + 1) * 32, 2048:4096] = sinl
    return {
        "consts_in": host_consts(),
        "ident_in": np.eye(128, dtype=f32),
        "cattn2d": np.ascontiguousarray(caw.T),
        "cproj2d": np.ascontiguousarray(cpw.T),
        "cprojb_row": cpb.reshape(1, 1024),
        "hmask32": hm, "row32A": r32A, "row32B": r32B, "rowOnes": rowOnes,
        "cmask16": cmv, "blkw": blkw, "l2oT2": l2oT2, "bias_cols": bias_cols,
        "ropeT": ropeT,
    }



POOL = 202208

OVERLAY = {
    "x": 16512,
    "xn_fm": 82048,
    "attn": 114816,
    "arena1": 147584,
    "arena2": 163968,
    "arena3": 180352,
    "xn2": 197248,
    "cols": 201344,
    "consts": 201408,
    "ident": 201440,
    "onesbf": 201952,
    "h_fm": 114816,
    "w1buf": 147584,
    "hbuf": 147584,
    "w2buf": 163968,
    "sm_sb": 163968,
    "g_sb": 172160,
    "gbf": 180352,
    "win_sb": 184448,
    "wout_sb": 185472,
    "bout_sb": 187520,
    "onesrow": 189568,
    "ffn_small": 189824,
    "kfm": 180352,
    "vfm": 188544,
    "etile": 196736,
    "agc": 180352,
    "q_hp": POOL + 0,
    "pt_sb": POOL + 4096,
    "sq": POOL + 8192,
    "rows_sb": POOL + 10240,
    "den_sb": POOL + 11776,
    "mxrow": POOL + 15872,
    "qw_sb": POOL + 16896,
    "obvb_sb": POOL + 18944,
    "kg_sb": POOL + 20992,
    "qkb_sb": POOL + 21248,
    "bcols": POOL + 21376,
    "gjc": POOL + 21664,
    "iirow": POOL + 21920,
    "qpos": POOL + 23968,
    "lat_sb": POOL + 16896,
    "blk_sb": POOL + 25088,
    "l2o_sb": POOL + 25600,
    "bias_sb": POOL + 25856,
    "hmask_sb": POOL + 25984,
    "negmc": POOL + 26016,
    "gx": POOL + 0,
    "selwT": POOL + 4096,
    "b2_sb": POOL + 12288,
    "b1_sb": POOL + 16384,
    "gcols": POOL + 17408,
    "selw": POOL + 17536,
    "gate8": POOL + 18048,
    "eq8": POOL + 18080,
    "wred_sb": POOL + 18112,
    "wgq": POOL + 18624,
    "redT_sb": POOL + 18880,
    "vtile": POOL + 0,
    "numb": POOL + 512,
    "ctxb": POOL + 4672,
    "lnwb": POOL + 6720,
    "borow": POOL + 7488,
    "cols2": POOL + 9536,
}


def build_fused_launch(nstages=8, first=0):
    pr = P("fused")
    pr.overlay = OVERLAY
    nc = pr.nc
    x_d = nc.dram_tensor("x_in", [TPC, D], F32, kind="ExternalInput").ap()
    consts_d = nc.dram_tensor("consts_in", [128, NCONST], F32, kind="ExternalInput").ap()
    ident_d = nc.dram_tensor("ident_in", [128, 128], F32, kind="ExternalInput").ap()

    def eio(pre, ext, internal):
        d = {}
        for nm, shp in ext:
            d[nm] = nc.dram_tensor(pre + nm, shp, F32, kind="ExternalInput").ap()
        for nm, shp, dt in internal:
            d[nm] = nc.dram_tensor(pre + nm, shp, dt, kind="Internal")
        return d

    ioa = eio("a_", [("qw2d", [1024, 1024]), ("kw2d", [1024, 1024]),
                     ("vw2d", [1024, 1024]), ("ow2d", [1024, 1024]),
                     ("qkb", [128, 16]), ("vb_row", [1, 1024]),
                     ("ob_row", [1, 1024]), ("hmask", [128, 2]),
                     ("rowA", [1, 128]), ("rowB", [1, 128]),
                     ("rowOnes", [1, 128]), ("cmask16", [128, 16 * 512])],
               [("kag_i", [1025, 2048], BF16), ("vag_i", [2048, 1040], BF16),
                ("kag_o", [4100, 2048], BF16), ("vag_o", [8192, 1040], BF16)])
    iom = eio("m_", [("w1", [8, 1024, 4096]), ("w2", [8, 4096, 1024]),
                     ("wred", [128, 128]), ("wg", [8, 16]),
                     ("b1t", [128, 256]), ("b2", [8, 1024])], [])
    iob = eio("b_", [("qw2d", [1024, 1024]), ("kvw2d", [1024, 512]),
                     ("wproj2d", [1024, 1024]), ("bproj_row", [1, 1024]),
                     ("hmask", [128, 2]), ("rowA", [1, 128]), ("rowB", [1, 128]),
                     ("rowOnes", [1, 128]), ("gjcols", [128, 64]),
                     ("iirow", [1, 512]), ("qpos", [1, 4]),
                     ("alibiT", [16, 128, 2944])],
               [("kag_i", [257, 2048], BF16), ("vag_i", [512, 1040], BF16),
                ("kag_o", [1028, 2048], BF16), ("vag_o", [2048, 1040], BF16)])
    ioe = eio("e_", [("lnwb", [128, 192]), ("l1a_wq", [6, 1024, 1024]),
                     ("l1a_wk", [6, 1024, 1024]), ("l1a_wv", [6, 1024, 1024]),
                     ("l1a_wo", [6, 1024, 1024]), ("l1a_w1", [6, 1024, 4096]),
                     ("l1a_w2", [6, 4096, 1024]), ("l1a_bo", [6, 1, 1024])],
               [("l1a_ag_i", [128, 1040], F32), ("l1a_ag_o", [512, 1040], F32),
                ("l1a_xsave", [128, NTT * 1024], F32)])
    iof = eio("f_", [("win", [128, NDC * 64]), ("binp", [64, 1]),
                     ("w1p", [64, 16]), ("b1p", [64, 16]), ("w2p", [64, 16]),
                     ("b2p", [64, 1]), ("woutp", [64, 1024]),
                     ("boutp", [1, 1024])], [])
    iog = eio("g_", [("cattn2d", [1024, 3072]), ("cproj2d", [1024, 1024]),
                     ("cprojb_row", [1, 1024]), ("hmask32", [128, 2]),
                     ("row32A", [1, 128]), ("row32B", [1, 128]),
                     ("rowOnes", [1, 128]), ("cmask16", [128, 16 * 512]),
                     ("blkw", [128, 256]), ("l2oT2", [64, 128]),
                     ("bias_cols", [128, 32]), ("ropeT", [128, 4096])],
               [("kag_i", [513, 2048], BF16), ("vag_i", [2048, 528], BF16),
                ("kag_o", [2052, 2048], BF16), ("vag_o", [8192, 528], BF16)])
    xo_d = nc.dram_tensor("x_out", [TPC, D], F32, kind="ExternalOutput").ap()

    pr.alloc_pbanks(8)
    pr.sbuf("mxrow", [1, 256], F32)
    x_sb, consts, ident = emit_load_basics(pr, x_d, consts_d, ident_d)

    state = {"fw": [], "saved": {}}

    def run(key, emitfn, reuse=None):
        pr.mark(key)
        saved = pr.sem_pool_begin(state["saved"].get(reuse))
        state["saved"][key] = saved
        d = emitfn(state["fw"])
        if isinstance(d, tuple):
            dn, dv = d
        else:
            dn, dv = d, pr.cur(d)
        state["fw"] = [(pr._pool_map.get(dn, dn), dv)]

    stages = [
        ("l0a", None, lambda f: emit_l0a(pr, "Q", x_sb, consts, ident,
                                                ioa, first_waits=f)),
        ("moe", None, lambda f: emit_moe(pr, "A", x_sb, consts, ident,
                                         iom["w1"], iom["w2"], iom["wred"],
                                         iom["wg"], iom["b1t"], iom["b2"],
                                         first_waits=f)),
        ("l0b", None, lambda f: emit_l0b(pr, "W", x_sb, consts, ident,
                                                iob, first_waits=f)),
        ("moe2", "moe", lambda f: emit_moe(pr, "A", x_sb, consts, ident,
                                           iom["w1"], iom["w2"], iom["wred"],
                                           iom["wg"], iom["b1t"], iom["b2"],
                                           first_waits=f)),
        ("l1a", None, lambda f: emit_l1a(pr, "L", x_sb, consts, ident,
                                                ioe, first_waits=f)),
        ("ffn", None, lambda f: emit_l1ffn(pr, "F", x_sb, consts, ident,
                                           iof["win"], iof["binp"], iof["w1p"],
                                           iof["b1p"], iof["w2p"], iof["b2p"],
                                           iof["woutp"], iof["boutp"],
                                           first_waits=f)),
        ("l1b", None, lambda f: emit_l1b(pr, "M", x_sb, consts, ident,
                                                iog, first_waits=f)),
        ("ffn2", "ffn", lambda f: emit_l1ffn(pr, "F", x_sb, consts, ident,
                                             iof["win"], iof["binp"],
                                             iof["w1p"], iof["b1p"],
                                             iof["w2p"], iof["b2p"],
                                             iof["woutp"], iof["boutp"],
                                             first_waits=f)),
    ]
    for key, reuse, fn in stages[first:nstages]:
        run(key, fn, reuse=reuse)
    pr._pool_on = False
    pr.mark("store")
    emit_store_x(pr, x_sb, xo_d, waits=state["fw"])
    nsems = len(pr.sems)
    assert nsems <= 100, f"too many sems: {nsems}"
    return pr.finish(), pr.n_ops, nsems


def fused_host_inputs(inp, core):
    h = {"consts_in": host_consts(), "ident_in": np.eye(128, dtype=np.float32)}
    for pre, d in [("a_", l0a_host_inputs(inp, core)),
                   ("m_", moe_host_inputs(inp)),
                   ("b_", l0b_host_inputs(inp, core)),
                   ("e_", l1a_host_inputs(inp, core)),
                   ("f_", l1ffn_host_inputs(inp)),
                   ("g_", l1b_host_inputs(inp, core))]:
        for k, v in d.items():
            if k in ("consts_in", "ident_in", "consts", "ident"):
                continue
            h[pre + k] = v
    return h


_PROG = []


def kernel(**inputs):
    if not _PROG:
        _PROG.append(build_fused_launch()[0])
    nc = _PROG[0]
    x = np.asarray(inputs["L0_attn_a_x"], np.float32).reshape(B * S, D)
    xs = shard_rows(x)
    im = [dict(fused_host_inputs(inputs, c), x_in=xs[c]) for c in range(NCORES)]
    res = run_bass_kernel_spmd(nc, im, core_ids=list(range(NCORES)))
    out = [np.asarray(res.results[c]["x_out"]) for c in range(NCORES)]
    return unshard_rows(out).reshape(B, S, D)


# version 05
# speedup vs baseline: 1.4388x; 1.0252x over previous
import contextlib
import numpy as np
import ml_dtypes
import concourse.bass as bass
import concourse.mybir as mybir
from concourse.bass_utils import run_bass_kernel_spmd

F32 = mybir.dt.float32
BF16 = mybir.dt.bfloat16
AF = mybir.ActivationFunctionType
ALU = mybir.AluOpType
AX = mybir.AxisListType
BF = ml_dtypes.bfloat16

B, S, D = 2, 8192, 1024
NCORES = 8
TPC = 2048
NTT = TPC // 128
NDC = D // 128

C_ONE, C_NEG1, C_EPS, C_WGEPS, C_LEAK, C_ZERO = 0, 1, 2, 3, 4, 5
NCONST = 8


def seq_segments(c):
    j = c % 4
    return [(512 * (j + 4 * k), 512 * (j + 4 * k) + 512) for k in range(4)]


def host_consts():
    row = np.zeros(NCONST, dtype=np.float32)
    row[C_ONE], row[C_NEG1], row[C_EPS] = 1.0, -1.0, 1e-5
    row[C_WGEPS], row[C_LEAK], row[C_ZERO] = 1e-4, 0.01, 0.0
    return np.tile(row[None, :], (128, 1))


class P:
    ENGS = ("t", "v", "s", "sp", "g")

    def __init__(self, name="mk"):
        self.nc = bass.Bass("TRN2", target_bir_lowering=False, name=name,
                            dynamic_dma_scratch_size=16384)
        self.stack = contextlib.ExitStack()
        self.q = {e: [] for e in self.ENGS}
        self.sems = {}
        self.cnt = {}
        self.sb = {}
        self.pbank = []
        self.n_ops = 0

    def sem(self, name):
        if name not in self.sems:
            self.sems[name] = self.stack.enter_context(self.nc.semaphore(name))
            self.cnt[name] = 0
        return self.sems[name]

    def sem_pool_begin(self, saved=None):
        self._pool_on = True
        if saved is not None:
            self._pool_map, self._pool_idx = saved
        else:
            self._pool_map, self._pool_idx = {}, {}
        return (self._pool_map, self._pool_idx)

    def _physname(self, name, eng=None):
        if not getattr(self, "_pool_on", False):
            return name
        if name.startswith("cc"):
            return "ZCC"
        if name in self.sems:
            return name
        if name in self._pool_map:
            return self._pool_map[name]
        assert eng is not None, f"sem {name} first used in a wait (pooling)"
        k = self._pool_idx.get(eng, 0)
        self._pool_idx[eng] = k + 1
        phys = f"Z{eng}{k}"
        self._pool_map[name] = phys
        return phys

    def sbuf(self, name, shape, dt):
        if name in self.sb:
            return self.sb[name]
        ovl = getattr(self, "overlay", None)
        if ovl is not None and name in ovl:
            h = self.nc.alloc_sbuf_tensor_at("sb_" + name, shape, dt,
                                             offset=ovl[name])
        else:
            h = self.stack.enter_context(self.nc.sbuf_tensor("sb_" + name, shape, dt))
        self.sb[name] = h
        return h

    def alloc_pbanks(self, n=8):
        for i in range(n):
            self.pbank.append(
                self.stack.enter_context(
                    self.nc.psum_tensor(f"pb{i}", [128, 512], F32)))

    def op(self, eng, fn, waits=(), inc=None, inc_by=1, incs=()):
        if inc is not None:
            inc = self._physname(inc, eng)
        incs = [(self._physname(n, eng), by) for (n, by) in incs]
        wl = [(self.sem(self._physname(n)), v) for (n, v) in waits if v > 0]
        il = []
        if not hasattr(self, "_semeng"):
            self._semeng = {}
        if inc is not None:
            il.append((self.sem(inc), inc_by))
            self.cnt[inc] += inc_by
        for n, by in incs:
            il.append((self.sem(n), by))
            self.cnt[n] += by
        for n in ([inc] if inc else []) + [n for n, _ in incs]:
            own = self._semeng.setdefault(n, eng)
            assert own == eng, f"sem {n} inc'd from {eng} and {own}"
        ret = self.cnt[inc] if inc is not None else None

        def g(e, fn=fn, wl=wl, il=il):
            for h, v in wl:
                e.wait_ge(h, v)
            r = fn(e)
            for h, by in il:
                r = r.then_inc(h, by)
        self.q[eng].append(g)
        self.n_ops += 1
        return ret

    def wait(self, eng, *wpairs):
        wl = [(self.sem(self._physname(n)), v) for (n, v) in wpairs if v > 0]

        def g(e, wl=wl):
            for h, v in wl:
                e.wait_ge(h, v)
        self.q[eng].append(g)

    def cur(self, name):
        if getattr(self, "_pool_on", False):
            if name.startswith("cc"):
                name = "ZCC"
            else:
                name = self._pool_map.get(name, name)
        return self.cnt.get(name, 0)

    def curb(self, name, eng):
        return self.cnt.get(self._physname(name, eng), 0)

    def mark(self, name):
        if not hasattr(self, "_marks"):
            self._marks = []
        self._marks.append((name, {e: len(self.q[e]) for e in self.ENGS}))

    def finish(self):
        marks = getattr(self, "_marks", [])
        with self.nc.Block() as b:
            for ename, deco in (("t", b.tensor), ("v", b.vector), ("s", b.scalar),
                                ("sp", b.sync), ("g", b.gpsimd)):
                ops = self.q[ename]
                if not ops:
                    continue
                segs = []
                prev_name, prev_idx = None, 0
                for name, idxs in marks:
                    i = idxs[ename]
                    if i > prev_idx:
                        segs.append((prev_name, prev_idx, i))
                    prev_name, prev_idx = name, i
                segs.append((prev_name, prev_idx, len(ops)))

                def runner(e, ops=ops, segs=segs):
                    for name, s, t in segs:
                        if name is None:
                            for f in ops[s:t]:
                                f(e)
                        else:
                            with self.nc.named_scope(name):
                                for f in ops[s:t]:
                                    f(e)
                deco(runner)
        self.stack.close()
        return self.nc



def emit_load_basics(P_, x_dram, consts_dram, ident_dram):
    x_sb = P_.sbuf("x", [128, NTT * 1024], F32)
    consts = P_.sbuf("consts", [128, NCONST], F32)
    ident = P_.sbuf("ident", [128, 128], F32)
    xr = x_dram.rearrange("(n p) d -> n p d", p=128)
    for n in range(NTT):
        P_.op("sp", lambda e, n=n: e.dma_start(
            out=x_sb[:, n * 1024:(n + 1) * 1024], in_=xr[n]), inc="ldx", inc_by=16)
    P_.op("sp", lambda e: e.dma_start(out=consts[:], in_=consts_dram), inc="ldx", inc_by=16)
    P_.op("sp", lambda e: e.dma_start(out=ident[:], in_=ident_dram), inc="ldx", inc_by=16)
    identb = P_.sbuf("identb", [128, 128], BF16)
    P_.op("g", lambda e: e.dma_start(out=identb[:], in_=ident_dram),
          inc="ldig", inc_by=16)
    return x_sb, consts, ident


LDX_DONE = (NTT + 2) * 16


def emit_store_x(P_, x_sb, xout_dram, waits):
    xr = xout_dram.rearrange("(n p) d -> n p d", p=128)
    for n in range(NTT):
        P_.op("sp", lambda e, n=n: e.dma_start(
            out=xr[n], in_=x_sb[:, n * 1024:(n + 1) * 1024]),
            waits=(waits if n == 0 else ()), inc="stx", inc_by=16)
    P_.wait("sp", ("stx", P_.cur("stx")))



def emit_ln_tr(P_, tag, x_sb, xn_fm, consts, ident, first_waits=(),
               gamma_col=None, beta_col=None, tile_cb=None, fp32_cb=None,
               banks=(0, 1)):
    sv, ss, st_, se = "lnv" + tag, "lns" + tag, "ttr" + tag, "tev" + tag
    ses = "tevs" + tag
    sxn = "xn" + tag
    se_b, ses_b = P_.curb(se, "v"), P_.curb(ses, "s")
    xn2 = P_.sbuf("xn2", [128, 1024], F32)
    use_bf = fp32_cb is None
    xn2b = P_.sbuf("xn2b", [128, 1024], BF16) if use_bf else None
    identb = P_.sbuf("identb", [128, 128], BF16)
    cols = P_.sbuf("cols", [128, 16], F32)
    csum, cm, cq, cvar, cm2, cr = (cols[:, i:i + 1] for i in range(6))
    gate_guard = [None]

    for tt in range(NTT):
        x_t = x_sb[:, tt * 1024:(tt + 1) * 1024]
        w0 = [("ldx", LDX_DONE)] + (list(first_waits) if tt == 0 else [])
        vw = [(sv, P_.cur(sv))]
        P_.op("v", lambda e, x_t=x_t: e.reduce_sum(csum, x_t, AX.X), waits=w0 + vw, inc=sv)
        P_.op("s", lambda e, x_t=x_t: e.activation(xn2[:], x_t, AF.Square, accum_out=cq),
              waits=w0 + [(ss, P_.cur(ss)), (st_, P_.cur(st_))], inc=ss)
        P_.op("v", lambda e: e.tensor_scalar(cm, csum, 1.0 / 1024, None, ALU.mult),
              waits=[(sv, P_.cur(sv))], inc=sv)
        P_.op("v", lambda e: e.tensor_mul(cm2, cm, cm), waits=[(sv, P_.cur(sv))], inc=sv)
        P_.op("v", lambda e: e.tensor_scalar(cvar, cq, 1.0 / 1024, None, ALU.mult),
              waits=[(ss, P_.cur(ss)), (sv, P_.cur(sv))], inc=sv)
        P_.op("v", lambda e: e.tensor_sub(cvar, cvar, cm2), waits=[(sv, P_.cur(sv))], inc=sv)
        P_.op("s", lambda e: e.activation(cvar, cvar, AF.Ln, bias=consts[:, C_EPS:C_EPS + 1]),
              waits=[(sv, P_.cur(sv))], inc=ss)
        P_.op("s", lambda e: e.activation(cr, cvar, AF.Exp, scale=-0.5),
              waits=[(ss, P_.cur(ss))], inc=ss)
        xn_t = xn2b[:, 0:1024] if use_bf else xn2[:, 0:1024]
        tw = [(st_, P_.cur(st_))] if tt >= 1 else []
        P_.op("v", lambda e, x_t=x_t, xn_t=xn_t: e.tensor_scalar(
            xn_t, x_t, cm, cr, ALU.subtract, ALU.mult),
            waits=[(sv, P_.cur(sv)), (ss, P_.cur(ss))] + tw, inc=sv)
        xn_ready = P_.cur(sv)
        for dc in range(NDC):
            if use_bf:
                bview = P_.pbank[banks[dc % 2]][:, 0:64].bitcast(BF16)
                idw = identb[:]
            else:
                bview = P_.pbank[banks[dc % 2]][:, 0:128]
                idw = ident[:]
            idx = tt * NDC + dc
            tw2 = [(sv, xn_ready)]
            if idx >= 2:
                tw2.append((se, se_b + idx - 1))
                if fp32_cb is not None:
                    tw2.append((ses, ses_b + idx - 1))
            else:
                tw2.append((se, se_b))
                if fp32_cb is not None:
                    tw2.append((ses, ses_b))
                if use_bf:
                    tw2.append(("ldig", 16))
            P_.op("t", lambda e, xn_t=xn_t, dc=dc, bview=bview, idw=idw: e.transpose(
                bview, xn_t[:, dc * 128:(dc + 1) * 128], idw),
                waits=tw2, inc=st_)
            trc = P_.cur(st_)
            dst = xn_fm[:, dc * TPC + tt * 128: dc * TPC + (tt + 1) * 128]
            if gamma_col is None:
                P_.op("v", lambda e, dst=dst, bview=bview: e.tensor_copy(dst, bview),
                      waits=[(st_, trc)], inc=se)
            else:
                gc, bc = gamma_col(dc), beta_col(dc)
                P_.op("v", lambda e, dst=dst, bview=bview, gc=gc, bc=bc: e.tensor_scalar(
                    dst, bview, gc, bc, ALU.mult, ALU.add),
                    waits=[(st_, trc)], inc=se)
            if fp32_cb is not None:
                fp32_cb(tt, dc, bview, trc, gate_guard[0])
        w_end = [(se, P_.cur(se))] + ([(ses, P_.cur(ses))] if fp32_cb is not None else [])
        P_.op("v", lambda e: e.engine_nop(), waits=w_end, inc=sxn)
        if tile_cb is not None:
            gate_guard[0] = tile_cb(tt, P_.cur(ses))
    return sxn, P_.cur(sxn)



def emit_moe(P_, tag, x_sb, consts, ident, w1_d, w2_d, wred_d, wg_d, b1t_d, b2_d,
             first_waits=()):
    xn_fm = P_.sbuf("xn_fm", [128, NDC * TPC], BF16)
    gx = P_.sbuf("gx", [128, NDC * 128], F32)
    gcols = P_.sbuf("gcols", [128, 32], F32)
    selw = P_.sbuf("selw", [128, NTT * 8], F32)
    selwT = P_.sbuf("selwT", [8, NTT * 128], F32)
    gate8 = P_.sbuf("gate8", [128, 8], F32)
    eq8 = P_.sbuf("eq8", [128, 8], F32)
    wred_sb = P_.sbuf("wred_sb", [128, 128], F32)
    wgq = P_.sbuf("wgq", [16, 64], F32)
    redT_sb = P_.sbuf("redT_sb", [16, 128], F32)
    b1_sb = P_.sbuf("b1_sb", [128, 8 * 32], F32)
    b2_sb = P_.sbuf("b2_sb", [8, 1024], F32)
    w1buf = P_.sbuf("w1buf", [128, 2 * 8 * 512], BF16)
    w2buf = P_.sbuf("w2buf", [128, 2 * 4 * 1024], BF16)
    h_fm = P_.sbuf("h_fm", [128, 32 * 512], BF16)

    ldw = "ml" + tag
    P_.op("sp", lambda e: e.dma_start(out=wred_sb[:], in_=wred_d),
          waits=list(first_waits), inc=ldw, inc_by=16)
    P_.op("sp", lambda e: e.dma_start(out=wgq[0:8, 0:16], in_=wg_d), inc=ldw, inc_by=16)
    P_.op("sp", lambda e: e.dma_start(out=b1_sb[:], in_=b1t_d), inc=ldw, inc_by=16)
    P_.op("sp", lambda e: e.dma_start(out=b2_sb[:], in_=b2_d), inc=ldw, inc_by=16)
    ldw_done = P_.cur(ldw)

    sv, ss, st_, se = "mv" + tag, "ms" + tag, "ttr" + tag, "tev" + tag
    wgs, wgn = wgq[0:8, 0:16], wgq[0:8, 16:32]
    n2, sd, rc = wgq[0:8, 40:41], wgq[0:8, 41:42], wgq[0:8, 42:43]
    wgnT = wgq[0:16, 32:40]
    P_.op("s", lambda e: e.activation(wgq[0:8, 48:64], wgs, AF.Square, accum_out=n2),
          waits=[(ldw, ldw_done), ("ldx", LDX_DONE)], inc=ss)
    P_.op("s", lambda e: e.activation(sd, n2, AF.Ln), waits=[(ss, P_.cur(ss))], inc=ss)
    P_.op("s", lambda e: e.activation(sd, sd, AF.Exp, scale=0.5),
          waits=[(ss, P_.cur(ss))], inc=ss)
    P_.op("v", lambda e: e.tensor_scalar(sd, sd, 1e-4, None, ALU.max),
          waits=[(ss, P_.cur(ss))], inc=sv)
    P_.op("v", lambda e: e.reciprocal(rc, sd), waits=[(sv, P_.cur(sv))], inc=sv)
    P_.op("v", lambda e: e.tensor_scalar(wgn, wgs, rc, None, ALU.mult),
          waits=[(sv, P_.cur(sv))], inc=sv)
    P_.op("t", lambda e: e.transpose(P_.pbank[2][0:16, 0:8], wgn, ident[0:8, 0:8]),
          waits=[(sv, P_.cur(sv))], inc=st_)
    P_.op("v", lambda e: e.tensor_copy(wgnT, P_.pbank[2][0:16, 0:8]),
          waits=[(st_, P_.cur(st_))], inc=sv)
    wgnT_ready = P_.cur(sv)

    red_ps = P_.pbank[3]
    gps = P_.pbank[4]
    sg, gv, gs = "gt" + tag, "gv" + tag, "gs" + tag
    cgmax, cgden, cgrden, cnf, cneg = (gcols[:, i:i + 1] for i in range(5))
    red_sb = gcols[:, 8:24]

    def fp32_cb(tt, dc, bview, trc, guard):
        dst = gx[:, dc * 128:(dc + 1) * 128]
        w = [(st_, trc), ("tev" + tag, P_.cur("tev" + tag))]
        if guard is not None:
            w.append(guard)
        P_.op("s", lambda e, dst=dst, bview=bview: e.copy(dst, bview),
              waits=w, inc="tevs" + tag)

    import os as _os
    _G = 6

    def tile_cb(tt, evac_count):
        if _G < 1:
            return None
        for dc in range(NDC):
            w = [("tevs" + tag, evac_count), (ldw, ldw_done)]
            P_.op("t", lambda e, dc=dc: e.matmul(
                red_ps[:, 0:16], gx[:, dc * 128:(dc + 1) * 128],
                wred_sb[:, dc * 16:(dc + 1) * 16],
                start=(dc == 0), stop=(dc == NDC - 1)), waits=w, inc=st_)
        P_.op("v", lambda e: e.tensor_copy(red_sb, red_ps[:, 0:16]),
              waits=[(st_, P_.cur(st_)), (gv, P_.cur(gv))], inc=gv)
        if _G < 2:
            return (gv, P_.cur(gv))
        P_.op("t", lambda e: e.transpose(gps[0:16, 0:128], red_sb, ident[:]),
              waits=[(gv, P_.cur(gv))], inc=st_)
        P_.op("v", lambda e: e.tensor_copy(redT_sb[:], gps[0:16, 0:128]),
              waits=[(st_, P_.cur(st_))], inc=gv)
        if _G < 3:
            return (gv, P_.cur(gv))
        P_.op("t", lambda e: e.matmul(P_.pbank[5][:, 0:8], redT_sb[:], wgnT,
                                      start=True, stop=True),
              waits=[(gv, P_.cur(gv)), (sv, wgnT_ready)], inc=st_)
        P_.op("v", lambda e: e.tensor_copy(gate8[:], P_.pbank[5][:, 0:8]),
              waits=[(st_, P_.cur(st_))], inc=gv)
        if _G < 4:
            return (gv, P_.cur(gv))
        P_.op("v", lambda e: e.tensor_reduce(cgmax, gate8[:], AX.X, ALU.max),
              waits=[(gv, P_.cur(gv))], inc=gv)
        P_.op("v", lambda e: e.tensor_scalar(cneg, cgmax, -1.0, None, ALU.mult),
              waits=[(gv, P_.cur(gv))], inc=gv)
        P_.op("s", lambda e: e.activation(gate8[:], gate8[:], AF.Exp, bias=cneg,
                                          accum_out=cgden),
              waits=[(gv, P_.cur(gv))], inc=gs)
        P_.op("v", lambda e: e.reciprocal(cgrden, cgden), waits=[(gs, P_.cur(gs))], inc=gv)
        P_.op("v", lambda e: e.tensor_scalar(gate8[:], gate8[:], cgrden, None, ALU.mult),
              waits=[(gv, P_.cur(gv))], inc=gv)
        P_.op("v", lambda e: e.tensor_reduce(cgmax, gate8[:], AX.X, ALU.max),
              waits=[(gv, P_.cur(gv))], inc=gv)
        P_.op("v", lambda e: e.memset(cnf, 1.0), waits=[(gv, P_.cur(gv))], inc=gv)
        if _G < 5:
            return (gv, P_.cur(gv))
        for e_i in range(8):
            P_.op("v", lambda e, e_i=e_i: e.tensor_single_scalar(
                eq8[:, e_i:e_i + 1], gate8[:, e_i:e_i + 1], cgmax, ALU.is_equal),
                waits=[(gv, P_.cur(gv))], inc=gv)
        for e_i in range(8):
            sc = selw[:, tt * 8 + e_i: tt * 8 + e_i + 1]
            P_.op("v", lambda e, e_i=e_i, sc=sc: e.tensor_mul(
                sc, eq8[:, e_i:e_i + 1], cnf), waits=[(gv, P_.cur(gv))], inc=gv)
            P_.op("v", lambda e, sc=sc: e.tensor_sub(cnf, cnf, sc),
                  waits=[(gv, P_.cur(gv))], inc=gv)
        for e_i in range(8):
            sc = selw[:, tt * 8 + e_i: tt * 8 + e_i + 1]
            P_.op("v", lambda e, sc=sc: e.tensor_scalar(
                sc, sc, cgrden, None, ALU.mult), waits=[(gv, P_.cur(gv))], inc=gv)
        if _G < 6:
            return (gv, P_.cur(gv))
        P_.op("t", lambda e, tt=tt: e.transpose(
            P_.pbank[6][0:8, 0:128], selw[:, tt * 8:(tt + 1) * 8], ident[:]),
            waits=[(gv, P_.cur(gv))], inc=st_)
        P_.op("v", lambda e, tt=tt: e.tensor_copy(
            selwT[:, tt * 128:(tt + 1) * 128], P_.pbank[6][0:8, 0:128]),
            waits=[(st_, P_.cur(st_))], inc=sg)
        return (sg, P_.cur(sg))

    import os
    phase = 4
    if phase == 1:
        sxn, sxn_tgt = emit_ln_tr(P_, tag, x_sb, xn_fm, consts, ident,
                                  first_waits=first_waits, banks=(0, 1))
        P_.op("v", lambda e: e.engine_nop(), waits=[(sxn, sxn_tgt)], inc="done" + tag)
        return "done" + tag
    sxn, sxn_tgt = emit_ln_tr(P_, tag, x_sb, xn_fm, consts, ident,
                              first_waits=first_waits,
                              tile_cb=tile_cb, fp32_cb=fp32_cb, banks=(0, 1))
    if phase == 2:
        P_.op("v", lambda e: e.engine_nop(),
              waits=[(sxn, sxn_tgt), (sg, P_.cur(sg)), (gv, P_.cur(gv))],
              inc="done" + tag)
        return "done" + tag

    gw1, gw2 = "w1" + tag, "w2" + tag
    sh, sy = "hh" + tag, "yy" + tag
    w1_re = w1_d.rearrange("ei (dc p) f -> ei p dc f", p=128)
    w2_re = w2_d.rearrange("ei (fc p) d -> ei p fc d", p=128)
    nw1, nw2 = [0], [0]
    w1_done_st = []
    w2_done_st = []

    def load_w1(e_i, fb):
        k = nw1[0]
        nw1[0] += 1
        slot = k % 2
        w = ([(st_, w1_done_st[k - 2])] if k >= 2
             else list(first_waits) + [(st_, P_.cur(st_))])
        sem = gw1 + str(slot)
        for dc in range(8):
            dst = w1buf[:, slot * 4096 + dc * 512: slot * 4096 + (dc + 1) * 512]
            src = w1_re[e_i, :, dc, fb * 512:(fb + 1) * 512]
            P_.op("g", lambda e, dst=dst, src=src: e.dma_start(out=dst, in_=src),
                  waits=(w if dc == 0 else ()), inc=sem, inc_by=16)
        return slot, (sem, P_.cur(sem))

    def load_w2(e_i, fg):
        k = nw2[0]
        nw2[0] += 1
        slot = k % 2
        w = ([(st_, w2_done_st[k - 2])] if k >= 2
             else list(first_waits) + [(st_, P_.cur(st_))])
        sem = gw2 + str(slot)
        for fq in range(4):
            dst = w2buf[:, slot * 4096 + fq * 1024: slot * 4096 + (fq + 1) * 1024]
            src = w2_re[e_i, :, fg * 4 + fq, :]
            P_.op("g", lambda e, dst=dst, src=src: e.dma_start(out=dst, in_=src),
                  waits=(w if fq == 0 else ()), inc=sem, inc_by=16)
        return slot, (sem, P_.cur(sem))

    import os
    _tbs = 4
    _nes = 8
    for tb in range(_tbs):
        for e_i in range(_nes):
            for fb in range(8):
                slot, rdy = load_w1(e_i, fb)
                for fq in range(4):
                    fc = fb * 4 + fq
                    bank = P_.pbank[fc % 2]
                    for dc in range(NDC):
                        w = []
                        if dc == 0:
                            w = [rdy]
                            if fc == 0:
                                w += [(sy, P_.cur(sy)), (sxn, sxn_tgt)]
                            gelu_cnt = P_.cur(sh)
                            if gelu_cnt >= 2:
                                w.append((sh, gelu_cnt - 1))
                        P_.op("t", lambda e, bank=bank, slot=slot, dc=dc, fq=fq, tb=tb: e.matmul(
                            bank[:],
                            w1buf[:, slot * 4096 + dc * 512 + fq * 128:
                                  slot * 4096 + dc * 512 + fq * 128 + 128],
                            xn_fm[:, dc * TPC + tb * 512: dc * TPC + (tb + 1) * 512],
                            start=(dc == 0), stop=(dc == NDC - 1)),
                            waits=w, inc=st_)
                    mmc = P_.cur(st_)
                    b1c = b1_sb[:, e_i * 32 + fc: e_i * 32 + fc + 1]
                    P_.op("s", lambda e, fc=fc, bank=bank, b1c=b1c: e.activation(
                        h_fm[:, fc * 512:(fc + 1) * 512], bank[:], AF.Gelu, bias=b1c),
                        waits=[(st_, mmc)], inc=sh)
                w1_done_st.append(P_.cur(st_))
            h_done = P_.cur(sh)
            for fg in range(8):
                slot, rdy = load_w2(e_i, fg)
                for fq in range(4):
                    fc = fg * 4 + fq
                    for tt4 in range(4):
                        for do in range(2):
                            bank = P_.pbank[tt4 * 2 + do]
                            w = []
                            if fq == 0 and tt4 == 0 and do == 0:
                                w = [rdy]
                                if fg == 0:
                                    w += [(sh, h_done), (sy, P_.cur(sy))]
                                    if tb == 0 and e_i == 0:
                                        w.append((sg, P_.cur(sg)))
                            P_.op("t", lambda e, bank=bank, slot=slot, fq=fq, fc=fc,
                                  tt4=tt4, do=do: e.matmul(
                                      bank[:],
                                      h_fm[:, fc * 512 + tt4 * 128:
                                           fc * 512 + tt4 * 128 + 128],
                                      w2buf[:, slot * 4096 + fq * 1024 + do * 512:
                                            slot * 4096 + fq * 1024 + do * 512 + 512],
                                      start=(fc == 0), stop=(fc == 31)),
                                  waits=w, inc=st_)
                w2_done_st.append(P_.cur(st_))
            w2done = P_.cur(st_)
            for tt4 in range(4):
                tt = tb * 4 + tt4
                wcol = selw[:, tt * 8 + e_i: tt * 8 + e_i + 1]
                for do in range(2):
                    bank = P_.pbank[tt4 * 2 + do]
                    xsl = x_sb[:, tt * 1024 + do * 512: tt * 1024 + do * 512 + 512]
                    P_.op("v", lambda e, bank=bank, wcol=wcol, xsl=xsl: e.scalar_tensor_tensor(
                        xsl, bank[:], wcol, xsl, ALU.mult, ALU.add),
                        waits=[(st_, w2done)], inc=sy)
    if phase == 3:
        P_.op("v", lambda e: e.engine_nop(), waits=[(sy, P_.cur(sy))], inc="done" + tag)
        return "done" + tag
    b2e = "b2e" + tag
    yb = P_.pbank[0]
    for tt in range(NTT):
        for do in range(2):
            w = [(sy, P_.cur(sy)), (ldw, ldw_done), (sg, P_.cur(sg))]
            if tt > 0 or do > 0:
                w.append((b2e, P_.cur(b2e)))
            P_.op("t", lambda e, tt=tt, do=do: e.matmul(
                yb[:], selwT[:, tt * 128:(tt + 1) * 128],
                b2_sb[:, do * 512:(do + 1) * 512], start=True, stop=True),
                waits=w, inc=st_)
            xsl = x_sb[:, tt * 1024 + do * 512: tt * 1024 + do * 512 + 512]
            P_.op("v", lambda e, xsl=xsl: e.tensor_add(xsl, xsl, yb[:]),
                  waits=[(st_, P_.cur(st_))], inc=b2e)
    done = P_.op("v", lambda e: e.engine_nop(),
                 waits=[(b2e, P_.cur(b2e)), (sv, P_.cur(sv)), (ss, P_.cur(ss)),
                        (sh, P_.cur(sh)), (sy, P_.cur(sy)), (sg, P_.cur(sg)),
                        (gv, P_.cur(gv)), (gs, P_.cur(gs))],
                 inc="done" + tag)
    return "done" + tag



def build_moe_launch():
    pr = P("moe")
    nc = pr.nc
    x_d = nc.dram_tensor("x_in", [TPC, D], F32, kind="ExternalInput").ap()
    consts_d = nc.dram_tensor("consts", [128, NCONST], F32, kind="ExternalInput").ap()
    ident_d = nc.dram_tensor("ident", [128, 128], F32, kind="ExternalInput").ap()
    w1_d = nc.dram_tensor("w1", [8, 1024, 4096], F32, kind="ExternalInput").ap()
    w2_d = nc.dram_tensor("w2", [8, 4096, 1024], F32, kind="ExternalInput").ap()
    wred_d = nc.dram_tensor("wred", [128, 128], F32, kind="ExternalInput").ap()
    wg_d = nc.dram_tensor("wg", [8, 16], F32, kind="ExternalInput").ap()
    b1t_d = nc.dram_tensor("b1t", [128, 256], F32, kind="ExternalInput").ap()
    b2_d = nc.dram_tensor("b2", [8, 1024], F32, kind="ExternalInput").ap()
    xo_d = nc.dram_tensor("x_out", [TPC, D], F32, kind="ExternalOutput").ap()
    pr.alloc_pbanks(8)
    x_sb, consts, ident = emit_load_basics(pr, x_d, consts_d, ident_d)
    dsem = emit_moe(pr, "A", x_sb, consts, ident, w1_d, w2_d, wred_d, wg_d,
                    b1t_d, b2_d)
    emit_store_x(pr, x_sb, xo_d, waits=[(dsem, 1)])
    return pr.finish(), pr.n_ops


def moe_host_inputs(inp):
    b1 = np.asarray(inp["L0_ffn_b1"], np.float32)
    b1t = np.ascontiguousarray(
        b1.reshape(8, 32, 128).transpose(2, 0, 1).reshape(128, 256))
    wredt = np.ascontiguousarray(
        np.asarray(inp["L0_ffn_w_red"], np.float32).reshape(8, 128, 16)
        .transpose(1, 0, 2).reshape(128, 128))
    return {
        "consts": host_consts(),
        "ident": np.eye(128, dtype=np.float32),
        "w1": np.asarray(inp["L0_ffn_w1"], np.float32),
        "w2": np.asarray(inp["L0_ffn_w2"], np.float32),
        "wred": wredt,
        "wg": np.asarray(inp["L0_ffn_wg"], np.float32),
        "b1t": b1t,
        "b2": np.asarray(inp["L0_ffn_b2"], np.float32),
    }


def shard_rows(full):
    out = []
    for c in range(NCORES):
        b = c // 4
        parts = [full[b * S + a: b * S + bb] for (a, bb) in seq_segments(c)]
        out.append(np.ascontiguousarray(np.concatenate(parts, 0)))
    return out


def unshard_rows(shards):
    full = np.empty((B * S,) + shards[0].shape[1:], shards[0].dtype)
    for c in range(NCORES):
        b = c // 4
        for i, (a, bb) in enumerate(seq_segments(c)):
            full[b * S + a: b * S + bb] = shards[c][i * 512:(i + 1) * 512]
    return full



def emit_l1ffn(P_, tag, x_sb, consts, ident, win_d, bin_d, w1_d, b1_d, w2_d,
               b2_d, wout_d, bout_d, first_waits=()):
    xn_fm = P_.sbuf("xn_fm", [128, NDC * TPC], BF16)
    win_sb = P_.sbuf("win_sb", [128, NDC * 64], BF16)
    sm_sb = P_.sbuf("sm_sb", [64, TPC], F32)
    hbuf = P_.sbuf("hbuf", [64, 2 * TPC], F32)
    g_sb = P_.sbuf("g_sb", [64, TPC], F32)
    gbf = P_.sbuf("gbf", [64, TPC], BF16)
    small = P_.sbuf("ffn_small", [64, 64], F32)
    wout_sb = P_.sbuf("wout_sb", [64, 1024], BF16)
    bout_sb = P_.sbuf("bout_sb", [1, 1024], BF16)
    onesrow = P_.sbuf("onesrow", [1, 128], BF16)

    ld = "fl" + tag
    P_.op("g", lambda e: e.dma_start(out=win_sb[:], in_=win_d),
          waits=list(first_waits), inc=ld, inc_by=16)
    P_.op("g", lambda e: e.dma_start(out=wout_sb[:], in_=wout_d), inc=ld, inc_by=16)
    P_.op("g", lambda e: e.dma_start(out=bout_sb[:], in_=bout_d), inc=ld, inc_by=16)
    lds = "fls" + tag
    P_.op("sp", lambda e: e.dma_start(out=small[:, 0:16], in_=w1_d),
          waits=list(first_waits), inc=lds, inc_by=16)
    P_.op("sp", lambda e: e.dma_start(out=small[:, 16:32], in_=b1_d), inc=lds, inc_by=16)
    P_.op("sp", lambda e: e.dma_start(out=small[:, 32:48], in_=w2_d), inc=lds, inc_by=16)
    P_.op("sp", lambda e: e.dma_start(out=small[:, 48:49], in_=b2_d), inc=lds, inc_by=16)
    P_.op("sp", lambda e: e.dma_start(out=small[:, 49:50], in_=bin_d), inc=lds, inc_by=16)
    ld_done = P_.cur(ld)
    lds_done = P_.cur(lds)
    sv, ss, st_ = "fv" + tag, "fs" + tag, "ttr" + tag
    P_.op("v", lambda e: e.memset(onesrow[:], 1.0), waits=list(first_waits), inc=sv)

    sxn, sxn_tgt = emit_ln_tr(P_, tag, x_sb, xn_fm, consts, ident,
                              first_waits=first_waits, banks=(0, 1))

    for tb in range(4):
        for dc in range(NDC):
            w = [(ld, ld_done), (lds, lds_done), (sxn, sxn_tgt)] if dc == 0 else []
            if dc == 0 and tb > 0:
                w.append((sv, P_.cur(sv)))
            P_.op("t", lambda e, tb=tb, dc=dc: e.matmul(
                P_.pbank[2][0:64, 0:512],
                win_sb[:, dc * 64:(dc + 1) * 64],
                xn_fm[:, dc * TPC + tb * 512: dc * TPC + (tb + 1) * 512],
                start=(dc == 0), stop=(dc == NDC - 1)), waits=w, inc=st_)
        P_.op("v", lambda e, tb=tb: e.tensor_scalar(
            sm_sb[0:64, tb * 512:(tb + 1) * 512], P_.pbank[2][0:64, 0:512],
            small[0:64, 49:50], None, ALU.add),
            waits=[(st_, P_.cur(st_))], inc=sv)
    s_done = P_.cur(sv)

    for i in range(16):
        hsl = hbuf[0:64, (i % 2) * TPC:(i % 2 + 1) * TPC]
        w = [(sv, s_done)]
        if i >= 2:
            w.append((sv, P_.cur(sv)))
        P_.op("s", lambda e, hsl=hsl, i=i: e.activation(
            hsl, sm_sb[:], AF.Relu, bias=small[0:64, 16 + i:17 + i],
            scale=small[0:64, i:i + 1]), waits=w, inc=ss)
        if i == 0:
            P_.op("v", lambda e, hsl=hsl: e.tensor_scalar(
                g_sb[:], hsl, small[0:64, 32:33], None, ALU.mult),
                waits=[(ss, P_.cur(ss))], inc=sv)
        else:
            P_.op("v", lambda e, hsl=hsl, i=i: e.scalar_tensor_tensor(
                g_sb[:], hsl, small[0:64, 32 + i:33 + i], g_sb[:], ALU.mult, ALU.add),
                waits=[(ss, P_.cur(ss)), (sv, P_.cur(sv))], inc=sv)
    P_.op("v", lambda e: e.tensor_scalar(gbf[:], g_sb[:], small[0:64, 48:49],
                                         None, ALU.add),
          waits=[(sv, P_.cur(sv))], inc=sv)
    g_done = P_.cur(sv)

    ye = "ye" + tag
    for tt in range(NTT):
        for do in range(2):
            bank = P_.pbank[3 + (tt * 2 + do) % 4]
            w = [(sv, g_done)]
            if tt * 2 + do >= 4:
                w.append((ye, P_.cur(ye) - 3))
            P_.op("t", lambda e, tt=tt, do=do, bank=bank: e.matmul(
                bank[:], gbf[0:64, tt * 128:(tt + 1) * 128],
                wout_sb[0:64, do * 512:(do + 1) * 512],
                start=True, stop=False), waits=w, inc=st_)
            P_.op("t", lambda e, do=do, bank=bank: e.matmul(
                bank[:], onesrow[:], bout_sb[0:1, do * 512:(do + 1) * 512],
                start=False, stop=True), inc=st_)
            xsl = x_sb[:, tt * 1024 + do * 512: tt * 1024 + do * 512 + 512]
            P_.op("v", lambda e, xsl=xsl, bank=bank: e.tensor_add(xsl, xsl, bank[:]),
                  waits=[(st_, P_.cur(st_))], inc=ye)
    P_.op("v", lambda e: e.engine_nop(),
          waits=[(ye, P_.cur(ye)), (sv, P_.cur(sv)), (ss, P_.cur(ss))],
          inc="done" + tag)
    return "done" + tag


def build_l1ffn_launch():
    pr = P("l1ffn")
    nc = pr.nc
    x_d = nc.dram_tensor("x_in", [TPC, D], F32, kind="ExternalInput").ap()
    consts_d = nc.dram_tensor("consts_in", [128, NCONST], F32, kind="ExternalInput").ap()
    ident_d = nc.dram_tensor("ident_in", [128, 128], F32, kind="ExternalInput").ap()
    win_d = nc.dram_tensor("win", [128, NDC * 64], F32, kind="ExternalInput").ap()
    bin_d = nc.dram_tensor("binp", [64, 1], F32, kind="ExternalInput").ap()
    w1_d = nc.dram_tensor("w1p", [64, 16], F32, kind="ExternalInput").ap()
    b1_d = nc.dram_tensor("b1p", [64, 16], F32, kind="ExternalInput").ap()
    w2_d = nc.dram_tensor("w2p", [64, 16], F32, kind="ExternalInput").ap()
    b2_d = nc.dram_tensor("b2p", [64, 1], F32, kind="ExternalInput").ap()
    wout_d = nc.dram_tensor("woutp", [64, 1024], F32, kind="ExternalInput").ap()
    bout_d = nc.dram_tensor("boutp", [1, 1024], F32, kind="ExternalInput").ap()
    xo_d = nc.dram_tensor("x_out", [TPC, D], F32, kind="ExternalOutput").ap()
    pr.alloc_pbanks(8)
    x_sb, consts, ident = emit_load_basics(pr, x_d, consts_d, ident_d)
    dsem = emit_l1ffn(pr, "F", x_sb, consts, ident, win_d, bin_d, w1_d, b1_d,
                      w2_d, b2_d, wout_d, bout_d)
    emit_store_x(pr, x_sb, xo_d, waits=[(dsem, 1)])
    return pr.finish(), pr.n_ops


def l1ffn_host_inputs(inp):
    win = np.asarray(inp["L1_ffn_Win"], np.float32)
    wint = np.ascontiguousarray(
        win.reshape(8, 128, 64).transpose(1, 0, 2).reshape(128, 8 * 64))
    return {
        "consts_in": host_consts(),
        "ident_in": np.eye(128, dtype=np.float32),
        "win": wint,
        "binp": np.asarray(inp["L1_ffn_bin_"], np.float32)[:, None],
        "w1p": np.asarray(inp["L1_ffn_W1"], np.float32),
        "b1p": np.asarray(inp["L1_ffn_b1"], np.float32),
        "w2p": np.asarray(inp["L1_ffn_W2"], np.float32),
        "b2p": np.asarray(inp["L1_ffn_b2"], np.float32)[:, None],
        "woutp": np.asarray(inp["L1_ffn_Wout"], np.float32),
        "boutp": np.asarray(inp["L1_ffn_bout"], np.float32)[None, :],
    }






def emit_l0a(P_, tag, x_sb, consts, ident, io, first_waits=()):
    nc = P_.nc
    xn_fm = P_.sbuf("xn_fm", [128, NDC * TPC], BF16)
    attn = P_.sbuf("attn", [128, 8 * TPC], BF16)
    arena1 = P_.sbuf("arena1", [128, 8192], BF16)
    arena2 = P_.sbuf("arena2", [128, 8192], BF16)
    arena3 = P_.sbuf("arena3", [128, 8448], BF16)
    qw_sb = P_.sbuf("qw_sb", [128, 1024], BF16)
    q_hp = P_.sbuf("q_hp", [128, 2048], BF16)
    pt_sb = P_.sbuf("pt_sb", [128, 4 * 512], BF16)
    sq = P_.sbuf("sq", [128, 512], F32)
    qkb_sb = P_.sbuf("qkb_sb", [128, 16], F32)
    hmask_sb = P_.sbuf("hmask_sb", [128, 2], F32)
    negmc = P_.sbuf("negmc", [128, 2], F32)
    rows_sb = P_.sbuf("rows_sb", [1, 3 * 128], F32)
    obvb_sb = P_.sbuf("obvb_sb", [1, 1024], BF16)
    onesbf = P_.sbuf("onesbf", [1, 128], BF16)
    mxrow = P_.sbuf("mxrow", [1, 192], F32)
    den_sb = P_.sbuf("den_sb", [1, 1024], F32)
    kg_sb = P_.sbuf("kg_sb", [1, 64], F32)

    kag_i, vag_i, kag_o, vag_o = io["kag_i"], io["vag_i"], io["kag_o"], io["vag_o"]
    groups = [[0, 1, 2, 3], [4, 5, 6, 7]]

    ld, ldg = "al" + tag, "ag" + tag
    sv, ss, st_ = "av" + tag, "as" + tag, "ttr" + tag
    P_.op("sp", lambda e: e.dma_start(out=qkb_sb[:], in_=io["qkb"]),
          waits=list(first_waits), inc=ld, inc_by=16)
    P_.op("sp", lambda e: e.dma_start(out=hmask_sb[:], in_=io["hmask"]), inc=ld, inc_by=16)
    P_.op("sp", lambda e: e.dma_start(out=rows_sb[0:1, 0:128], in_=io["rowA"]), inc=ld, inc_by=16)
    P_.op("sp", lambda e: e.dma_start(out=rows_sb[0:1, 128:256], in_=io["rowB"]), inc=ld, inc_by=16)
    P_.op("sp", lambda e: e.dma_start(out=rows_sb[0:1, 256:384], in_=io["rowOnes"]), inc=ld, inc_by=16)
    P_.op("g", lambda e: e.dma_start(out=obvb_sb[0:1, 0:1024], in_=io["vb_row"]),
          waits=list(first_waits), inc="vbr" + tag, inc_by=16)
    ld_done, ldg_done = P_.cur(ld), P_.cur("vbr" + tag)
    P_.op("v", lambda e: e.memset(onesbf[:], 1.0), waits=list(first_waits), inc=sv)
    P_.op("v", lambda e: e.memset(arena1[:, 5120:7200], 1.0), inc=sv)
    ones_done = P_.cur(sv)

    sxn, sxn_tgt = emit_ln_tr(P_, tag, x_sb, xn_fm, consts, ident,
                              first_waits=first_waits, banks=(0, 1))

    kw_re = io["kw2d"].rearrange("(dc p) n -> dc p n", p=128)
    vw_re = io["vw2d"].rearrange("(dc p) n -> dc p n", p=128)
    qw_re = io["qw2d"].rearrange("(dc p) n -> dc p n", p=128)
    ow_re = io["ow2d"].rearrange("(dc p) n -> dc p n", p=128)

    kwslot_done = []
    norm_reads = []
    kev_done = []
    for hp in range(8):
        slot = hp % 2
        wsem = "kw" + tag + str(slot)
        w = ([(st_, kwslot_done[hp - 2])] if hp >= 2
             else list(first_waits) + [(st_, P_.cur(st_))])
        for dc in range(NDC):
            dst = arena1[:, slot * 1024 + dc * 128: slot * 1024 + dc * 128 + 128]
            P_.op("g", lambda e, dst=dst, dc=dc, hp=hp: e.dma_start(
                out=dst, in_=kw_re[dc, :, hp * 128:(hp + 1) * 128]),
                waits=(w if dc == 0 else ()), inc=wsem, inc_by=16)
        kw_rdy = (wsem, P_.cur(wsem))
        for tb in range(4):
            it = hp * 4 + tb
            bank = P_.pbank[2 + tb % 2]
            spk = "ks" + tag + str(tb % 2)
            for dc in range(NDC):
                mw = []
                if dc == 0:
                    mw = [kw_rdy, (sxn, sxn_tgt)]
                    if it >= 2:
                        mw.append((sv, kev_done[it - 2]))
                P_.op("t", lambda e, bank=bank, slot=slot, dc=dc, tb=tb: e.matmul(
                    bank[:], arena1[:, slot * 1024 + dc * 128: slot * 1024 + dc * 128 + 128],
                    xn_fm[:, dc * TPC + tb * 512: dc * TPC + (tb + 1) * 512],
                    start=(dc == 0), stop=(dc == NDC - 1)), waits=mw, inc=st_)
            mmc = P_.cur(st_)
            kst = arena1[:, 4096 + (tb % 2) * 512: 4096 + (tb % 2) * 512 + 512]
            kev_w = [(st_, mmc), (ld, ld_done)]
            if it >= 2:
                kev_w.append((spk, P_.cur(spk)))
            P_.op("v", lambda e, kst=kst, bank=bank, hp=hp: e.tensor_scalar(
                kst, bank[:], qkb_sb[:, 8 + hp:9 + hp], None, ALU.add),
                waits=kev_w, inc=sv)
            kev_done.append(P_.cur(sv))
            P_.op("s", lambda e, bank=bank: e.activation(sq[:, 0:512], bank[:],
                                                         AF.Square),
                  waits=[(sv, P_.cur(sv)), (st_, P_.cur(st_))], inc=ss)
            nw = [(ss, P_.cur(ss)), (ld, ld_done)]
            if norm_reads:
                nw.append((sv, norm_reads[-1]))
            P_.op("t", lambda e: e.matmul(P_.pbank[6][0:1, 0:512], hmask_sb[:, 0:1],
                                          sq[:, 0:512], start=True, stop=True),
                  waits=nw, inc=st_)
            P_.op("t", lambda e: e.matmul(P_.pbank[7][0:1, 0:512], hmask_sb[:, 1:2],
                                          sq[:, 0:512], start=True, stop=True),
                  inc=st_)
            nc_ = P_.cur(st_)
            c0 = 64 + (2 * hp) * 4 + tb
            c1 = 64 + (2 * hp + 1) * 4 + tb
            P_.op("v", lambda e, c0=c0: e.tensor_reduce(
                mxrow[0:1, c0:c0 + 1], P_.pbank[6][0:1, 0:512], AX.X, ALU.max),
                waits=[(st_, nc_)], inc=sv)
            P_.op("v", lambda e, c1=c1: e.tensor_reduce(
                mxrow[0:1, c1:c1 + 1], P_.pbank[7][0:1, 0:512], AX.X, ALU.max),
                inc=sv)
            norm_reads.append(P_.cur(sv))
            P_.op("sp", lambda e, hp=hp, tb=tb, kst=kst: e.dma_start(
                out=kag_i.ap()[hp * 128:(hp + 1) * 128, tb * 512:(tb + 1) * 512],
                in_=kst), waits=[(sv, P_.cur(sv) - 2)], inc=spk, inc_by=16)
        kwslot_done.append(P_.cur(st_))
    for h in range(16):
        src = mxrow[0:1, 64 + h * 4: 64 + h * 4 + 4]
        P_.op("v", lambda e, src=src, h=h: e.tensor_reduce(
            mxrow[0:1, 128 + h:129 + h], src, AX.X, ALU.max),
            waits=([(sv, P_.cur(sv))] if h == 0 else []), inc=sv)
    P_.op("v", lambda e: e.memset(den_sb[:], 0.0), waits=[(sv, P_.cur(sv))], inc=sv)
    P_.op("v", lambda e: e.tensor_copy(den_sb[0:1, 0:16], mxrow[0:1, 128:144]),
          waits=[(sv, P_.cur(sv))], inc=sv)
    P_.op("g", lambda e: e.dma_start(out=kag_i.ap()[1024:1025, 0:1024],
                                     in_=den_sb[0:1, 0:1024]),
          waits=[(sv, P_.cur(sv))], inc=ldg, inc_by=16)
    P_.op("g", lambda e: e.dma_start(out=kag_i.ap()[1024:1025, 1024:2048],
                                     in_=den_sb[0:1, 0:1024]), inc=ldg, inc_by=16)

    vstage = 5120
    vev_done = []
    for vb in range(2):
        wsem = "vw" + tag + str(vb)
        vww = [(st_, kwslot_done[-1]), (sv, ones_done)]
        if vb == 1:
            vww = [(st_, vb1_wait_st)]
        for dc in range(NDC):
            dst = arena1[:, dc * 512:(dc + 1) * 512]
            P_.op("g", lambda e, dst=dst, dc=dc, vb=vb: e.dma_start(
                out=dst, in_=vw_re[dc, :, vb * 512:(vb + 1) * 512]),
                waits=(vww if dc == 0 else ()), inc=wsem, inc_by=16)
        vw_rdy = (wsem, P_.cur(wsem))
        for tt in range(NTT):
            it = vb * NTT + tt
            bank = P_.pbank[4 + tt % 2]
            spv = "vs" + tag + str(tt % 2)
            for dc in range(NDC):
                mw = []
                if dc == 0:
                    mw = [vw_rdy, ("vbr" + tag, ldg_done)]
                    if it >= 2:
                        mw.append((sv, vev_done[it - 2]))
                P_.op("t", lambda e, bank=bank, dc=dc, tt=tt: e.matmul(
                    bank[:], xn_fm[:, dc * TPC + tt * 128: dc * TPC + (tt + 1) * 128],
                    arena1[:, dc * 512:(dc + 1) * 512],
                    start=(dc == 0), stop=False), waits=mw, inc=st_)
            P_.op("t", lambda e, bank=bank, vb=vb: e.matmul(
                bank[:], onesbf[:], obvb_sb[0:1, vb * 512:(vb + 1) * 512],
                start=False, stop=True), inc=st_)
            mmc = P_.cur(st_)
            off = vstage + (tt % 2) * 1040
            dst3 = arena1[:, off: off + 1040].rearrange(
                "p (h c) -> p h c", h=8)[:, :, 0:64]
            vev_w = [(st_, mmc)]
            if it >= 2:
                vev_w.append((spv, P_.cur(spv)))
            P_.op("v", lambda e, dst3=dst3, bank=bank: e.tensor_copy(
                dst3, bank[:].rearrange("p (h c) -> p h c", h=8)),
                waits=vev_w, inc=sv)
            vev_done.append(P_.cur(sv))
            if tt % 2 == 1:
                for h in range(8):
                    src_even = arena1[:, vstage + h * 130: vstage + h * 130 + 65]
                    src_odd = arena1[:, vstage + 1040 + h * 130: vstage + 1040 + h * 130 + 65]
                    gh = vb * 8 + h
                    ktp = tt // 2
                    P_.op("sp", lambda e, src_even=src_even, gh=gh, ktp=ktp: e.dma_start(
                        out=vag_i.ap()[gh * 128:(gh + 1) * 128, ktp * 130: ktp * 130 + 65],
                        in_=src_even),
                        waits=([("vs" + tag + "0", P_.cur(sv) * 0 + 0)] if False else
                               [(sv, P_.cur(sv))] if h == 0 else ()),
                        inc="vs" + tag + "0", inc_by=16)
                    P_.op("sp", lambda e, src_odd=src_odd, gh=gh, ktp=ktp: e.dma_start(
                        out=vag_i.ap()[gh * 128:(gh + 1) * 128, ktp * 130 + 65: ktp * 130 + 130],
                        in_=src_odd), inc="vs" + tag + "1", inc_by=16)
        if vb == 0:
            vb1_wait_st = P_.cur(st_)
    vmm_done = P_.cur(st_)
    vag_writes_done = [("vs" + tag + "0", P_.cur("vs" + tag + "0")),
                      ("vs" + tag + "1", P_.cur("vs" + tag + "1"))]
    kag_writes_done = [("ks" + tag + "0", P_.cur("ks" + tag + "0")),
                       ("ks" + tag + "1", P_.cur("ks" + tag + "1")),
                       (ldg, P_.cur(ldg))]

    cc = "cc" + tag
    P_.op("g", lambda e: e.collective_compute(
        "AllGather", ALU.bypass, replica_groups=groups,
        ins=[kag_i.ap().opt()], outs=[kag_o.ap().opt()]),
        waits=kag_writes_done + [(sv, P_.cur(sv))], inc=cc)
    P_.op("g", lambda e: e.collective_compute(
        "AllGather", ALU.bypass, replica_groups=groups,
        ins=[vag_i.ap().opt()], outs=[vag_o.ap().opt()]),
        waits=vag_writes_done, inc=cc)
    cc_done = P_.cur(cc)

    cm = "cm" + tag
    P_.op("g", lambda e: e.dma_start(out=arena1[:], in_=io["cmask16"]),
          waits=[(sv, P_.cur(sv)), ("vs" + tag + "0", P_.cur("vs" + tag + "0")),
                 ("vs" + tag + "1", P_.cur("vs" + tag + "1")),
                 ("ks" + tag + "0", P_.cur("ks" + tag + "0")),
                 ("ks" + tag + "1", P_.cur("ks" + tag + "1")),
                 (st_, vmm_done)],
          inc=cm, inc_by=16)
    cm_done = P_.cur(cm)
    kx = "kx" + tag
    for r in range(4):
        P_.op("g", lambda e, r=r: e.dma_start(
            out=kg_sb[0:1, r * 16:(r + 1) * 16],
            in_=kag_o.ap()[r * 1025 + 1024: r * 1025 + 1025, 0:16]),
            waits=([(cc, cc_done)] if r == 0 else ()), inc=kx, inc_by=16)
    kmg = mxrow[0:1, 144:160]
    P_.op("v", lambda e: e.tensor_reduce(
        kmg, kg_sb[0:1, 0:64].rearrange("o (r h) -> o h r", r=4), AX.X, ALU.max),
        waits=[(kx, P_.cur(kx))], inc=sv)
    kmg_done = P_.cur(sv)

    sp2 = "st2" + tag
    sxp = "xp" + tag
    smk = "mk" + tag
    sde = "de" + tag
    P_.sem(sde); P_.sem(sxp); P_.sem(smk)
    ql = "ql" + tag + "0"
    qflash_done_st = []
    for hp in range(8):
        wsem = "qw" + tag + "0"
        qlw = [(st_, qmm_done_st[-1])] if hp >= 1 else [(st_, P_.cur(st_))]
        for dc in range(NDC):
            dst = qw_sb[:, dc * 128: dc * 128 + 128]
            P_.op("g", lambda e, dst=dst, dc=dc, hp=hp: e.dma_start(
                out=dst, in_=qw_re[dc, :, hp * 128:(hp + 1) * 128]),
                waits=(qlw if dc == 0 else ()), inc=wsem, inc_by=16)
        qw_rdy = (wsem, P_.cur(wsem))
        if hp == 0:
            qmm_done_st = []
        qev_q = []
        for tb in range(4):
            bank = P_.pbank[tb % 2]
            for dc in range(NDC):
                mw = []
                if dc == 0:
                    mw = [qw_rdy]
                    if tb < 2:
                        mw.append((sxp, P_.cur(sxp)))
                    else:
                        mw.append((sv, qev_q[tb - 2]))
                P_.op("t", lambda e, bank=bank, dc=dc, tb=tb: e.matmul(
                    bank[:], qw_sb[:, dc * 128: dc * 128 + 128],
                    xn_fm[:, dc * TPC + tb * 512: dc * TPC + (tb + 1) * 512],
                    start=(dc == 0), stop=(dc == NDC - 1)), waits=mw, inc=st_)
            mmc = P_.cur(st_)
            P_.op("v", lambda e, bank=bank, hp=hp, tb=tb: e.tensor_scalar(
                q_hp[:, tb * 512:(tb + 1) * 512], bank[:],
                qkb_sb[:, hp:hp + 1], 0.125, ALU.add, ALU.mult),
                waits=[(st_, mmc), (ld, ld_done)], inc=sv)
            qev_q.append(P_.cur(sv))
            P_.op("s", lambda e, bank=bank: e.activation(
                sq[:, 0:512], bank[:], AF.Square),
                waits=[(sv, P_.cur(sv)), (st_, P_.cur(st_))], inc=ss)
            nw = [(ss, P_.cur(ss)), (sde, P_.cur(sde))]
            if norm_reads:
                nw.append((sv, norm_reads[-1]))
            P_.op("t", lambda e: e.matmul(P_.pbank[6][0:1, 0:512], hmask_sb[:, 0:1],
                                          sq[:, 0:512], start=True, stop=True),
                  waits=nw, inc=st_)
            P_.op("t", lambda e: e.matmul(P_.pbank[7][0:1, 0:512], hmask_sb[:, 1:2],
                                          sq[:, 0:512], start=True, stop=True),
                  inc=st_)
            nc_ = P_.cur(st_)
            P_.op("v", lambda e, tb=tb: e.tensor_reduce(
                mxrow[0:1, tb:tb + 1], P_.pbank[6][0:1, 0:512], AX.X, ALU.max),
                waits=[(st_, nc_)], inc=sv)
            P_.op("v", lambda e, tb=tb: e.tensor_reduce(
                mxrow[0:1, 4 + tb:5 + tb], P_.pbank[7][0:1, 0:512], AX.X, ALU.max),
                inc=sv)
            norm_reads.append(P_.cur(sv))
        qmm_done_st.append(P_.cur(st_))
        P_.op("v", lambda e: e.tensor_reduce(
            mxrow[0:1, 8:9], mxrow[0:1, 0:4], AX.X, ALU.max),
            waits=[(sv, P_.cur(sv)), (sv, kmg_done)], inc=sv)
        P_.op("v", lambda e: e.tensor_reduce(
            mxrow[0:1, 9:10], mxrow[0:1, 4:8], AX.X, ALU.max), inc=sv)
        P_.op("v", lambda e, hp=hp: e.tensor_mul(
            mxrow[0:1, 10:12], mxrow[0:1, 8:10], kmg[0:1, 2 * hp:2 * hp + 2]),
            waits=[(sv, P_.cur(sv))], inc=sv)
        P_.op("s", lambda e: e.activation(mxrow[0:1, 12:14], mxrow[0:1, 10:12],
                                          AF.Ln), waits=[(sv, P_.cur(sv))], inc=ss)
        P_.op("s", lambda e: e.activation(mxrow[0:1, 14:16], mxrow[0:1, 12:14],
                                          AF.Exp, scale=0.5),
              waits=[(ss, P_.cur(ss))], inc=ss)
        P_.op("v", lambda e: e.tensor_scalar(mxrow[0:1, 16:18], mxrow[0:1, 14:16],
                                             -0.125, None, ALU.mult),
              waits=[(ss, P_.cur(ss))], inc=sv)
        P_.op("t", lambda e: e.matmul(P_.pbank[6][:, 0:2], rows_sb[0:1, 256:384],
                                      mxrow[0:1, 16:18], start=True, stop=True),
              waits=[(sv, P_.cur(sv))], inc=st_)
        P_.op("v", lambda e: e.tensor_copy(negmc[:], P_.pbank[6][:, 0:2]),
              waits=[(st_, P_.cur(st_))], inc=sv)
        negm_done = P_.cur(sv)

        stw = [(st_, qflash_done_st[-1])] if qflash_done_st else [(cc, cc_done)]
        nk = 0
        for r in range(4):
            for lb in range(4):
                gq = r + 4 * lb
                P_.op("sp", lambda e, r=r, lb=lb, gq=gq, hp=hp: e.dma_start(
                    out=arena2[:, gq * 512:(gq + 1) * 512],
                    in_=kag_o.ap()[r * 1025 + hp * 128: r * 1025 + (hp + 1) * 128,
                                   lb * 512:(lb + 1) * 512]),
                    waits=(stw if nk == 0 else ()), inc=sp2, inc_by=16)
                nk += 1
        for r in range(4):
            for lb in range(4):
                gq = r + 4 * lb
                for hh in range(2):
                    h = 2 * hp + hh
                    P_.op("sp", lambda e, r=r, lb=lb, gq=gq, h=h, hh=hh: e.dma_start(
                        out=arena3[:, hh * 4224 + gq * 260: hh * 4224 + gq * 260 + 260],
                        in_=vag_o.ap()[r * 2048 + h * 128: r * 2048 + (h + 1) * 128,
                                       lb * 260:(lb + 1) * 260]),
                        inc=sp2, inc_by=16)
        strips_rdy = (sp2, P_.cur(sp2))

        for qb in range(4):
            nkt = 16 * (qb + 1)
            pvb_e, pvb_o = P_.pbank[4], P_.pbank[5]
            exps_d = {}

            def emit_pv(kt, qb=qb, nkt=nkt, pvb_e=pvb_e, pvb_o=pvb_o):
                ptA = pt_sb[:, (kt % 2) * 512:(kt % 2) * 512 + 512]
                ptB = pt_sb[:, 1024 + (kt % 2) * 512: 1024 + (kt % 2) * 512 + 512]
                wloc = kt - 16 * qb
                masked = wloc >= 0
                if masked:
                    cmt = arena1[:, wloc * 512:(wloc + 1) * 512]
                    P_.op("v", lambda e, ptA=ptA, cmt=cmt: e.tensor_mul(
                        ptA, ptA, cmt), waits=[(sxp, exps_d[kt])], inc=smk)
                    P_.op("v", lambda e, ptB=ptB, cmt=cmt: e.tensor_mul(
                        ptB, ptB, cmt), inc=smk)
                pvw = [(smk, P_.cur(smk))] if masked else [(sxp, exps_d[kt])]
                if kt == 0:
                    pvw.append((sde, P_.cur(sde)))
                P_.op("t", lambda e, kt=kt, ptA=ptA, nkt=nkt: e.matmul(
                    pvb_e[0:65, 0:512],
                    arena3[:, kt * 65:(kt + 1) * 65], ptA,
                    start=(kt == 0), stop=(kt == nkt - 1)), waits=pvw, inc=st_)
                P_.op("t", lambda e, kt=kt, ptB=ptB, nkt=nkt: e.matmul(
                    pvb_o[0:65, 0:512],
                    arena3[:, 4224 + kt * 65: 4224 + (kt + 1) * 65], ptB,
                    start=(kt == 0), stop=(kt == nkt - 1)), inc=st_)

            for kt in range(nkt):
                sA = P_.pbank[kt % 2]
                sB = P_.pbank[2 + kt % 2]
                mw = [strips_rdy, (sv, negm_done), (cm, cm_done)]
                if kt >= 2:
                    mw = [(sxp, P_.cur(sxp) - 2)]
                elif qb > 0:
                    mw.append((sxp, P_.cur(sxp)))
                P_.op("t", lambda e, sA=sA, kt=kt, qb=qb: e.matmul(
                    sA[:], arena2[0:64, kt * 128:(kt + 1) * 128],
                    q_hp[0:64, qb * 512:(qb + 1) * 512],
                    start=True, stop=True), waits=mw, inc=st_)
                P_.op("t", lambda e, sB=sB, kt=kt, qb=qb: e.matmul(
                    sB[:], arena2[64:128, kt * 128:(kt + 1) * 128],
                    q_hp[64:128, qb * 512:(qb + 1) * 512],
                    start=True, stop=True, tile_position=(64, 0)), inc=st_)
                smm = P_.cur(st_)
                ptA = pt_sb[:, (kt % 2) * 512:(kt % 2) * 512 + 512]
                ptB = pt_sb[:, 1024 + (kt % 2) * 512: 1024 + (kt % 2) * 512 + 512]
                P_.op("s", lambda e, ptA=ptA, sA=sA: e.activation(
                    ptA, sA[:], AF.Exp, bias=negmc[:, 0:1]),
                    waits=[(st_, smm)], inc=sxp)
                P_.op("s", lambda e, ptB=ptB, sB=sB: e.activation(
                    ptB, sB[:], AF.Exp, bias=negmc[:, 1:2]), inc=sxp)
                exps_d[kt] = P_.cur(sxp)
                if kt >= 1:
                    emit_pv(kt - 1)
            emit_pv(nkt - 1)
            sweep_done = P_.cur(st_)
            denw = [(st_, sweep_done)]
            if hp == 0 and qb == 0:
                denw.append((ldg, P_.cur(ldg)))
            P_.op("v", lambda e: e.tensor_copy(den_sb[0:1, 0:512],
                                               pvb_e[64:65, 0:512]),
                  waits=denw, inc=sde)
            P_.op("v", lambda e: e.tensor_copy(den_sb[0:1, 512:1024],
                                               pvb_o[64:65, 0:512]), inc=sde)
            P_.op("v", lambda e: e.reciprocal(den_sb[0:1, 0:512],
                                              den_sb[0:1, 0:512]),
                  waits=[(sde, P_.cur(sde))], inc=sde)
            P_.op("v", lambda e: e.reciprocal(den_sb[0:1, 512:1024],
                                              den_sb[0:1, 512:1024]), inc=sde)
            rd = P_.cur(sde)
            P_.op("t", lambda e: e.matmul(P_.pbank[6][:], rows_sb[0:1, 0:128],
                                          den_sb[0:1, 0:512],
                                          start=True, stop=False),
                  waits=[(sde, rd), (sv, P_.cur(sv))], inc=st_)
            P_.op("t", lambda e: e.matmul(P_.pbank[6][:], rows_sb[0:1, 128:256],
                                          den_sb[0:1, 512:1024],
                                          start=False, stop=True), inc=st_)
            exm = P_.cur(st_)
            asl = attn[:, hp * 2048 + qb * 512: hp * 2048 + qb * 512 + 512]
            P_.op("v", lambda e, asl=asl: e.tensor_copy(asl[0:64, :],
                                                        pvb_e[0:64, 0:512]),
                  waits=[(st_, exm)], inc=sde)
            P_.op("v", lambda e, asl=asl: e.tensor_copy(asl[64:128, :],
                                                        pvb_o[0:64, 0:512]),
                  inc=sde)
            P_.op("v", lambda e, asl=asl: e.tensor_mul(asl, asl, P_.pbank[6][:]),
                  waits=[(sde, P_.cur(sde))], inc=sde)
        qflash_done_st.append(P_.cur(st_))

    ow = "ow" + tag
    P_.op("g", lambda e: e.dma_start(out=obvb_sb[0:1, 0:1024], in_=io["ob_row"]),
          waits=[(st_, vmm_done)], inc=ow, inc_by=16)
    for hp in range(8):
        for dc_half in range(2):
            P_.op("g", lambda e, hp=hp, dc_half=dc_half: e.dma_start(
                out=arena2[:, hp * 1024 + dc_half * 512: hp * 1024 + dc_half * 512 + 512],
                in_=ow_re[hp, :, dc_half * 512:(dc_half + 1) * 512]),
                waits=([(st_, qflash_done_st[-1])] if hp == 0 and dc_half == 0 else ()),
                inc=ow, inc_by=16)
    ow_rdy = (ow, P_.cur(ow))
    oe = "oe" + tag
    oe_b = P_.curb(oe, "v")
    for qb in range(4):
        for tq in range(4):
            tt = qb * 4 + tq
            for mb in range(2):
                gi = (qb * 4 + tq) * 2 + mb
                bank = P_.pbank[6 + gi % 2]
                for hp in range(8):
                    mw = []
                    if hp == 0:
                        mw = [ow_rdy, (sde, P_.cur(sde))]
                        if gi >= 2:
                            mw.append((oe, oe_b + gi - 1))
                    P_.op("t", lambda e, bank=bank, hp=hp, qb=qb, tq=tq, mb=mb: e.matmul(
                        bank[:], attn[:, hp * 2048 + qb * 512 + tq * 128:
                                      hp * 2048 + qb * 512 + tq * 128 + 128],
                        arena2[:, hp * 1024 + mb * 512: hp * 1024 + mb * 512 + 512],
                        start=(hp == 0), stop=False), waits=mw, inc=st_)
                P_.op("t", lambda e, bank=bank, mb=mb: e.matmul(
                    bank[:], onesbf[:], obvb_sb[0:1, mb * 512:(mb + 1) * 512],
                    start=False, stop=True), inc=st_)
                xsl = x_sb[:, tt * 1024 + mb * 512: tt * 1024 + mb * 512 + 512]
                P_.op("v", lambda e, xsl=xsl, bank=bank: e.tensor_add(
                    xsl, xsl, bank[:]), waits=[(st_, P_.cur(st_))], inc=oe)
    P_.op("v", lambda e: e.engine_nop(),
          waits=[(oe, P_.cur(oe)), (sv, P_.cur(sv)), (ss, P_.cur(ss))],
          inc="done" + tag)
    return "done" + tag


def build_l0a_launch():
    pr = P("l0a")
    nc = pr.nc
    x_d = nc.dram_tensor("x_in", [TPC, D], F32, kind="ExternalInput").ap()
    consts_d = nc.dram_tensor("consts_in", [128, NCONST], F32, kind="ExternalInput").ap()
    ident_d = nc.dram_tensor("ident_in", [128, 128], F32, kind="ExternalInput").ap()
    io = {}
    for nm, shp in [("qw2d", [1024, 1024]), ("kw2d", [1024, 1024]),
                    ("vw2d", [1024, 1024]), ("ow2d", [1024, 1024]),
                    ("qkb", [128, 16]), ("vb_row", [1, 1024]),
                    ("ob_row", [1, 1024]), ("hmask", [128, 2]),
                    ("rowA", [1, 128]), ("rowB", [1, 128]),
                    ("rowOnes", [1, 128]), ("cmask16", [128, 16 * 512])]:
        io[nm] = nc.dram_tensor(nm, shp, F32, kind="ExternalInput").ap()
    io["kag_i"] = nc.dram_tensor("kag_i", [1025, 2048], BF16, kind="Internal")
    io["vag_i"] = nc.dram_tensor("vag_i", [2048, 1040], BF16, kind="Internal")
    io["kag_o"] = nc.dram_tensor("kag_o", [4100, 2048], BF16, kind="Internal")
    io["vag_o"] = nc.dram_tensor("vag_o", [8192, 1040], BF16, kind="Internal")
    xo_d = nc.dram_tensor("x_out", [TPC, D], F32, kind="ExternalOutput").ap()
    pr.alloc_pbanks(8)
    x_sb, consts, ident = emit_load_basics(pr, x_d, consts_d, ident_d)
    dsem = emit_l0a(pr, "Q", x_sb, consts, ident, io)
    emit_store_x(pr, x_sb, xo_d, waits=[(dsem, 1)])
    return pr.finish(), pr.n_ops


def l0a_host_inputs(inp, core):
    j = core % 4
    qw = np.asarray(inp["L0_attn_a_Q_w"], np.float32)
    kw = np.asarray(inp["L0_attn_a_K_w"], np.float32)
    vw = np.asarray(inp["L0_attn_a_V_w"], np.float32)
    owt = np.asarray(inp["L0_attn_a_O_w"], np.float32)
    qb = np.asarray(inp["L0_attn_a_Q_b"], np.float32)
    kb = np.asarray(inp["L0_attn_a_K_b"], np.float32)
    vb = np.asarray(inp["L0_attn_a_V_b"], np.float32)
    ob = np.asarray(inp["L0_attn_a_O_b"], np.float32)
    qkb = np.zeros((128, 16), np.float32)
    for hp in range(8):
        qkb[0:64, hp] = qb[2 * hp]
        qkb[64:128, hp] = qb[2 * hp + 1]
        qkb[0:64, 8 + hp] = kb[2 * hp]
        qkb[64:128, 8 + hp] = kb[2 * hp + 1]
    hm = np.zeros((128, 2), np.float32)
    hm[0:64, 0] = 1.0
    hm[64:128, 1] = 1.0
    rowA = np.zeros((1, 128), np.float32); rowA[0, 0:64] = 1.0
    rowB = np.zeros((1, 128), np.float32); rowB[0, 64:128] = 1.0
    rowOnes = np.ones((1, 128), np.float32)
    cm = np.zeros((128, 16 * 512), np.float32)
    jj = np.arange(128)[:, None]
    ii = np.arange(512)[None, :]
    for w in range(16):
        cm[:, w * 512:(w + 1) * 512] = (128 * w + jj <= 512 * j + ii)
    return {
        "consts_in": host_consts(),
        "ident_in": np.eye(128, dtype=np.float32),
        "qw2d": np.ascontiguousarray(qw.transpose(1, 0, 2).reshape(1024, 1024)),
        "kw2d": np.ascontiguousarray(kw.transpose(1, 0, 2).reshape(1024, 1024)),
        "vw2d": np.ascontiguousarray(vw.transpose(1, 0, 2).reshape(1024, 1024)),
        "ow2d": np.ascontiguousarray(owt.reshape(1024, 1024)),
        "qkb": qkb, "vb_row": vb.reshape(1, 1024), "ob_row": ob.reshape(1, 1024),
        "hmask": hm, "rowA": rowA, "rowB": rowB, "rowOnes": rowOnes,
        "cmask16": cm,
    }




def slopes16():
    start = 2.0 ** (-2.0 ** (-(np.log2(16) - 3.0)))
    return np.float64(start) ** np.arange(1, 17)


def emit_l0b(P_, tag, x_sb, consts, ident, io, first_waits=()):
    nc = P_.nc
    xn_fm = P_.sbuf("xn_fm", [128, NDC * TPC], BF16)
    attn = P_.sbuf("attn", [128, 8 * TPC], BF16)
    arena1 = P_.sbuf("arena1", [128, 8192], BF16)
    arena2 = P_.sbuf("arena2", [128, 8192], BF16)
    arena3 = P_.sbuf("arena3", [128, 8448], BF16)
    qw_sb = P_.sbuf("qw_sb", [128, 1024], BF16)
    q_hp = P_.sbuf("q_hp", [128, 2048], BF16)
    pt_sb = P_.sbuf("pt_sb", [128, 4 * 512], BF16)
    sq = P_.sbuf("sq", [128, 512], F32)
    hmask_sb = P_.sbuf("hmask_sb", [128, 2], F32)
    negmc = P_.sbuf("negmc", [128, 2], F32)
    rows_sb = P_.sbuf("rows_sb", [1, 3 * 128], F32)
    obvb_sb = P_.sbuf("obvb_sb", [1, 1024], BF16)
    onesbf = P_.sbuf("onesbf", [1, 128], BF16)
    mxrow = P_.sbuf("mxrow", [1, 192], F32)
    den_sb = P_.sbuf("den_sb", [1, 1024], F32)
    kg_sb = P_.sbuf("kg_sb", [1, 64], F32)
    bcols = P_.sbuf("bcols", [128, 65], F32)
    gjc = P_.sbuf("gjc", [128, 64], F32)
    iirow = P_.sbuf("iirow", [1, 512], F32)
    qpos = P_.sbuf("qpos", [1, 8], F32)

    kag_i, vag_i, kag_o, vag_o = io["kag_i"], io["vag_i"], io["kag_o"], io["vag_o"]
    groups = [[0, 1, 2, 3], [4, 5, 6, 7]]
    SL = slopes16()

    ld, sv, ss, st_ = "bl" + tag, "bv" + tag, "bs" + tag, "ttr" + tag
    sde, sxp, smk = "de" + tag, "xp" + tag, "mk" + tag
    P_.op("sp", lambda e: e.dma_start(out=hmask_sb[:], in_=io["hmask"]),
          waits=list(first_waits), inc=ld, inc_by=16)
    P_.op("sp", lambda e: e.dma_start(out=rows_sb[0:1, 0:128], in_=io["rowA"]), inc=ld, inc_by=16)
    P_.op("sp", lambda e: e.dma_start(out=rows_sb[0:1, 128:256], in_=io["rowB"]), inc=ld, inc_by=16)
    P_.op("sp", lambda e: e.dma_start(out=rows_sb[0:1, 256:384], in_=io["rowOnes"]), inc=ld, inc_by=16)
    P_.op("sp", lambda e: e.dma_start(out=gjc[:], in_=io["gjcols"]), inc=ld, inc_by=16)
    P_.op("sp", lambda e: e.dma_start(out=iirow[:], in_=io["iirow"]), inc=ld, inc_by=16)
    P_.op("sp", lambda e: e.dma_start(out=qpos[0:1, 0:4], in_=io["qpos"]), inc=ld, inc_by=16)
    P_.op("g", lambda e: e.dma_start(out=obvb_sb[:], in_=io["bproj_row"]),
          waits=list(first_waits), inc="vbr" + tag, inc_by=16)
    ld_done = P_.cur(ld)
    vbr_done = P_.cur("vbr" + tag)
    P_.op("v", lambda e: e.memset(onesbf[:], 1.0), waits=list(first_waits), inc=sv)
    P_.op("v", lambda e: e.memset(arena1[:, 5120:6160], 1.0), inc=sv)
    ones_done = P_.cur(sv)

    sxn, sxn_tgt = emit_ln_tr(P_, tag, x_sb, xn_fm, consts, ident,
                              first_waits=first_waits, banks=(0, 1))

    kvw_re = io["kvw2d"].rearrange("(dc p) n -> dc p n", p=128)
    qw_re = io["qw2d"].rearrange("(dc p) n -> dc p n", p=128)
    pw_re = io["wproj2d"].rearrange("(dc p) n -> dc p n", p=128)

    kvw = "kvw" + tag
    for dc in range(NDC):
        P_.op("g", lambda e, dc=dc: e.dma_start(
            out=arena1[:, dc * 512:(dc + 1) * 512], in_=kvw_re[dc]),
            waits=(list(first_waits) + [(st_, P_.cur(st_))] if dc == 0 else ()),
            inc=kvw, inc_by=16)
    kvw_done = P_.cur(kvw)

    norm_reads = []
    kev_done = []
    for kvp in range(2):
        for tb in range(4):
            it = kvp * 4 + tb
            bank = P_.pbank[2 + tb % 2]
            spk = "ks" + tag + str(tb % 2)
            for dc in range(NDC):
                mw = []
                if dc == 0:
                    mw = [(kvw, kvw_done), (sxn, sxn_tgt)]
                    if it >= 2:
                        mw.append((sv, kev_done[it - 2]))
                P_.op("t", lambda e, bank=bank, kvp=kvp, dc=dc, tb=tb: e.matmul(
                    bank[:], arena1[:, dc * 512 + kvp * 128: dc * 512 + kvp * 128 + 128],
                    xn_fm[:, dc * TPC + tb * 512: dc * TPC + (tb + 1) * 512],
                    start=(dc == 0), stop=(dc == NDC - 1)), waits=mw, inc=st_)
            mmc = P_.cur(st_)
            kst = arena1[:, 4096 + (tb % 2) * 512: 4096 + (tb % 2) * 512 + 512]
            kev_w = [(st_, mmc)]
            if it >= 2:
                kev_w.append((spk, P_.cur(spk)))
            P_.op("v", lambda e, kst=kst, bank=bank: e.tensor_copy(kst, bank[:]),
                  waits=kev_w, inc=sv)
            kev_done.append(P_.cur(sv))
            P_.op("s", lambda e, bank=bank: e.activation(sq[:, 0:512], bank[:],
                                                         AF.Square),
                  waits=[(sv, P_.cur(sv)), (st_, P_.cur(st_))], inc=ss)
            nw = [(ss, P_.cur(ss)), (ld, ld_done)]
            if norm_reads:
                nw.append((sv, norm_reads[-1]))
            P_.op("t", lambda e: e.matmul(P_.pbank[6][0:1, 0:512], hmask_sb[:, 0:1],
                                          sq[:, 0:512], start=True, stop=True),
                  waits=nw, inc=st_)
            P_.op("t", lambda e: e.matmul(P_.pbank[7][0:1, 0:512], hmask_sb[:, 1:2],
                                          sq[:, 0:512], start=True, stop=True),
                  inc=st_)
            nc_ = P_.cur(st_)
            c0 = 64 + (2 * kvp) * 4 + tb
            c1 = 64 + (2 * kvp + 1) * 4 + tb
            P_.op("v", lambda e, c0=c0: e.tensor_reduce(
                mxrow[0:1, c0:c0 + 1], P_.pbank[6][0:1, 0:512], AX.X, ALU.max),
                waits=[(st_, nc_)], inc=sv)
            P_.op("v", lambda e, c1=c1: e.tensor_reduce(
                mxrow[0:1, c1:c1 + 1], P_.pbank[7][0:1, 0:512], AX.X, ALU.max),
                inc=sv)
            norm_reads.append(P_.cur(sv))
            P_.op("sp", lambda e, kvp=kvp, tb=tb, kst=kst: e.dma_start(
                out=kag_i.ap()[kvp * 128:(kvp + 1) * 128, tb * 512:(tb + 1) * 512],
                in_=kst), waits=[(sv, P_.cur(sv) - 2)], inc=spk, inc_by=16)
    for kv4 in range(4):
        src = mxrow[0:1, 64 + kv4 * 4: 64 + kv4 * 4 + 4]
        P_.op("v", lambda e, src=src, kv4=kv4: e.tensor_reduce(
            mxrow[0:1, 128 + kv4:129 + kv4], src, AX.X, ALU.max),
            waits=([(sv, P_.cur(sv))] if kv4 == 0 else []), inc=sv)
    P_.op("v", lambda e: e.memset(den_sb[:], 0.0), waits=[(sv, P_.cur(sv))], inc=sv)
    P_.op("v", lambda e: e.tensor_copy(den_sb[0:1, 0:4], mxrow[0:1, 128:132]),
          waits=[(sv, P_.cur(sv))], inc=sv)
    ldg = "ag" + tag
    P_.op("g", lambda e: e.dma_start(out=kag_i.ap()[256:257, 0:1024],
                                     in_=den_sb[0:1, 0:1024]),
          waits=[(sv, P_.cur(sv))], inc=ldg, inc_by=16)
    P_.op("g", lambda e: e.dma_start(out=kag_i.ap()[256:257, 1024:2048],
                                     in_=den_sb[0:1, 0:1024]), inc=ldg, inc_by=16)

    vstage = 5120
    vev_done = []
    kproj_done = P_.cur(st_)
    for tt in range(NTT):
        bank = P_.pbank[4 + tt % 2]
        spv = "vs" + tag + str(tt % 2)
        for dc in range(NDC):
            mw = []
            if dc == 0:
                mw = [(kvw, kvw_done), (sv, ones_done)]
                if tt >= 2:
                    mw.append((sv, vev_done[tt - 2]))
            P_.op("t", lambda e, bank=bank, dc=dc, tt=tt: e.matmul(
                bank[0:128, 0:256],
                xn_fm[:, dc * TPC + tt * 128: dc * TPC + (tt + 1) * 128],
                arena1[:, dc * 512 + 256: dc * 512 + 512],
                start=(dc == 0), stop=(dc == NDC - 1)), waits=mw, inc=st_)
        mmc = P_.cur(st_)
        off = vstage + (tt % 2) * 520
        dst3 = arena1[:, off: off + 520].rearrange("p (h c) -> p h c", h=4)[:, :, 0:64]
        vev_w = [(st_, mmc)]
        if tt >= 2:
            vev_w.append((spv, P_.cur(spv)))
        P_.op("v", lambda e, dst3=dst3, bank=bank: e.tensor_copy(
            dst3, bank[0:128, 0:256].rearrange("p (h c) -> p h c", h=4)),
            waits=vev_w, inc=sv)
        vev_done.append(P_.cur(sv))
        if tt % 2 == 1:
            for h in range(4):
                se_ = arena1[:, vstage + h * 130: vstage + h * 130 + 65]
                so_ = arena1[:, vstage + 520 + h * 130: vstage + 520 + h * 130 + 65]
                ktp = tt // 2
                P_.op("sp", lambda e, se_=se_, h=h, ktp=ktp: e.dma_start(
                    out=vag_i.ap()[h * 128:(h + 1) * 128, ktp * 130: ktp * 130 + 65],
                    in_=se_), waits=([(sv, P_.cur(sv))] if h == 0 else ()),
                    inc="vs" + tag + "0", inc_by=16)
                P_.op("sp", lambda e, so_=so_, h=h, ktp=ktp: e.dma_start(
                    out=vag_i.ap()[h * 128:(h + 1) * 128, ktp * 130 + 65: ktp * 130 + 130],
                    in_=so_), inc="vs" + tag + "1", inc_by=16)
    vmm_done = P_.cur(st_)

    cc = "cc" + tag
    P_.op("g", lambda e: e.collective_compute(
        "AllGather", ALU.bypass, replica_groups=groups,
        ins=[kag_i.ap().opt()], outs=[kag_o.ap().opt()]),
        waits=[("ks" + tag + "0", P_.cur("ks" + tag + "0")),
               ("ks" + tag + "1", P_.cur("ks" + tag + "1")),
               (ldg, P_.cur(ldg))], inc=cc)
    P_.op("g", lambda e: e.collective_compute(
        "AllGather", ALU.bypass, replica_groups=groups,
        ins=[vag_i.ap().opt()], outs=[vag_o.ap().opt()]),
        waits=[("vs" + tag + "0", P_.cur("vs" + tag + "0")),
               ("vs" + tag + "1", P_.cur("vs" + tag + "1"))], inc=cc)
    cc_done = P_.cur(cc)

    kx = "kx" + tag
    for r in range(4):
        P_.op("g", lambda e, r=r: e.dma_start(
            out=kg_sb[0:1, r * 16:r * 16 + 4],
            in_=kag_o.ap()[r * 257 + 256: r * 257 + 257, 0:4]),
            waits=([(cc, cc_done)] if r == 0 else ()), inc=kx, inc_by=16)
    kmg = mxrow[0:1, 144:148]
    P_.op("v", lambda e: e.tensor_reduce(
        kmg, kg_sb[0:1, 0:64].rearrange("o (r h) -> o h r", r=4)[:, 0:4, :],
        AX.X, ALU.max), waits=[(kx, P_.cur(kx))], inc=sv)
    kmg_done = P_.cur(sv)

    sp2 = "st2" + tag
    at_ = "at" + tag
    flash_done_st = []
    for kvh in range(4):
        stw = [(st_, flash_done_st[-1])] if flash_done_st else [(cc, cc_done), (st_, vmm_done)]
        P_.op("v", lambda e: e.memset(arena2[64:65, 0:8192], 1.0),
              waits=([(sv, kmg_done)] if kvh == 0 else [(sxp, P_.cur(sxp))]), inc=sv)
        ones_k = P_.cur(sv)
        nk = 0
        for r in range(4):
            for lb in range(4):
                gq = r + 4 * lb
                P_.op("sp", lambda e, r=r, lb=lb, gq=gq, kvh=kvh: e.dma_start(
                    out=arena2[0:64, gq * 512:(gq + 1) * 512],
                    in_=kag_o.ap()[r * 257 + kvh * 64: r * 257 + (kvh + 1) * 64,
                                   lb * 512:(lb + 1) * 512]),
                    waits=(stw if nk == 0 else ()), inc=sp2, inc_by=16)
                nk += 1
                P_.op("sp", lambda e, r=r, lb=lb, gq=gq, kvh=kvh: e.dma_start(
                    out=arena3[:, gq * 260:(gq + 1) * 260],
                    in_=vag_o.ap()[r * 512 + kvh * 128: r * 512 + (kvh + 1) * 128,
                                   lb * 260:(lb + 1) * 260]),
                    inc=sp2, inc_by=16)
        strips_rdy = (sp2, P_.cur(sp2))

        for hh in range(2):
            hA = kvh + 8 * hh
            hB = kvh + 4 + 8 * hh
            wsem = "qw" + tag + "0"
            qlw = [(st_, P_.cur(st_))]
            for dc in range(NDC):
                P_.op("g", lambda e, dc=dc, hA=hA, hB=hB: e.dma_start(
                    out=qw_sb[:, dc * 128: dc * 128 + 64],
                    in_=qw_re[dc, :, hA * 64:(hA + 1) * 64]),
                    waits=(qlw if dc == 0 else ()), inc=wsem, inc_by=16)
                P_.op("g", lambda e, dc=dc, hA=hA, hB=hB: e.dma_start(
                    out=qw_sb[:, dc * 128 + 64: dc * 128 + 128],
                    in_=qw_re[dc, :, hB * 64:(hB + 1) * 64]), inc=wsem, inc_by=16)
            qw_rdy = (wsem, P_.cur(wsem))
            qev = []
            for tb in range(4):
                bank = P_.pbank[tb % 2]
                for dc in range(NDC):
                    mw = []
                    if dc == 0:
                        mw = [qw_rdy, (sxp, P_.cur(sxp))]
                        if tb >= 2:
                            mw.append((sv, qev[tb - 2]))
                    P_.op("t", lambda e, bank=bank, dc=dc, tb=tb: e.matmul(
                        bank[:], qw_sb[:, dc * 128: dc * 128 + 128],
                        xn_fm[:, dc * TPC + tb * 512: dc * TPC + (tb + 1) * 512],
                        start=(dc == 0), stop=(dc == NDC - 1)), waits=mw, inc=st_)
                mmc = P_.cur(st_)
                P_.op("v", lambda e, bank=bank, tb=tb: e.tensor_scalar(
                    q_hp[0:64, tb * 512:(tb + 1) * 512], bank[0:64, :],
                    0.125, None, ALU.mult), waits=[(st_, mmc)], inc=sv)
                P_.op("v", lambda e, bank=bank, tb=tb: e.tensor_scalar(
                    arena3[0:64, 4224 + tb * 512: 4224 + (tb + 1) * 512],
                    bank[64:128, :], 0.125, None, ALU.mult), inc=sv)
                P_.op("s", lambda e, bank=bank: e.activation(
                    sq[:, 0:512], bank[:], AF.Square), waits=[(sv, P_.cur(sv))], inc=ss)
                nw = [(ss, P_.cur(ss)), (sde, P_.cur(sde))]
                if norm_reads:
                    nw.append((sv, norm_reads[-1]))
                P_.op("t", lambda e: e.matmul(P_.pbank[6][0:1, 0:512], hmask_sb[:, 0:1],
                                              sq[:, 0:512], start=True, stop=True),
                      waits=nw, inc=st_)
                P_.op("t", lambda e: e.matmul(P_.pbank[7][0:1, 0:512], hmask_sb[:, 1:2],
                                              sq[:, 0:512], start=True, stop=True),
                      inc=st_)
                nc_ = P_.cur(st_)
                P_.op("v", lambda e, tb=tb: e.tensor_reduce(
                    mxrow[0:1, tb:tb + 1], P_.pbank[6][0:1, 0:512], AX.X, ALU.max),
                    waits=[(st_, nc_)], inc=sv)
                P_.op("v", lambda e, tb=tb: e.tensor_reduce(
                    mxrow[0:1, 4 + tb:5 + tb], P_.pbank[7][0:1, 0:512], AX.X, ALU.max),
                    inc=sv)
                norm_reads.append(P_.cur(sv))
                qev.append(P_.cur(sv))
            P_.op("v", lambda e: e.tensor_reduce(
                mxrow[0:1, 8:9], mxrow[0:1, 0:4], AX.X, ALU.max),
                waits=[(sv, P_.cur(sv)), (sv, kmg_done)], inc=sv)
            P_.op("v", lambda e: e.tensor_reduce(
                mxrow[0:1, 9:10], mxrow[0:1, 4:8], AX.X, ALU.max), inc=sv)
            P_.op("v", lambda e, kvh=kvh: e.tensor_scalar(
                mxrow[0:1, 10:12], mxrow[0:1, 8:10], kmg[0:1, kvh:kvh + 1],
                None, ALU.mult), waits=[(sv, P_.cur(sv))], inc=sv)
            P_.op("s", lambda e: e.activation(mxrow[0:1, 12:14], mxrow[0:1, 10:12],
                                              AF.Ln), waits=[(sv, P_.cur(sv))], inc=ss)
            P_.op("s", lambda e: e.activation(mxrow[0:1, 14:16], mxrow[0:1, 12:14],
                                              AF.Exp, scale=0.5),
                  waits=[(ss, P_.cur(ss))], inc=ss)
            P_.op("v", lambda e: e.tensor_scalar(mxrow[0:1, 16:18], mxrow[0:1, 14:16],
                                                 -0.125, None, ALU.mult),
                  waits=[(ss, P_.cur(ss))], inc=sv)
            P_.op("t", lambda e: e.matmul(P_.pbank[6][:, 0:2], rows_sb[0:1, 256:384],
                                          mxrow[0:1, 16:18], start=True, stop=True),
                  waits=[(sv, P_.cur(sv)), (sde, P_.cur(sde))], inc=st_)
            P_.op("v", lambda e: e.tensor_copy(negmc[:], P_.pbank[6][:, 0:2]),
                  waits=[(st_, P_.cur(st_))], inc=sv)
            negm_done = P_.cur(sv)

            for hx, h in ((0, hA), (1, hB)):
                qbuf = q_hp if hx == 0 else None
                slope8 = float(SL[h] / 8.0)
                if hx == 0:
                    qrow = q_hp[64:65, 0:2048]
                else:
                    qrow = arena3[64:65, 4224:6272]
                for r4 in range(4):
                    P_.op("v", lambda e, qrow=qrow, r4=r4, slope8=slope8: e.tensor_scalar(
                        qrow[0:1, r4 * 512:(r4 + 1) * 512], iirow[0:1, 0:512],
                        -slope8, None, ALU.mult),
                        waits=([(sv, negm_done), (ld, ld_done)] if r4 == 0 else []),
                        inc=sv)
                qrow_done = P_.cur(sv)
                P_.op("g", lambda e, h=h: e.dma_start(
                    out=arena1[:, 0:2944], in_=io["alibiT"][h]),
                    waits=[(smk, P_.cur(smk))] +
                          ([(st_, vmm_done),
                            ("ks" + tag + "0", P_.cur("ks" + tag + "0")),
                            ("ks" + tag + "1", P_.cur("ks" + tag + "1"))]
                           if kvh == 0 and hh == 0 and hx == 0 else []),
                    inc=at_, inc_by=16)
                at_done = P_.cur(at_)
                for qb in range(4):
                    P_.op("v", lambda e, qb=qb, slope8=slope8: e.tensor_scalar(
                        mxrow[0:1, 20:21], qpos[0:1, qb:qb + 1], -slope8, None,
                        ALU.mult), waits=[(sv, P_.cur(sv)), (ld, ld_done)], inc=sv)
                    P_.op("v", lambda e, hx=hx: e.tensor_add(
                        mxrow[0:1, 20:21], mxrow[0:1, 20:21],
                        mxrow[0:1, 16 + hx:17 + hx]), waits=[(sv, P_.cur(sv))], inc=sv)
                    P_.op("t", lambda e: e.matmul(
                        P_.pbank[6][:, 4:5], rows_sb[0:1, 256:384],
                        mxrow[0:1, 20:21], start=True, stop=True),
                        waits=[(sv, P_.cur(sv)), (sde, P_.cur(sde))], inc=st_)
                    P_.op("v", lambda e: e.tensor_copy(bcols[:, 64:65],
                                                       P_.pbank[6][:, 4:5]),
                          waits=[(st_, P_.cur(st_))], inc=sv)
                    P_.op("v", lambda e, slope8=slope8: e.tensor_scalar(
                        bcols[:, 0:64], gjc[:], slope8, bcols[:, 64:65],
                        ALU.mult, ALU.add),
                        waits=[(sv, P_.cur(sv))], inc=sv)
                    bcols_done = P_.cur(sv)
                    k = qb
                    pvb = P_.pbank[4]
                    exps_done = {}

                    def emit_pv(kt, k=k, pvb=pvb, at_done=at_done, hx=hx):
                        win = 16 * k <= kt < 16 * k + 20
                        pt = pt_sb[:, (kt % 3) * 512:(kt % 3) * 512 + 512]
                        if win:
                            o = 2432 - 128 * (kt - 16 * k)
                            P_.op("v", lambda e, pt=pt, o=o: e.tensor_mul(
                                pt, pt, arena1[:, o:o + 512]),
                                waits=[(sxp, exps_done[kt]), (at_, at_done)],
                                inc=smk)
                        pvw = ([(smk, P_.cur(smk))] if win
                               else [(sxp, exps_done[kt])])
                        if kt == 0:
                            pvw.append((sde, P_.cur(sde)))
                        P_.op("t", lambda e, kt=kt, pt=pt, pvb=pvb: e.matmul(
                            pvb[0:65, 0:512], arena3[:, kt * 65:(kt + 1) * 65], pt,
                            start=(kt == 0), stop=(kt == 63)), waits=pvw, inc=st_)

                    for kt in range(64):
                        sbk = P_.pbank[kt % 2]
                        past = kt < 16 * k
                        mw = [(sv, bcols_done)]
                        if kt < 2:
                            if hh == 0 and hx == 0 and qb == 0:
                                mw.append(strips_rdy)
                            mw.append((sxp, P_.cur(sxp)))
                        else:
                            mw = [(sxp, P_.cur(sxp) - 1), (sv, bcols_done)]
                        KK = 65 if past else 64
                        if hx == 0:
                            qap = q_hp[0:KK, qb * 512:(qb + 1) * 512]
                        else:
                            qap = arena3[0:KK, 4224 + qb * 512: 4224 + (qb + 1) * 512]
                        P_.op("t", lambda e, sbk=sbk, kt=kt, KK=KK, qap=qap: e.matmul(
                            sbk[:], arena2[0:KK, kt * 128:(kt + 1) * 128], qap,
                            start=True, stop=True), waits=mw, inc=st_)
                        smm = P_.cur(st_)
                        pt = pt_sb[:, (kt % 3) * 512:(kt % 3) * 512 + 512]
                        bias = bcols[:, kt:kt + 1] if past else negmc[:, hx:hx + 1]
                        P_.op("s", lambda e, pt=pt, sbk=sbk, bias=bias: e.activation(
                            pt, sbk[:], AF.Exp, bias=bias),
                            waits=[(st_, smm)], inc=sxp)
                        exps_done[kt] = P_.cur(sxp)
                        if kt >= 1:
                            emit_pv(kt - 1)
                    emit_pv(63)
                    sweep_done = P_.cur(st_)
                    denw = [(st_, sweep_done)]
                    if kvh == 0 and hh == 0 and hx == 0 and qb == 0:
                        denw.append((ldg, P_.cur(ldg)))
                    P_.op("v", lambda e: e.tensor_copy(den_sb[0:1, 0:512],
                                                       pvb[64:65, 0:512]),
                          waits=denw, inc=sde)
                    P_.op("v", lambda e: e.reciprocal(den_sb[0:1, 0:512],
                                                      den_sb[0:1, 0:512]),
                          waits=[(sde, P_.cur(sde))], inc=sde)
                    rd = P_.cur(sde)
                    rowm = rows_sb[0:1, 0:128] if h % 2 == 0 else rows_sb[0:1, 128:256]
                    P_.op("t", lambda e, rowm=rowm: e.matmul(
                        P_.pbank[6][:], rowm, den_sb[0:1, 0:512],
                        start=True, stop=True),
                        waits=[(sde, rd), (sv, P_.cur(sv))], inc=st_)
                    exm = P_.cur(st_)
                    hp = h // 2
                    r0 = (h % 2) * 64
                    asl = attn[r0:r0 + 64, hp * 2048 + qb * 512: hp * 2048 + qb * 512 + 512]
                    P_.op("v", lambda e, asl=asl: e.tensor_copy(asl, pvb[0:64, 0:512]),
                          waits=[(st_, exm)], inc=sde)
                    P_.op("v", lambda e, asl=asl, r0=r0: e.tensor_mul(
                        asl, asl, P_.pbank[6][r0:r0 + 64, 0:512]),
                        waits=[(sde, P_.cur(sde))], inc=sde)
        flash_done_st.append(P_.cur(st_))

    ow = "ow" + tag
    for hp in range(8):
        for half in range(2):
            P_.op("g", lambda e, hp=hp, half=half: e.dma_start(
                out=arena2[:, hp * 1024 + half * 512: hp * 1024 + half * 512 + 512],
                in_=pw_re[hp, :, half * 512:(half + 1) * 512]),
                waits=([(st_, flash_done_st[-1])] if hp == 0 and half == 0 else ()),
                inc=ow, inc_by=16)
    ow_rdy = (ow, P_.cur(ow))
    oe = "oe" + tag
    oe_b = P_.curb(oe, "v")
    for tt in range(NTT):
        qb, tq = tt // 4, tt % 4
        for mb in range(2):
            gi = tt * 2 + mb
            bank = P_.pbank[6 + gi % 2]
            for hp in range(8):
                mw = []
                if hp == 0:
                    mw = [ow_rdy, (sde, P_.cur(sde)), ("vbr" + tag, vbr_done)]
                    if gi >= 2:
                        mw.append((oe, oe_b + gi - 1))
                P_.op("t", lambda e, bank=bank, hp=hp, tt=tt, mb=mb: e.matmul(
                    bank[:], attn[:, hp * 2048 + tt * 128: hp * 2048 + tt * 128 + 128],
                    arena2[:, hp * 1024 + mb * 512: hp * 1024 + mb * 512 + 512],
                    start=(hp == 0), stop=False), waits=mw, inc=st_)
            P_.op("t", lambda e, bank=bank, mb=mb: e.matmul(
                bank[:], onesbf[:], obvb_sb[0:1, mb * 512:(mb + 1) * 512],
                start=False, stop=True), inc=st_)
            xsl = x_sb[:, tt * 1024 + mb * 512: tt * 1024 + mb * 512 + 512]
            P_.op("v", lambda e, xsl=xsl, bank=bank: e.tensor_add(
                xsl, xsl, bank[:]), waits=[(st_, P_.cur(st_))], inc=oe)
    P_.op("v", lambda e: e.engine_nop(),
          waits=[(oe, P_.cur(oe)), (sv, P_.cur(sv)), (ss, P_.cur(ss))],
          inc="done" + tag)
    return "done" + tag


def build_l0b_launch():
    pr = P("l0b")
    nc = pr.nc
    x_d = nc.dram_tensor("x_in", [TPC, D], F32, kind="ExternalInput").ap()
    consts_d = nc.dram_tensor("consts_in", [128, NCONST], F32, kind="ExternalInput").ap()
    ident_d = nc.dram_tensor("ident_in", [128, 128], F32, kind="ExternalInput").ap()
    io = {}
    for nm, shp in [("qw2d", [1024, 1024]), ("kvw2d", [1024, 512]),
                    ("wproj2d", [1024, 1024]), ("bproj_row", [1, 1024]),
                    ("hmask", [128, 2]), ("rowA", [1, 128]), ("rowB", [1, 128]),
                    ("rowOnes", [1, 128]), ("gjcols", [128, 64]),
                    ("iirow", [1, 512]), ("qpos", [1, 4])]:
        io[nm] = nc.dram_tensor(nm, shp, F32, kind="ExternalInput").ap()
    io["alibiT"] = nc.dram_tensor("alibiT", [16, 128, 2944], F32, kind="ExternalInput").ap()
    io["kag_i"] = nc.dram_tensor("kag_i", [257, 2048], BF16, kind="Internal")
    io["vag_i"] = nc.dram_tensor("vag_i", [512, 1040], BF16, kind="Internal")
    io["kag_o"] = nc.dram_tensor("kag_o", [1028, 2048], BF16, kind="Internal")
    io["vag_o"] = nc.dram_tensor("vag_o", [2048, 1040], BF16, kind="Internal")
    xo_d = nc.dram_tensor("x_out", [TPC, D], F32, kind="ExternalOutput").ap()
    pr.alloc_pbanks(8)
    x_sb, consts, ident = emit_load_basics(pr, x_d, consts_d, ident_d)
    dsem = emit_l0b(pr, "W", x_sb, consts, ident, io)
    emit_store_x(pr, x_sb, xo_d, waits=[(dsem, 1)])
    return pr.finish(), pr.n_ops


def l0b_host_inputs(inp, core):
    j = core % 4
    wq = np.asarray(inp["L0_attn_b_Wq"], np.float32)
    wkv = np.asarray(inp["L0_attn_b_Wkv"], np.float32)
    wproj = np.asarray(inp["L0_attn_b_Wproj"], np.float32)
    bproj = np.asarray(inp["L0_attn_b_bproj"], np.float32)
    hm = np.zeros((128, 2), np.float32)
    hm[0:64, 0] = 1.0
    hm[64:128, 1] = 1.0
    rowA = np.zeros((1, 128), np.float32); rowA[0, 0:64] = 1.0
    rowB = np.zeros((1, 128), np.float32); rowB[0, 64:128] = 1.0
    rowOnes = np.ones((1, 128), np.float32)
    jj = np.arange(128, dtype=np.float32)[:, None]
    ktv = np.arange(64, dtype=np.float32)[None, :]
    gj = 128.0 * ktv + jj
    iirow = np.arange(512, dtype=np.float32)[None, :]
    qpos = np.array([[512.0 * (j + 4 * k) for k in range(4)]], np.float32)
    SL = slopes16()
    dlt = np.arange(-2047, 2560, dtype=np.float64)
    aT = np.zeros((16, 128, 2944), np.float32)
    jjl = np.arange(128)[:, None]
    cc = np.arange(2944)[None, :]
    darg = jjl - cc - 512 * j + 2432
    for h in range(16):
        g = np.where(darg <= 0, np.exp(SL[h] * darg / 8.0), 1.0)
        aT[h] = g.astype(np.float32)
    return {
        "consts_in": host_consts(),
        "ident_in": np.eye(128, dtype=np.float32),
        "qw2d": np.ascontiguousarray(wq.T),
        "kvw2d": np.ascontiguousarray(wkv.T),
        "wproj2d": np.ascontiguousarray(wproj.T),
        "bproj_row": bproj.reshape(1, 1024),
        "hmask": hm, "rowA": rowA, "rowB": rowB, "rowOnes": rowOnes,
        "gjcols": gj.astype(np.float32), "iirow": iirow, "qpos": qpos,
        "alibiT": aT,
    }




def emit_l1a(P_, tag0, x_sb, consts, ident, io, first_waits=()):
    nc = P_.nc
    xn_fm = P_.sbuf("xn_fm", [128, NDC * TPC], BF16)
    attn = P_.sbuf("attn", [128, 8 * TPC], BF16)
    w1buf = P_.sbuf("w1buf", [128, 2 * 8 * 512], BF16)
    w2buf = P_.sbuf("w2buf", [128, 2 * 4 * 1024], BF16)
    agc = P_.sbuf("agc", [128, 4160], F32)
    kfm = P_.sbuf("kfm", [128, 2048], F32)
    vfm = P_.sbuf("vfm", [128, 2048], F32)
    numb = P_.sbuf("numb", [128, 8 * 128 + 16], F32)
    ctxb = P_.sbuf("ctxb", [128, 8 * 128], BF16)
    etile = P_.sbuf("etile", [128, 2 * 128], BF16)
    vtile = P_.sbuf("vtile", [128, 2 * 128], BF16)
    lnwb = P_.sbuf("lnwb", [128, 192], F32)
    borow = P_.sbuf("borow", [1, 1024], BF16)
    onesbf = P_.sbuf("onesbf", [1, 128], BF16)
    cols2 = P_.sbuf("cols2", [128, 24], F32)
    qs = P_.sbuf("xn2", [128, 1024], F32)

    ag_i, ag_o = io["l1a_ag_i"], io["l1a_ag_o"]
    groups = [[0, 1, 2, 3], [4, 5, 6, 7]]

    ld = "al" + tag0
    P_.op("sp", lambda e: e.dma_start(out=lnwb[:], in_=io["lnwb"]),
          waits=list(first_waits), inc=ld, inc_by=16)
    P_.op("v", lambda e: e.memset(onesbf[:], 1.0), waits=list(first_waits),
          inc="on" + tag0)
    ld_done = P_.cur(ld)
    ones_done = P_.cur("on" + tag0)

    wq_re = io["l1a_wq"].rearrange("i (dc p) n -> i dc p n", p=128)
    wk_re = io["l1a_wk"].rearrange("i (dc p) n -> i dc p n", p=128)
    wv_re = io["l1a_wv"].rearrange("i (dc p) n -> i dc p n", p=128)
    wo_re = io["l1a_wo"].rearrange("i (dc p) n -> i dc p n", p=128)
    w1_re = io["l1a_w1"].rearrange("i (dc p) f -> i p dc f", p=128)
    w2_re = io["l1a_w2"].rearrange("i (fc p) d -> i p fc d", p=128)

    st_ = "ttr" + tag0

    xsv = "xsv" + tag0
    P_.op("sp", lambda e: e.dma_start(out=io["l1a_xsave"].ap()[:], in_=x_sb[:]),
          waits=[("ldx", LDX_DONE)] + list(first_waits), inc=xsv, inc_by=16)
    xsv_done = P_.cur(xsv)
    cols = P_.sbuf("cols", [128, 16], F32)
    pv, ps = "plv" + tag0, "pls" + tag0
    for tt in range(NTT):
        x_t = x_sb[:, tt * 1024:(tt + 1) * 1024]
        w0 = [("ldx", LDX_DONE)] + (list(first_waits) if tt == 0 else [])
        P_.op("v", lambda e, x_t=x_t: e.reduce_sum(cols[:, 0:1], x_t, AX.X),
              waits=w0 + [(pv, P_.cur(pv))], inc=pv)
        P_.op("s", lambda e, x_t=x_t: e.activation(qs[:], x_t, AF.Square,
                                                   accum_out=cols[:, 2:3]),
              waits=w0 + [(ps, P_.cur(ps))], inc=ps)
        P_.op("v", lambda e: e.tensor_scalar(cols[:, 1:2], cols[:, 0:1],
                                             1.0 / 1024, None, ALU.mult),
              waits=[(pv, P_.cur(pv))], inc=pv)
        P_.op("v", lambda e: e.tensor_mul(cols[:, 3:4], cols[:, 1:2], cols[:, 1:2]),
              waits=[(pv, P_.cur(pv))], inc=pv)
        P_.op("v", lambda e: e.tensor_scalar(cols[:, 4:5], cols[:, 2:3],
                                             1.0 / 1024, None, ALU.mult),
              waits=[(ps, P_.cur(ps)), (pv, P_.cur(pv))], inc=pv)
        P_.op("v", lambda e: e.tensor_sub(cols[:, 4:5], cols[:, 4:5], cols[:, 3:4]),
              waits=[(pv, P_.cur(pv))], inc=pv)
        P_.op("s", lambda e: e.activation(cols[:, 4:5], cols[:, 4:5], AF.Ln,
                                          bias=consts[:, C_EPS:C_EPS + 1]),
              waits=[(pv, P_.cur(pv))], inc=ps)
        P_.op("s", lambda e: e.activation(cols[:, 5:6], cols[:, 4:5], AF.Exp,
                                          scale=-0.5), waits=[(ps, P_.cur(ps))], inc=ps)
        aw = [(pv, P_.cur(pv)), (ps, P_.cur(ps))]
        if tt == 0:
            aw.append((xsv, xsv_done))
        P_.op("v", lambda e, x_t=x_t: e.tensor_scalar(
            x_t, x_t, cols[:, 1:2], cols[:, 5:6], ALU.subtract, ALU.mult),
            waits=aw, inc=pv)
    prev_done = [(pv, P_.cur(pv))]

    st_kv5 = [0]
    ctx5 = [0]
    dn_b = P_.curb("done" + tag0, "v")
    sv, ss = "av" + tag0, "as" + tag0
    lntag = tag0 + "N"
    for li in range(6):
        tag = tag0 + str(li)
        st0 = P_.curb(st_, "t")
        g1 = lambda dc, li=li: lnwb[:, li * 32 + 0 * 8 + dc: li * 32 + 0 * 8 + dc + 1]
        b1 = lambda dc, li=li: lnwb[:, li * 32 + 1 * 8 + dc: li * 32 + 1 * 8 + dc + 1]
        g2 = lambda dc, li=li: lnwb[:, li * 32 + 2 * 8 + dc: li * 32 + 2 * 8 + dc + 1]
        b2 = lambda dc, li=li: lnwb[:, li * 32 + 3 * 8 + dc: li * 32 + 3 * 8 + dc + 1]

        sxn, sxn_tgt = emit_ln_tr(P_, lntag, x_sb, xn_fm, consts, ident,
                                  first_waits=prev_done + [
                                      (ld, ld_done), (sv, P_.cur(sv)),
                                      (ss, P_.cur(ss)),
                                      ("hh" + tag0, P_.cur("hh" + tag0)),
                                      ("yy" + tag0, P_.cur("yy" + tag0)),
                                      ("oe" + tag0, P_.cur("oe" + tag0))],
                                  gamma_col=g1, beta_col=b1, banks=(0, 1))

        kv_done_st = []
        for c in range(8):
            slot = c % 2
            wsem = "kw" + tag0 + str(slot)
            w = ([(st_, kv_done_st[c - 2])] if c >= 2
                 else [(st_, st0)] + (list(first_waits) if li == 0 else []))
            for dc in range(NDC):
                P_.op("g", lambda e, dc=dc, c=c, slot=slot, li=li: e.dma_start(
                    out=w1buf[:, slot * 4096 + dc * 256: slot * 4096 + dc * 256 + 128],
                    in_=wk_re[li, dc, :, c * 128:(c + 1) * 128]),
                    waits=(w if dc == 0 else ()), inc=wsem, inc_by=16)
                P_.op("g", lambda e, dc=dc, c=c, slot=slot, li=li: e.dma_start(
                    out=w1buf[:, slot * 4096 + dc * 256 + 128: slot * 4096 + dc * 256 + 256],
                    in_=wv_re[li, dc, :, c * 128:(c + 1) * 128]),
                    inc=wsem, inc_by=16)
            kv_rdy = (wsem, P_.cur(wsem))
            kev = []
            for tb in range(4):
                bank = P_.pbank[2 + tb % 2]
                for dc in range(NDC):
                    mw = []
                    if dc == 0:
                        mw = [kv_rdy, (sxn, sxn_tgt)]
                        if tb >= 2:
                            mw.append((sv, kev[tb - 2]))
                        elif c > 0 or li > 0:
                            mw.append((sv, P_.cur(sv)))
                    P_.op("t", lambda e, bank=bank, slot=slot, dc=dc, tb=tb: e.matmul(
                        bank[:], w1buf[:, slot * 4096 + dc * 256: slot * 4096 + dc * 256 + 128],
                        xn_fm[:, dc * TPC + tb * 512: dc * TPC + (tb + 1) * 512],
                        start=(dc == 0), stop=(dc == NDC - 1)), waits=mw, inc=st_)
                P_.op("v", lambda e, bank=bank, tb=tb: e.tensor_copy(
                    kfm[:, tb * 512:(tb + 1) * 512], bank[:]),
                    waits=[(st_, P_.cur(st_))], inc=sv)
                kev.append(P_.cur(sv))
            P_.op("v", lambda e: e.tensor_reduce(cols2[:, 0:1], kfm[:], AX.X, ALU.max),
                  waits=[(sv, P_.cur(sv))], inc=sv)
            P_.op("v", lambda e: e.tensor_scalar(cols2[:, 1:2], cols2[:, 0:1], -1.0,
                                                 None, ALU.mult),
                  waits=[(sv, P_.cur(sv))], inc=sv)
            P_.op("s", lambda e, c=c: e.activation(
                kfm[:], kfm[:], AF.Exp, bias=cols2[:, 1:2],
                accum_out=numb[:, 1032 + c: 1033 + c]),
                waits=[(sv, P_.cur(sv)), (ss, P_.cur(ss))], inc=ss)
            P_.op("v", lambda e, c=c: e.tensor_copy(numb[:, 1024 + c:1025 + c],
                                                    cols2[:, 0:1]),
                  waits=[(sv, P_.cur(sv))], inc=sv)
            vev = []
            for tb in range(4):
                bank = P_.pbank[2 + tb % 2]
                for dc in range(NDC):
                    mw = []
                    if dc == 0:
                        mw = [kv_rdy, (sv, P_.cur(sv)), (ss, P_.cur(ss))]
                        if tb >= 2:
                            mw.append((sv, vev[tb - 2]))
                    P_.op("t", lambda e, bank=bank, slot=slot, dc=dc, tb=tb: e.matmul(
                        bank[:], w1buf[:, slot * 4096 + dc * 256 + 128:
                                       slot * 4096 + dc * 256 + 256],
                        xn_fm[:, dc * TPC + tb * 512: dc * TPC + (tb + 1) * 512],
                        start=(dc == 0), stop=(dc == NDC - 1)), waits=mw, inc=st_)
                P_.op("v", lambda e, bank=bank, tb=tb: e.tensor_copy(
                    vfm[:, tb * 512:(tb + 1) * 512], bank[:]),
                    waits=[(st_, P_.cur(st_))], inc=sv)
                vev.append(P_.cur(sv))
            for tt in range(NTT):
                bke = P_.pbank[tt % 2]
                trw = [(sv, P_.cur(sv)), (ss, P_.cur(ss))]
                P_.op("t", lambda e, bke=bke, tt=tt: e.transpose(
                    bke[:, 0:128], kfm[:, tt * 128:(tt + 1) * 128], ident[:]),
                    waits=trw, inc=st_)
                P_.op("v", lambda e, bke=bke, tt=tt: e.tensor_copy(
                    etile[:, (tt % 2) * 128:(tt % 2) * 128 + 128], bke[:, 0:128]),
                    waits=[(st_, P_.cur(st_))], inc=sv)
                P_.op("t", lambda e, bke=bke, tt=tt: e.transpose(
                    bke[:, 128:256], vfm[:, tt * 128:(tt + 1) * 128], ident[:]),
                    inc=st_)
                P_.op("v", lambda e, bke=bke, tt=tt: e.tensor_copy(
                    vtile[:, (tt % 2) * 128:(tt % 2) * 128 + 128], bke[:, 128:256]),
                    waits=[(st_, P_.cur(st_))], inc=sv)
                P_.op("t", lambda e, tt=tt: e.matmul(
                    P_.pbank[5][:, 0:128],
                    etile[:, (tt % 2) * 128:(tt % 2) * 128 + 128],
                    vtile[:, (tt % 2) * 128:(tt % 2) * 128 + 128],
                    start=(tt == 0), stop=(tt == NTT - 1)),
                    waits=[(sv, P_.cur(sv))] +
                          ([("nv" + tag0, P_.cur("nv" + tag0))] if tt == 0 else []),
                    inc=st_)
            P_.op("v", lambda e, c=c: e.tensor_copy(numb[:, c * 128:(c + 1) * 128],
                                                    P_.pbank[5][:, 0:128]),
                  waits=[(st_, P_.cur(st_))], inc="nv" + tag0)
            kv_done_st.append(P_.cur(st_))
        if li == 5:
            st_kv5[0] = P_.cur(st_)
        num_done = P_.cur("nv" + tag0)

        spg = "ks" + tag0 + "0"
        P_.op("sp", lambda e: e.dma_start(out=ag_i.ap()[:, 0:1040],
                                          in_=numb[:, 0:1040]),
              waits=[("nv" + tag0, num_done), (sv, P_.cur(sv)), (ss, P_.cur(ss))],
              inc=spg, inc_by=16)
        cc = "cc" + tag0
        P_.op("g", lambda e: e.collective_compute(
            "AllGather", ALU.bypass, replica_groups=groups,
            ins=[ag_i.ap().opt()], outs=[ag_o.ap().opt()]),
            waits=[(spg, P_.cur(spg))], inc=cc)
        cc_done = P_.cur(cc)

        for nb in range(2):
            wsem = "vw" + tag0 + str(nb)
            for dc in range(NDC):
                P_.op("g", lambda e, dc=dc, nb=nb, li=li: e.dma_start(
                    out=w2buf[:, nb * 4096 + dc * 512: nb * 4096 + (dc + 1) * 512],
                    in_=wq_re[li, dc, :, nb * 512:(nb + 1) * 512]),
                    waits=([(st_, st0)] if dc == 0 else ()), inc=wsem, inc_by=16)
        qw_rdy = [("vw" + tag0 + "0", P_.cur("vw" + tag0 + "0")),
                  ("vw" + tag0 + "1", P_.cur("vw" + tag0 + "1"))]
        qs3 = qs[:].rearrange("p (h d) -> p h d", h=8)
        mx8b = cols2[:, 8:16].unsqueeze(-1).broadcast_to([128, 8, 128])
        sm8b = cols2[:, 16:24].unsqueeze(-1).broadcast_to([128, 8, 128])
        for tt in range(NTT):
            for nb in range(2):
                bank = P_.pbank[2 + nb]
                for dc in range(NDC):
                    mw = []
                    if dc == 0:
                        mw = list(qw_rdy)
                        mw.append((sv, P_.cur(sv)))
                    P_.op("t", lambda e, bank=bank, nb=nb, dc=dc, tt=tt: e.matmul(
                        bank[:], xn_fm[:, dc * TPC + tt * 128: dc * TPC + (tt + 1) * 128],
                        w2buf[:, nb * 4096 + dc * 512: nb * 4096 + (dc + 1) * 512],
                        start=(dc == 0), stop=(dc == NDC - 1)), waits=mw, inc=st_)
                P_.op("v", lambda e, bank=bank, nb=nb: e.tensor_copy(
                    qs[:, nb * 512:(nb + 1) * 512], bank[:]),
                    waits=[(st_, P_.cur(st_))], inc=sv)
            P_.op("v", lambda e: e.tensor_reduce(cols2[:, 8:16], qs3, AX.X,
                                                 ALU.max),
                  waits=[(sv, P_.cur(sv))], inc=sv)
            P_.op("v", lambda e: e.tensor_tensor(qs3, qs3, mx8b, ALU.subtract),
                  waits=[(sv, P_.cur(sv))], inc=sv)
            P_.op("s", lambda e: e.activation(qs[:], qs[:], AF.Exp),
                  waits=[(sv, P_.cur(sv)), (ss, P_.cur(ss))], inc=ss)
            P_.op("v", lambda e: e.tensor_reduce(cols2[:, 16:24], qs3, AX.X,
                                                 ALU.add),
                  waits=[(sv, P_.cur(sv)), (ss, P_.cur(ss))], inc=sv)
            P_.op("v", lambda e: e.reciprocal(cols2[:, 16:24], cols2[:, 16:24]),
                  waits=[(sv, P_.cur(sv))], inc=sv)
            P_.op("v", lambda e: e.tensor_tensor(qs3, qs3, sm8b, ALU.mult),
                  waits=[(sv, P_.cur(sv))], inc=sv)
            qn_ready = P_.cur(sv)
            for c in range(8):
                bank = P_.pbank[2 + c % 2]
                trw = [(sv, qn_ready)]
                if c >= 2 or tt > 0:
                    trw.append((sv, P_.cur(sv)))
                P_.op("t", lambda e, bank=bank, c=c, tt=tt: e.transpose(
                    bank[:, 0:128], qs[:, c * 128:(c + 1) * 128], ident[:]),
                    waits=trw, inc=st_)
                P_.op("v", lambda e, bank=bank, c=c, tt=tt: e.tensor_copy(
                    attn[:, c * TPC + tt * 128: c * TPC + (tt + 1) * 128],
                    bank[:, 0:128]), waits=[(st_, P_.cur(st_))], inc=sv)
        qn_done = P_.cur(sv)

        kx = "kx" + tag0
        for r in range(4):
            kw = []
            if r == 0:
                kw = [(cc, cc_done)]
                if li > 0:
                    kw.append(("done" + tag0, dn_b + li))
            P_.op("sp", lambda e, r=r: e.dma_start(
                out=agc[:, r * 1040:(r + 1) * 1040],
                in_=ag_o.ap()[r * 128:(r + 1) * 128, 0:1040]),
                waits=kw, inc=kx, inc_by=16)
        kx_done = P_.cur(kx)
        for c in range(8):
            P_.op("v", lambda e, c=c: e.tensor_tensor(
                cols2[:, 2:3], agc[:, 1024 + c:1025 + c],
                agc[:, 1040 + 1024 + c:1040 + 1025 + c], ALU.max),
                waits=[(kx, kx_done), (sv, P_.cur(sv))], inc=sv)
            for r in range(2, 4):
                P_.op("v", lambda e, r=r, c=c: e.tensor_tensor(
                    cols2[:, 2:3], cols2[:, 2:3],
                    agc[:, r * 1040 + 1024 + c: r * 1040 + 1025 + c], ALU.max),
                    waits=[(sv, P_.cur(sv))], inc=sv)
            P_.op("v", lambda e: e.tensor_scalar(cols2[:, 3:4], cols2[:, 2:3], -1.0,
                                                 None, ALU.mult),
                  waits=[(sv, P_.cur(sv))], inc=sv)
            for r in range(4):
                P_.op("s", lambda e, r=r, c=c: e.activation(
                    cols2[:, 4 + r:5 + r], agc[:, r * 1040 + 1024 + c: r * 1040 + 1025 + c],
                    AF.Exp, bias=cols2[:, 3:4]),
                    waits=[(sv, P_.cur(sv)), (ss, P_.cur(ss))], inc=ss)
            P_.op("v", lambda e, c=c: e.tensor_scalar(
                cols2[:, 20:21], agc[:, 1032 + c:1033 + c], cols2[:, 4:5], None,
                ALU.mult), waits=[(ss, P_.cur(ss))], inc=sv)
            for r in range(1, 4):
                P_.op("v", lambda e, r=r, c=c: e.scalar_tensor_tensor(
                    cols2[:, 20:21], agc[:, r * 1040 + 1032 + c: r * 1040 + 1033 + c],
                    cols2[:, 4 + r:5 + r], cols2[:, 20:21], ALU.mult, ALU.add),
                    waits=[(sv, P_.cur(sv))], inc=sv)
            P_.op("v", lambda e: e.reciprocal(cols2[:, 21:22], cols2[:, 20:21]),
                  waits=[(sv, P_.cur(sv))], inc=sv)
            P_.op("v", lambda e, c=c: e.tensor_scalar(
                qs[:, 0:128], agc[:, c * 128:(c + 1) * 128], cols2[:, 4:5], None,
                ALU.mult), waits=[(sv, P_.cur(sv))], inc=sv)
            for r in range(1, 4):
                P_.op("v", lambda e, r=r, c=c: e.scalar_tensor_tensor(
                    qs[:, 0:128], agc[:, r * 1040 + c * 128: r * 1040 + (c + 1) * 128],
                    cols2[:, 4 + r:5 + r], qs[:, 0:128], ALU.mult, ALU.add),
                    waits=[(sv, P_.cur(sv))], inc=sv)
            P_.op("v", lambda e, c=c: e.tensor_scalar(
                ctxb[:, c * 128:(c + 1) * 128], qs[:, 0:128], cols2[:, 21:22],
                None, ALU.mult), waits=[(sv, P_.cur(sv))], inc=sv)
        ctx_done = P_.cur(sv)
        if li == 5:
            ctx5[0] = ctx_done
        for c in range(8):
            for tb in range(4):
                bank = P_.pbank[2 + tb % 2]
                mw = [(sv, ctx_done)]
                if c * 4 + tb >= 2:
                    mw.append((sv, P_.cur(sv)))
                P_.op("t", lambda e, bank=bank, c=c, tb=tb: e.matmul(
                    bank[:], ctxb[:, c * 128:(c + 1) * 128],
                    attn[:, c * TPC + tb * 512: c * TPC + (tb + 1) * 512],
                    start=True, stop=True), waits=mw, inc=st_)
                P_.op("v", lambda e, bank=bank, c=c, tb=tb: e.tensor_copy(
                    xn_fm[:, c * TPC + tb * 512: c * TPC + (tb + 1) * 512], bank[:]),
                    waits=[(st_, P_.cur(st_))], inc=sv)
        attnfm_done = P_.cur(sv)

        for nb in range(2):
            wsem = "vw" + tag0 + str(nb)
            for dc in range(NDC):
                P_.op("g", lambda e, dc=dc, nb=nb, li=li: e.dma_start(
                    out=w2buf[:, nb * 4096 + dc * 512: nb * 4096 + (dc + 1) * 512],
                    in_=wo_re[li, dc, :, nb * 512:(nb + 1) * 512]),
                    waits=([(st_, P_.cur(st_))] if dc == 0 else ()), inc=wsem, inc_by=16)
        ow_rdy = [("vw" + tag0 + "0", P_.cur("vw" + tag0 + "0")),
                  ("vw" + tag0 + "1", P_.cur("vw" + tag0 + "1"))]
        P_.op("g", lambda e, li=li: e.dma_start(out=borow[:], in_=io["l1a_bo"][li]),
              waits=[(st_, st0)], inc="bo" + tag0, inc_by=16)
        bo_rdy = P_.cur("bo" + tag0)
        oe = "oe" + tag0
        oe_b = P_.curb(oe, "v")
        for tt in range(NTT):
            for nb in range(2):
                gi = tt * 2 + nb
                bank = P_.pbank[6 + gi % 2]
                for dc in range(NDC):
                    mw = []
                    if dc == 0:
                        mw = ow_rdy + [(sv, attnfm_done), ("bo" + tag0, bo_rdy),
                                       ("on" + tag0, ones_done)]
                        if gi >= 2:
                            mw.append((oe, oe_b + gi - 1))
                    P_.op("t", lambda e, bank=bank, nb=nb, dc=dc, tt=tt: e.matmul(
                        bank[:], xn_fm[:, dc * TPC + tt * 128: dc * TPC + (tt + 1) * 128],
                        w2buf[:, nb * 4096 + dc * 512: nb * 4096 + (dc + 1) * 512],
                        start=(dc == 0), stop=False), waits=mw, inc=st_)
                P_.op("t", lambda e, bank=bank, nb=nb: e.matmul(
                    bank[:], onesbf[:], borow[0:1, nb * 512:(nb + 1) * 512],
                    start=False, stop=True), inc=st_)
                xsl = x_sb[:, tt * 1024 + nb * 512: tt * 1024 + nb * 512 + 512]
                P_.op("v", lambda e, xsl=xsl, bank=bank: e.tensor_add(
                    xsl, xsl, bank[:]), waits=[(st_, P_.cur(st_))], inc=oe)
        oproj_done = P_.cur(oe)

        sxn2, sxn2_tgt = emit_ln_tr(P_, lntag, x_sb, xn_fm, consts, ident,
                                    first_waits=[
                                        (oe, oproj_done), (sv, P_.cur(sv)),
                                        (ss, P_.cur(ss)),
                                        ("hh" + tag0, P_.cur("hh" + tag0)),
                                        ("yy" + tag0, P_.cur("yy" + tag0))],
                                    gamma_col=g2, beta_col=b2, banks=(0, 1))
        nw1, nw2 = [0], [0]
        w1_done_st, w2_done_st = [], []
        sh, sy = "hh" + tag0, "yy" + tag0
        sh_b = P_.curb(sh, "v")
        for tb in range(4):
            for fb in range(8):
                k = nw1[0]; nw1[0] += 1
                slot = k % 2
                wsem = "kw" + tag0 + str(slot)
                w = [(st_, w1_done_st[k - 2])] if k >= 2 else [(st_, P_.cur(st_))]
                for dc in range(NDC):
                    P_.op("g", lambda e, dc=dc, fb=fb, slot=slot, li=li: e.dma_start(
                        out=w1buf[:, slot * 4096 + dc * 512: slot * 4096 + (dc + 1) * 512],
                        in_=w1_re[li, :, dc, fb * 512:(fb + 1) * 512]),
                        waits=(w if dc == 0 else ()), inc=wsem, inc_by=16)
                w1_rdy = (wsem, P_.cur(wsem))
                for fq in range(4):
                    fc = fb * 4 + fq
                    bank = P_.pbank[fc % 2]
                    for dc in range(NDC):
                        mw = []
                        if dc == 0:
                            mw = [w1_rdy, (sxn2, sxn2_tgt)]
                            gc = P_.cur(sh)
                            if gc - sh_b >= 2:
                                mw.append((sh, gc - 1))
                            if fc == 0 and fb == 0:
                                mw.append((sy, P_.cur(sy)))
                                mw.append((sv, P_.cur(sv)))
                        P_.op("t", lambda e, bank=bank, slot=slot, dc=dc, fq=fq, tb=tb: e.matmul(
                            bank[:], w1buf[:, slot * 4096 + dc * 512 + fq * 128:
                                           slot * 4096 + dc * 512 + fq * 128 + 128],
                            xn_fm[:, dc * TPC + tb * 512: dc * TPC + (tb + 1) * 512],
                            start=(dc == 0), stop=(dc == NDC - 1)), waits=mw, inc=st_)
                    mmc = P_.cur(st_)
                    hsl = attn[:, fc * 512:(fc + 1) * 512]
                    P_.op("v", lambda e, bank=bank: e.tensor_scalar(
                        qs[:, 0:512], bank[:], 0.0, 0.01, ALU.min, ALU.mult),
                        waits=[(st_, mmc), (sv, P_.cur(sv)), (sh, P_.cur(sh))],
                        inc=sv)
                    P_.op("v", lambda e, bank=bank, hsl=hsl: e.scalar_tensor_tensor(
                        hsl, bank[:], 0.0, qs[:, 0:512], ALU.max, ALU.add),
                        waits=[(sv, P_.cur(sv))], inc=sh)
                w1_done_st.append(P_.cur(st_))
            h_done = P_.cur(sh)
            for fg in range(8):
                k = nw2[0]; nw2[0] += 1
                slot = k % 2
                wsem = "vw" + tag0 + str(slot)
                w = [(st_, w2_done_st[k - 2])] if k >= 2 else [(st_, P_.cur(st_))]
                for fq in range(4):
                    P_.op("g", lambda e, fq=fq, fg=fg, slot=slot, li=li: e.dma_start(
                        out=w2buf[:, slot * 4096 + fq * 1024: slot * 4096 + (fq + 1) * 1024],
                        in_=w2_re[li, :, fg * 4 + fq, :]),
                        waits=(w if fq == 0 else ()), inc=wsem, inc_by=16)
                w2_rdy = (wsem, P_.cur(wsem))
                for fq in range(4):
                    fc = fg * 4 + fq
                    for tt4 in range(4):
                        for do in range(2):
                            bank = P_.pbank[tt4 * 2 + do]
                            mw = []
                            if fq == 0 and tt4 == 0 and do == 0:
                                mw = [w2_rdy]
                                if fg == 0:
                                    mw += [(sh, h_done), (sy, P_.cur(sy))]
                            P_.op("t", lambda e, bank=bank, slot=slot, fq=fq, fc=fc,
                                  tt4=tt4, do=do: e.matmul(
                                      bank[:],
                                      attn[:, fc * 512 + tt4 * 128: fc * 512 + tt4 * 128 + 128],
                                      w2buf[:, slot * 4096 + fq * 1024 + do * 512:
                                            slot * 4096 + fq * 1024 + do * 512 + 512],
                                      start=(fc == 0), stop=(fc == 31)),
                                  waits=mw, inc=st_)
                w2_done_st.append(P_.cur(st_))
            w2mm = P_.cur(st_)
            for tt4 in range(4):
                tt = tb * 4 + tt4
                for do in range(2):
                    bank = P_.pbank[tt4 * 2 + do]
                    xsl = x_sb[:, tt * 1024 + do * 512: tt * 1024 + do * 512 + 512]
                    P_.op("v", lambda e, bank=bank, xsl=xsl: e.tensor_add(
                        xsl, xsl, bank[:]), waits=[(st_, w2mm)], inc=sy)
        P_.op("v", lambda e: e.engine_nop(),
              waits=[(sy, P_.cur(sy)), (ss, P_.cur(ss)), (sv, P_.cur(sv)),
                     (sh, P_.cur(sh)), (oe, P_.cur(oe)),
                     ("nv" + tag0, P_.cur("nv" + tag0))], inc="done" + tag0)
        prev_done = [("done" + tag0, li + 1)]

    fav = "fav" + tag0
    xs_ap = io["l1a_xsave"].ap()
    for ci in range(8):
        sl = "xr" + tag0 + str(ci % 2)
        buf = [kfm, vfm][ci % 2]
        w = ([(st_, st_kv5[0]), (sv, ctx5[0])] if ci < 2 else [(fav, ci - 1)])
        P_.op("sp", lambda e, ci=ci, buf=buf: e.dma_start(
            out=buf[:], in_=xs_ap[:, ci * 2048:(ci + 1) * 2048]),
            waits=w, inc=sl, inc_by=16)
        aw = [(sl, P_.cur(sl)), ("done" + tag0, dn_b + 6)]
        P_.op("v", lambda e, ci=ci, buf=buf: e.tensor_add(
            x_sb[:, ci * 2048:(ci + 1) * 2048],
            x_sb[:, ci * 2048:(ci + 1) * 2048], buf[:]),
            waits=aw, inc=fav)
    return (fav, P_.cur(fav))


def build_l1a_launch():
    pr = P("l1a")
    nc = pr.nc
    x_d = nc.dram_tensor("x_in", [TPC, D], F32, kind="ExternalInput").ap()
    consts_d = nc.dram_tensor("consts_in", [128, NCONST], F32, kind="ExternalInput").ap()
    ident_d = nc.dram_tensor("ident_in", [128, 128], F32, kind="ExternalInput").ap()
    io = {}
    io["lnwb"] = nc.dram_tensor("lnwb", [128, 192], F32, kind="ExternalInput").ap()
    for nm, shp in [("l1a_wq", [6, 1024, 1024]), ("l1a_wk", [6, 1024, 1024]),
                    ("l1a_wv", [6, 1024, 1024]), ("l1a_wo", [6, 1024, 1024]),
                    ("l1a_w1", [6, 1024, 4096]), ("l1a_w2", [6, 4096, 1024]),
                    ("l1a_bo", [6, 1, 1024])]:
        io[nm] = nc.dram_tensor(nm, shp, F32, kind="ExternalInput").ap()
    io["l1a_ag_i"] = nc.dram_tensor("l1a_ag_i", [128, 1040], F32, kind="Internal")
    io["l1a_ag_o"] = nc.dram_tensor("l1a_ag_o", [512, 1040], F32, kind="Internal")
    io["l1a_xsave"] = nc.dram_tensor("l1a_xsave", [128, NTT * 1024], F32,
                                     kind="Internal")
    xo_d = nc.dram_tensor("x_out", [TPC, D], F32, kind="ExternalOutput").ap()
    pr.alloc_pbanks(8)
    x_sb, consts, ident = emit_load_basics(pr, x_d, consts_d, ident_d)
    dsem = emit_l1a(pr, "L", x_sb, consts, ident, io)
    emit_store_x(pr, x_sb, xo_d, waits=[dsem])
    return pr.finish(), pr.n_ops


def l1a_host_inputs(inp, core):
    f32 = np.float32
    lnwb = np.zeros((128, 192), f32)
    for li in range(6):
        for kind, nm in enumerate(["ln1_w", "ln1_b", "ln2_w", "ln2_b"]):
            v = np.asarray(inp[f"L1_attn_a_{nm}"], f32)[li]
            for dc in range(8):
                lnwb[:, li * 32 + kind * 8 + dc] = v[dc * 128:(dc + 1) * 128]
    return {
        "consts_in": host_consts(),
        "ident_in": np.eye(128, dtype=f32),
        "lnwb": lnwb,
        "l1a_wq": np.asarray(inp["L1_attn_a_wq"], f32),
        "l1a_wk": np.asarray(inp["L1_attn_a_wk"], f32),
        "l1a_wv": np.asarray(inp["L1_attn_a_wv"], f32),
        "l1a_wo": np.asarray(inp["L1_attn_a_wo"], f32),
        "l1a_w1": np.asarray(inp["L1_attn_a_w1"], f32),
        "l1a_w2": np.asarray(inp["L1_attn_a_w2"], f32),
        "l1a_bo": np.asarray(inp["L1_attn_a_bo"], f32).reshape(6, 1, 1024),
    }



RT32 = 0.17677669529663687


def emit_l1b(P_, tag, x_sb, consts, ident, io, first_waits=()):
    nc = P_.nc
    xn_fm = P_.sbuf("xn_fm", [128, NDC * TPC], BF16)
    attn = P_.sbuf("attn", [128, 8 * TPC], BF16)
    arena1 = P_.sbuf("arena1", [128, 8192], BF16)
    arena2 = P_.sbuf("arena2", [128, 8192], BF16)
    arena3 = P_.sbuf("arena3", [128, 8448], BF16)
    qw_sb = None
    q_hp = P_.sbuf("q_hp", [128, 2048], BF16)
    pt_sb = P_.sbuf("pt_sb", [128, 4 * 512], BF16)
    qw_sb = pt_sb[:, 0:1024]
    sq = P_.sbuf("sq", [128, 512], F32)
    lat_sb = P_.sbuf("lat_sb", [64, 2 * 2048], BF16)
    hmask_sb = P_.sbuf("hmask_sb", [128, 2], F32)
    negmc = P_.sbuf("negmc", [128, 2], F32)
    rows_sb = P_.sbuf("rows_sb", [1, 3 * 128], F32)

    onesbf = P_.sbuf("onesbf", [1, 128], BF16)
    mxrow = P_.sbuf("mxrow", [1, 256], F32)
    kg_sb = mxrow[0:1, 192:256]
    obvb_sb = None
    den_sb = P_.sbuf("den_sb", [1, 1024], F32)

    blk_sb = P_.sbuf("blk_sb", [128, 256], BF16)
    l2o_sb = P_.sbuf("l2o_sb", [64, 128], BF16)
    bias_sb = P_.sbuf("bias_sb", [128, 32], F32)

    kag_i, vag_i, kag_o, vag_o = io["kag_i"], io["vag_i"], io["kag_o"], io["vag_o"]
    groups = [[0, 1, 2, 3], [4, 5, 6, 7]]

    ld, ldg = "ll" + tag, "lg" + tag
    sv, ss, st_ = "lv" + tag, "ls" + tag, "ttr" + tag
    sde, sxp, smk = "de" + tag, "xp" + tag, "mk" + tag
    P_.op("sp", lambda e: e.dma_start(out=hmask_sb[:], in_=io["hmask32"]),
          waits=list(first_waits), inc=ld, inc_by=16)
    P_.op("sp", lambda e: e.dma_start(out=rows_sb[0:1, 0:128], in_=io["row32A"]), inc=ld, inc_by=16)
    P_.op("sp", lambda e: e.dma_start(out=rows_sb[0:1, 128:256], in_=io["row32B"]), inc=ld, inc_by=16)
    P_.op("sp", lambda e: e.dma_start(out=rows_sb[0:1, 256:384], in_=io["rowOnes"]), inc=ld, inc_by=16)
    P_.op("sp", lambda e: e.dma_start(out=bias_sb[:], in_=io["bias_cols"]), inc=ld, inc_by=16)
    P_.op("g", lambda e: e.dma_start(out=blk_sb[:], in_=io["blkw"]),
          waits=list(first_waits), inc=ldg, inc_by=16)
    P_.op("g", lambda e: e.dma_start(out=l2o_sb[:], in_=io["l2oT2"]), inc=ldg, inc_by=16)
    P_.op("g", lambda e: e.dma_start(out=arena3[:, 4224:8320], in_=io["ropeT"]),
          inc="rp" + tag, inc_by=16)
    ld_done, ldg_done = P_.cur(ld), P_.cur(ldg)
    rope_done = P_.cur("rp" + tag)
    P_.op("v", lambda e: e.memset(onesbf[:], 1.0), waits=list(first_waits), inc=sv)
    P_.op("v", lambda e: e.memset(arena3[:, 0:2112], 1.0), inc=sv)
    ones_done = P_.cur(sv)

    sxn, sxn_tgt = emit_ln_tr(P_, tag, x_sb, xn_fm, consts, ident,
                              first_waits=first_waits, banks=(0, 1))

    caw_re = io["cattn2d"].rearrange("(dc p) n -> dc p n", p=128)
    cpw_re = io["cproj2d"].rearrange("(dc p) n -> dc p n", p=128)
    cos4 = arena3[:, 4224:6272]
    sin4 = arena3[:, 6272:8320]

    q2lT2 = blk_sb[:, 0:64]
    kv2lT2 = blk_sb[:, 64:128]
    RT2 = blk_sb[0:64, 128:192]
    wqkT2 = blk_sb[0:64, 192:256]

    import os as _os0
    _npr = 8
    kev_done = []
    norm_reads = []
    kw_done_st = []
    for pr_ in range(_npr):
        wsem = "kw" + tag + str(pr_ % 2)
        w = [(st_, kw_done_st[pr_ - 2])] if pr_ >= 2 else []
        slot = pr_ % 2
        for dc in range(NDC):
            P_.op("g", lambda e, dc=dc, pr_=pr_, slot=slot: e.dma_start(
                out=arena2[:, slot * 2048 + dc * 256: slot * 2048 + dc * 256 + 128],
                in_=caw_re[dc, :, 1024 + pr_ * 128: 1024 + (pr_ + 1) * 128]),
                waits=(w if dc == 0 else ()), inc=wsem, inc_by=16)
            P_.op("g", lambda e, dc=dc, pr_=pr_, slot=slot: e.dma_start(
                out=arena2[:, slot * 2048 + dc * 256 + 128: slot * 2048 + dc * 256 + 256],
                in_=caw_re[dc, :, 2048 + pr_ * 128: 2048 + (pr_ + 1) * 128]),
                inc=wsem, inc_by=16)
        kw_rdy = (wsem, P_.cur(wsem))
        for kv in range(2):
            for tb in range(4):
                it = pr_ * 8 + kv * 4 + tb
                bank = P_.pbank[2 + tb % 2]
                for dc in range(NDC):
                    mw = []
                    if dc == 0:
                        mw = [kw_rdy, (sxn, sxn_tgt), (ldg, ldg_done)]
                        if kev_done[-2:] and it >= 2:
                            mw.append((sv, kev_done[-2]))
                    P_.op("t", lambda e, bank=bank, slot=slot, dc=dc, kv=kv, tb=tb: e.matmul(
                        bank[:], arena2[:, slot * 2048 + dc * 256 + kv * 128:
                                        slot * 2048 + dc * 256 + kv * 128 + 128],
                        xn_fm[:, dc * TPC + tb * 512: dc * TPC + (tb + 1) * 512],
                        start=(dc == 0), stop=(dc == NDC - 1)), waits=mw, inc=st_)
                mmc = P_.cur(st_)
                bc = bias_sb[:, 8 + kv * 8 + pr_: 9 + kv * 8 + pr_]
                evw = [(st_, mmc), (ld, ld_done), (sv, P_.cur(sv))]
                if kv == 1 and tb == 0:
                    evw.append(kst_sent)
                P_.op("v", lambda e, bank=bank, tb=tb, bc=bc: e.tensor_scalar(
                    q_hp[:, tb * 512:(tb + 1) * 512], bank[:], bc, None, ALU.add),
                    waits=evw, inc=sv)
                kev_done.append(P_.cur(sv))
            if kv == 0:
                for tb in range(4):
                    bank = P_.pbank[6]
                    mw = [(sv, P_.cur(sv))]
                    if tb == 0:
                        mw.append((sde, P_.cur(sde)))
                    P_.op("t", lambda e, bank=bank, tb=tb: e.matmul(
                        bank[0:64, 0:512], kv2lT2,
                        q_hp[:, tb * 512:(tb + 1) * 512],
                        start=True, stop=True), waits=mw, inc=st_)
                    P_.op("v", lambda e, bank=bank, tb=tb: e.tensor_scalar(
                        lat_sb[0:64, tb * 512:(tb + 1) * 512], bank[0:64, 0:512],
                        bias_sb[0:64, 25:26], None, ALU.add),
                        waits=[(st_, P_.cur(st_))], inc=sv)
                lat_done = P_.cur(sv)
            if kv == 0:
                for tb in range(4):
                    bank = P_.pbank[7]
                    P_.op("t", lambda e, bank=bank, tb=tb: e.matmul(
                        bank[0:64, 0:512], RT2,
                        lat_sb[0:64, tb * 512:(tb + 1) * 512],
                        start=True, stop=True),
                        waits=[(sv, P_.cur(sv))] + ([(sde, P_.cur(sde))] if tb == 0 else []),
                        inc=st_)
                    rmm = P_.cur(st_)
                    t1 = lat_sb[0:64, 2048 + tb * 512: 2048 + (tb + 1) * 512]
                    P_.op("v", lambda e, t1=t1, tb=tb: e.tensor_mul(
                        t1, lat_sb[0:64, tb * 512:(tb + 1) * 512],
                        cos4[0:64, tb * 512:(tb + 1) * 512]),
                        waits=[(st_, rmm), ("rp" + tag, rope_done)], inc=sv)
                    t2 = pt_sb[0:64, 512:1024]
                    P_.op("v", lambda e, t2=t2, bank=bank, tb=tb: e.tensor_mul(
                        t2, bank[0:64, 0:512],
                        sin4[0:64, tb * 512:(tb + 1) * 512]),
                        waits=[(sv, P_.cur(sv))], inc=sv)
                    P_.op("v", lambda e, t1=t1, t2=t2: e.tensor_add(t1, t1, t2),
                          waits=[(sv, P_.cur(sv))], inc=sv)
                krope_done = P_.cur(sv)
                for tb in range(4):
                    P_.op("s", lambda e, tb=tb: e.activation(
                        sq[0:64, 0:512], lat_sb[0:64, 2048 + tb * 512: 2048 + (tb + 1) * 512],
                        AF.Square), waits=[(sv, krope_done), (st_, P_.cur(st_))] +
                        ([(ss, P_.cur(ss))] if tb > 0 else []), inc=ss)
                    nw = [(ss, P_.cur(ss)), (ld, ld_done), (sv, P_.cur(sv))]
                    P_.op("t", lambda e: e.matmul(P_.pbank[6][0:1, 0:512],
                                                  hmask_sb[0:64, 0:1],
                                                  sq[0:64, 0:512],
                                                  start=True, stop=True),
                          waits=nw, inc=st_)
                    P_.op("t", lambda e: e.matmul(P_.pbank[7][0:1, 0:512],
                                                  hmask_sb[0:64, 1:2],
                                                  sq[0:64, 0:512],
                                                  start=True, stop=True), inc=st_)
                    nc_ = P_.cur(st_)
                    c0 = 64 + (2 * pr_) * 4 + tb
                    c1 = 64 + (2 * pr_ + 1) * 4 + tb
                    P_.op("v", lambda e, c0=c0: e.tensor_reduce(
                        mxrow[0:1, c0:c0 + 1], P_.pbank[6][0:1, 0:512], AX.X,
                        ALU.max), waits=[(st_, nc_)], inc=sv)
                    P_.op("v", lambda e, c1=c1: e.tensor_reduce(
                        mxrow[0:1, c1:c1 + 1], P_.pbank[7][0:1, 0:512], AX.X,
                        ALU.max), inc=sv)
                    norm_reads.append(P_.cur(sv))
                spk = "ks" + tag + str(pr_ % 2)
                kst = q_hp[0:64, 0:2048]
                kst_w = [(sv, P_.cur(sv))]
                if pr_ >= 2:
                    kst_w.append((spk, P_.cur(spk)))
                P_.op("v", lambda e, kst=kst: e.tensor_copy(
                    kst, lat_sb[0:64, 2048:4096]), waits=kst_w, inc=sv)
                P_.op("sp", lambda e, pr_=pr_, kst=kst: e.dma_start(
                    out=kag_i.ap()[pr_ * 64:(pr_ + 1) * 64, 0:2048], in_=kst),
                    waits=[(sv, P_.cur(sv))], inc=spk, inc_by=16)
                kst_sent = (spk, P_.cur(spk))
            else:
                spv = "vs" + tag + str(pr_ % 2)
                for tb in range(4):
                    bank6 = P_.pbank[6]
                    P_.op("t", lambda e, bank6=bank6, tb=tb: e.matmul(
                        bank6[0:64, 0:512], kv2lT2,
                        q_hp[:, tb * 512:(tb + 1) * 512],
                        start=True, stop=True),
                        waits=[(sv, P_.cur(sv))] + ([(sde, P_.cur(sde))] if tb == 0 else []),
                        inc=st_)
                    P_.op("v", lambda e, bank6=bank6: e.tensor_scalar(
                        sq[0:64, 0:512], bank6[0:64, 0:512],
                        bias_sb[0:64, 25:26], None, ALU.add),
                        waits=[(st_, P_.cur(st_))], inc=sv)
                    vsq = P_.cur(sv)
                    for tt4 in range(4):
                        tt = tb * 4 + tt4
                        for hx in range(2):
                            bank = P_.pbank[7]
                            tw = [(sv, vsq)]
                            if not (tb == 0 and tt4 == 0 and hx == 0):
                                tw.append((sv, P_.cur(sv)))
                            elif pr_ == 0:
                                tw.append((sde, P_.cur(sde)))
                            P_.op("t", lambda e, tt4=tt4, hx=hx, bank=bank: e.transpose(
                                bank[0:128, 0:32],
                                sq[hx * 32:(hx + 1) * 32, tt4 * 128:(tt4 + 1) * 128],
                                ident[hx * 32:(hx + 1) * 32, hx * 32:(hx + 1) * 32]),
                                waits=tw, inc=st_)
                            dst = arena3[:, hx * 1056 + tt * 33: hx * 1056 + tt * 33 + 32]
                            vtw = [(st_, P_.cur(st_))]
                            if tb == 0 and tt4 == 0 and hx == 0:
                                vtw.append((sv, ones_done))
                                if pr_ >= 1:
                                    vtw.append(("vs" + tag + "0", P_.cur("vs" + tag + "0")))
                                    vtw.append(("vs" + tag + "1", P_.cur("vs" + tag + "1")))
                            P_.op("v", lambda e, dst=dst, bank=bank: e.tensor_copy(
                                dst, bank[0:128, 0:32]), waits=vtw, inc=sv)
                for hx in range(2):
                    h = 2 * pr_ + hx
                    P_.op("sp", lambda e, h=h, hx=hx: e.dma_start(
                        out=vag_i.ap()[h * 128:(h + 1) * 128, 0:528],
                        in_=arena3[:, hx * 1056: hx * 1056 + 528]),
                        waits=([(sv, P_.cur(sv))] if hx == 0 else ()),
                        inc=spv, inc_by=16)
        kw_done_st.append(P_.cur(st_))
    if _npr < 8:
        P_.op("v", lambda e: e.engine_nop(),
              waits=[(sv, P_.cur(sv)), (ss, P_.cur(ss)), (st_, 0)], inc="done" + tag)
        return "done" + tag
    for h in range(16):
        src = mxrow[0:1, 64 + h * 4: 64 + h * 4 + 4]
        P_.op("v", lambda e, src=src, h=h: e.tensor_reduce(
            mxrow[0:1, 128 + h:129 + h], src, AX.X, ALU.max),
            waits=([(sv, P_.cur(sv))] if h == 0 else []), inc=sv)
    P_.op("v", lambda e: e.memset(den_sb[:], 0.0), waits=[(sv, P_.cur(sv))], inc=sv)
    P_.op("v", lambda e: e.tensor_copy(den_sb[0:1, 0:16], mxrow[0:1, 128:144]),
          waits=[(sv, P_.cur(sv))], inc=sv)
    P_.op("g", lambda e: e.dma_start(out=kag_i.ap()[512:513, 0:1024],
                                     in_=den_sb[0:1, 0:1024]),
          waits=[(sv, P_.cur(sv))], inc=ldg, inc_by=16)
    P_.op("g", lambda e: e.dma_start(out=kag_i.ap()[512:513, 1024:2048],
                                     in_=den_sb[0:1, 0:1024]), inc=ldg, inc_by=16)

    cc = "cc" + tag
    P_.op("g", lambda e: e.collective_compute(
        "AllGather", ALU.bypass, replica_groups=groups,
        ins=[kag_i.ap().opt()], outs=[kag_o.ap().opt()]),
        waits=[("ks" + tag + "0", P_.cur("ks" + tag + "0")),
               ("ks" + tag + "1", P_.cur("ks" + tag + "1")),
               (ldg, P_.cur(ldg))], inc=cc)
    P_.op("g", lambda e: e.collective_compute(
        "AllGather", ALU.bypass, replica_groups=groups,
        ins=[vag_i.ap().opt()], outs=[vag_o.ap().opt()]),
        waits=[("vs" + tag + "0", P_.cur("vs" + tag + "0")),
               ("vs" + tag + "1", P_.cur("vs" + tag + "1"))], inc=cc)
    cc_done = P_.cur(cc)

    cm = "cm" + tag
    P_.op("g", lambda e: e.dma_start(out=arena1[:], in_=io["cmask16"]),
          inc=cm, inc_by=16)
    cm_done = P_.cur(cm)
    kx = "kx" + tag
    for r in range(4):
        P_.op("g", lambda e, r=r: e.dma_start(
            out=kg_sb[0:1, r * 16:(r + 1) * 16],
            in_=kag_o.ap()[r * 513 + 512: r * 513 + 513, 0:16]),
            waits=([(cc, cc_done)] if r == 0 else ()), inc=kx, inc_by=16)
    kmg = mxrow[0:1, 144:160]
    P_.op("v", lambda e: e.tensor_reduce(
        kmg, kg_sb[0:1, 0:64].rearrange("o (r h) -> o h r", r=4), AX.X, ALU.max),
        waits=[(kx, P_.cur(kx))], inc=sv)
    kmg_done = P_.cur(sv)

    import os as _os1
    if False:
        P_.op("v", lambda e: e.engine_nop(), waits=[(sv, P_.cur(sv)), (cm, cm_done)],
              inc="done" + tag)
        return "done" + tag
    sp2 = "st2" + tag
    flash_done_st = []
    for pr_ in range(8):
        stw = [(st_, flash_done_st[-1])] if flash_done_st else [(cc, cc_done)]
        nk = 0
        for r in range(4):
            for lb in range(4):
                gq = r + 4 * lb
                P_.op("sp", lambda e, r=r, lb=lb, gq=gq, pr_=pr_: e.dma_start(
                    out=arena2[0:64, gq * 512:(gq + 1) * 512],
                    in_=kag_o.ap()[r * 513 + pr_ * 64: r * 513 + (pr_ + 1) * 64,
                                   lb * 512:(lb + 1) * 512]),
                    waits=(stw if nk == 0 else ()), inc=sp2, inc_by=16)
                nk += 1
                for hx in range(2):
                    h = 2 * pr_ + hx
                    P_.op("sp", lambda e, r=r, lb=lb, gq=gq, h=h, hx=hx: e.dma_start(
                        out=arena3[:, hx * 2112 + gq * 132: hx * 2112 + (gq + 1) * 132],
                        in_=vag_o.ap()[r * 2048 + h * 128: r * 2048 + (h + 1) * 128,
                                       lb * 132:(lb + 1) * 132]),
                        inc=sp2, inc_by=16)
        strips_rdy = (sp2, P_.cur(sp2))

        wsem = "qw" + tag + "0"
        qlw = [(st_, P_.cur(st_)), (sv, P_.cur(sv)), (sde, P_.cur(sde)),
               (sxp, P_.cur(sxp)), (smk, P_.cur(smk))]
        for dc in range(NDC):
            P_.op("g", lambda e, dc=dc, pr_=pr_: e.dma_start(
                out=qw_sb[:, dc * 128:(dc + 1) * 128],
                in_=caw_re[dc, :, pr_ * 128:(pr_ + 1) * 128]),
                waits=(qlw if dc == 0 else ()), inc=wsem, inc_by=16)
        qw_rdy = (wsem, P_.cur(wsem))
        qev = []
        for tb in range(4):
            bank = P_.pbank[tb % 2]
            for dc in range(NDC):
                mw = []
                if dc == 0:
                    mw = [qw_rdy, (sxp, P_.cur(sxp))]
                    if tb >= 2:
                        mw.append((sv, qev[tb - 2]))
                P_.op("t", lambda e, bank=bank, dc=dc, tb=tb: e.matmul(
                    bank[:], qw_sb[:, dc * 128:(dc + 1) * 128],
                    xn_fm[:, dc * TPC + tb * 512: dc * TPC + (tb + 1) * 512],
                    start=(dc == 0), stop=(dc == NDC - 1)), waits=mw, inc=st_)
            mmc = P_.cur(st_)
            bc = bias_sb[:, pr_:pr_ + 1]
            P_.op("v", lambda e, bank=bank, tb=tb, bc=bc: e.tensor_scalar(
                q_hp[:, tb * 512:(tb + 1) * 512], bank[:], bc, None, ALU.add),
                waits=[(st_, mmc)], inc=sv)
            qev.append(P_.cur(sv))
        qfm_done = P_.cur(sv)
        for tb in range(4):
            bank = P_.pbank[6]
            P_.op("t", lambda e, bank=bank, tb=tb: e.matmul(
                bank[0:64, 0:512], q2lT2, q_hp[:, tb * 512:(tb + 1) * 512],
                start=True, stop=True),
                waits=[(sv, P_.cur(sv))] + ([(sde, P_.cur(sde))] if tb == 0 else []),
                inc=st_)
            P_.op("v", lambda e, bank=bank, tb=tb: e.tensor_scalar(
                lat_sb[0:64, tb * 512:(tb + 1) * 512], bank[0:64, 0:512],
                bias_sb[0:64, 24:25], None, ALU.add),
                waits=[(st_, P_.cur(st_))], inc=sv)
        lat_done = P_.cur(sv)
        for tb in range(4):
            bank = P_.pbank[7]
            P_.op("t", lambda e, bank=bank, tb=tb: e.matmul(
                bank[0:64, 0:512], RT2, lat_sb[0:64, tb * 512:(tb + 1) * 512],
                start=True, stop=True),
                waits=[(sv, P_.cur(sv))] + ([(sde, P_.cur(sde))] if tb == 0 else []),
                inc=st_)
            rmm = P_.cur(st_)
            t1 = lat_sb[0:64, 2048 + tb * 512: 2048 + (tb + 1) * 512]
            P_.op("v", lambda e, t1=t1, tb=tb: e.tensor_mul(
                t1, lat_sb[0:64, tb * 512:(tb + 1) * 512],
                cos4[0:64, tb * 512:(tb + 1) * 512]),
                waits=[(st_, rmm)], inc=sv)
            t2 = pt_sb[0:64, 512:1024]
            P_.op("v", lambda e, t2=t2, bank=bank, tb=tb: e.tensor_mul(
                t2, bank[0:64, 0:512], sin4[0:64, tb * 512:(tb + 1) * 512]),
                waits=[(sv, P_.cur(sv))], inc=sv)
            P_.op("v", lambda e, t1=t1, t2=t2: e.tensor_add(t1, t1, t2),
                  waits=[(sv, P_.cur(sv))], inc=sv)
        qrope_done = P_.cur(sv)
        for tb in range(4):
            bank = P_.pbank[6]
            P_.op("t", lambda e, bank=bank, tb=tb: e.matmul(
                bank[0:64, 0:512], wqkT2,
                lat_sb[0:64, 2048 + tb * 512: 2048 + (tb + 1) * 512],
                start=True, stop=True),
                waits=[(sv, P_.cur(sv))] + ([(sde, P_.cur(sde))] if tb == 0 else []),
                inc=st_)
            P_.op("v", lambda e, bank=bank, tb=tb: e.tensor_scalar(
                lat_sb[0:64, 0 + tb * 512: 0 + (tb + 1) * 512],
                bank[0:64, 0:512], bias_sb[0:64, 26:27], RT32, ALU.add, ALU.mult),
                waits=[(st_, P_.cur(st_))], inc=sv)
            P_.op("s", lambda e, tb=tb: e.activation(
                sq[0:64, 0:512], lat_sb[0:64, 0 + tb * 512: 0 + (tb + 1) * 512],
                AF.Square), waits=[(sv, P_.cur(sv)), (ss, P_.cur(ss))], inc=ss)
            nw = [(ss, P_.cur(ss)), (sde, P_.cur(sde)), (sv, P_.cur(sv))]
            P_.op("t", lambda e: e.matmul(P_.pbank[6][0:1, 0:512],
                                          hmask_sb[0:64, 0:1], sq[0:64, 0:512],
                                          start=True, stop=True), waits=nw, inc=st_)
            P_.op("t", lambda e: e.matmul(P_.pbank[7][0:1, 0:512],
                                          hmask_sb[0:64, 1:2], sq[0:64, 0:512],
                                          start=True, stop=True), inc=st_)
            nc_ = P_.cur(st_)
            P_.op("v", lambda e, tb=tb: e.tensor_reduce(
                mxrow[0:1, tb:tb + 1], P_.pbank[6][0:1, 0:512], AX.X, ALU.max),
                waits=[(st_, nc_)], inc=sv)
            P_.op("v", lambda e, tb=tb: e.tensor_reduce(
                mxrow[0:1, 4 + tb:5 + tb], P_.pbank[7][0:1, 0:512], AX.X, ALU.max),
                inc=sv)
            norm_reads.append(P_.cur(sv))
        P_.op("v", lambda e: e.tensor_reduce(
            mxrow[0:1, 8:9], mxrow[0:1, 0:4], AX.X, ALU.max),
            waits=[(sv, P_.cur(sv)), (sv, kmg_done)], inc=sv)
        P_.op("v", lambda e: e.tensor_reduce(
            mxrow[0:1, 9:10], mxrow[0:1, 4:8], AX.X, ALU.max), inc=sv)
        P_.op("v", lambda e, pr_=pr_: e.tensor_mul(
            mxrow[0:1, 10:12], mxrow[0:1, 8:10], kmg[0:1, 2 * pr_:2 * pr_ + 2]),
            waits=[(sv, P_.cur(sv))], inc=sv)
        P_.op("s", lambda e: e.activation(mxrow[0:1, 12:14], mxrow[0:1, 10:12],
                                          AF.Ln), waits=[(sv, P_.cur(sv))], inc=ss)
        P_.op("s", lambda e: e.activation(mxrow[0:1, 14:16], mxrow[0:1, 12:14],
                                          AF.Exp, scale=0.5),
              waits=[(ss, P_.cur(ss))], inc=ss)
        P_.op("v", lambda e: e.tensor_scalar(mxrow[0:1, 16:18], mxrow[0:1, 14:16],
                                             -1.0, None, ALU.mult),
              waits=[(ss, P_.cur(ss))], inc=sv)
        P_.op("t", lambda e: e.matmul(P_.pbank[6][:, 0:2], rows_sb[0:1, 256:384],
                                      mxrow[0:1, 16:18], start=True, stop=True),
              waits=[(sv, P_.cur(sv)), (sde, P_.cur(sde))], inc=st_)
        P_.op("v", lambda e: e.tensor_copy(negmc[:], P_.pbank[6][:, 0:2]),
              waits=[(st_, P_.cur(st_))], inc=sv)
        negm_done = P_.cur(sv)

        for qb in range(4):
            nkt = 16 * (qb + 1)
            pvA, pvB = P_.pbank[4], P_.pbank[5]
            exps_d = {}

            def emit_pv(kt, qb=qb, nkt=nkt, pvA=pvA, pvB=pvB):
                ptA = pt_sb[:, (kt % 2) * 512:(kt % 2) * 512 + 512]
                ptB = pt_sb[:, 1024 + (kt % 2) * 512: 1024 + (kt % 2) * 512 + 512]
                wloc = kt - 16 * qb
                masked = wloc >= 0
                if masked:
                    cmt = arena1[:, wloc * 512:(wloc + 1) * 512]
                    P_.op("v", lambda e, ptA=ptA, cmt=cmt: e.tensor_mul(
                        ptA, ptA, cmt), waits=[(sxp, exps_d[kt])], inc=smk)
                    P_.op("v", lambda e, ptB=ptB, cmt=cmt: e.tensor_mul(
                        ptB, ptB, cmt), inc=smk)
                pvw = [(smk, P_.cur(smk))] if masked else [(sxp, exps_d[kt])]
                if kt == 0:
                    pvw.append((sde, P_.cur(sde)))
                P_.op("t", lambda e, kt=kt, ptA=ptA, nkt=nkt: e.matmul(
                    pvA[0:33, 0:512], arena3[:, kt * 33:(kt + 1) * 33], ptA,
                    start=(kt == 0), stop=(kt == nkt - 1)), waits=pvw, inc=st_)
                P_.op("t", lambda e, kt=kt, ptB=ptB, nkt=nkt: e.matmul(
                    pvB[0:33, 0:512], arena3[:, 2112 + kt * 33: 2112 + (kt + 1) * 33],
                    ptB, start=(kt == 0), stop=(kt == nkt - 1)), inc=st_)

            for kt in range(nkt):
                sA = P_.pbank[kt % 2]
                sB = P_.pbank[2 + kt % 2]
                mw = [strips_rdy, (sv, negm_done), (cm, cm_done)]
                if kt >= 2:
                    mw = [(sxp, P_.cur(sxp) - 2)]
                elif qb > 0:
                    mw.append((sxp, P_.cur(sxp)))
                P_.op("t", lambda e, sA=sA, kt=kt, qb=qb: e.matmul(
                    sA[:], arena2[0:32, kt * 128:(kt + 1) * 128],
                    lat_sb[0:32, qb * 512:(qb + 1) * 512],
                    start=True, stop=True), waits=mw, inc=st_)
                P_.op("t", lambda e, sB=sB, kt=kt, qb=qb: e.matmul(
                    sB[:], arena2[32:64, kt * 128:(kt + 1) * 128],
                    lat_sb[32:64, qb * 512:(qb + 1) * 512],
                    start=True, stop=True, tile_position=(32, 0)), inc=st_)
                smm = P_.cur(st_)
                ptA = pt_sb[:, (kt % 2) * 512:(kt % 2) * 512 + 512]
                ptB = pt_sb[:, 1024 + (kt % 2) * 512: 1024 + (kt % 2) * 512 + 512]
                ew = [(st_, smm)]
                if kt < 2:
                    ew.append((sde, P_.cur(sde)))
                P_.op("s", lambda e, ptA=ptA, sA=sA: e.activation(
                    ptA, sA[:], AF.Exp, bias=negmc[:, 0:1]),
                    waits=ew, inc=sxp)
                P_.op("s", lambda e, ptB=ptB, sB=sB: e.activation(
                    ptB, sB[:], AF.Exp, bias=negmc[:, 1:2]), inc=sxp)
                exps_d[kt] = P_.cur(sxp)
                if kt >= 1:
                    emit_pv(kt - 1)
            emit_pv(nkt - 1)
            sweep_done = P_.cur(st_)
            denw = [(st_, sweep_done)]
            if pr_ == 0 and qb == 0:
                denw.append((ldg, P_.cur(ldg)))
            P_.op("v", lambda e: e.tensor_copy(den_sb[0:1, 0:512],
                                               pvA[32:33, 0:512]),
                  waits=denw, inc=sde)
            P_.op("v", lambda e: e.tensor_copy(den_sb[0:1, 512:1024],
                                               pvB[32:33, 0:512]), inc=sde)
            P_.op("v", lambda e: e.reciprocal(den_sb[0:1, 0:512],
                                              den_sb[0:1, 0:512]),
                  waits=[(sde, P_.cur(sde))], inc=sde)
            P_.op("v", lambda e: e.reciprocal(den_sb[0:1, 512:1024],
                                              den_sb[0:1, 512:1024]), inc=sde)
            rd = P_.cur(sde)
            P_.op("t", lambda e: e.matmul(P_.pbank[6][0:64, 0:512],
                                          rows_sb[0:1, 0:64],
                                          den_sb[0:1, 0:512],
                                          start=True, stop=False),
                  waits=[(sde, rd), (sv, P_.cur(sv))], inc=st_)
            P_.op("t", lambda e: e.matmul(P_.pbank[6][0:64, 0:512],
                                          rows_sb[0:1, 128:192],
                                          den_sb[0:1, 512:1024],
                                          start=False, stop=True), inc=st_)
            exm = P_.cur(st_)
            yl = pt_sb[0:64, 512:1024]
            P_.op("v", lambda e, yl=yl: e.tensor_copy(yl[0:32, :], pvA[0:32, 0:512]),
                  waits=[(st_, exm)], inc=sde)
            P_.op("v", lambda e, yl=yl: e.tensor_copy(yl[32:64, :], pvB[0:32, 0:512]),
                  inc=sde)
            P_.op("v", lambda e, yl=yl: e.tensor_mul(yl, yl, P_.pbank[6][0:64, 0:512]),
                  waits=[(sde, P_.cur(sde))], inc=sde)
            yl_done = P_.cur(sde)
            P_.op("t", lambda e, yl=yl: e.matmul(
                P_.pbank[7][:, 0:512], l2o_sb[0:64, 0:128], yl,
                start=True, stop=True),
                waits=[(sde, yl_done), (ldg, ldg_done)], inc=st_)
            lmm = P_.cur(st_)
            asl = attn[:, pr_ * 2048 + qb * 512: pr_ * 2048 + qb * 512 + 512]
            P_.op("v", lambda e, asl=asl: e.tensor_scalar(
                asl, P_.pbank[7][:, 0:512], bias_sb[:, 27:28], None, ALU.add),
                waits=[(st_, lmm)], inc=sde)
        flash_done_st.append(P_.cur(st_))

    import os as _os2
    if False:
        P_.op("v", lambda e: e.engine_nop(), waits=[(sde, P_.cur(sde))],
              inc="done" + tag)
        return "done" + tag
    ow = "ow" + tag
    obvb_sb = arena3[0:1, 0:1024]
    P_.op("g", lambda e: e.dma_start(out=obvb_sb, in_=io["cprojb_row"]),
          waits=[(st_, flash_done_st[-1])], inc=ow, inc_by=16)
    for hp in range(8):
        for half in range(2):
            P_.op("g", lambda e, hp=hp, half=half: e.dma_start(
                out=arena2[:, hp * 1024 + half * 512: hp * 1024 + half * 512 + 512],
                in_=cpw_re[hp, :, half * 512:(half + 1) * 512]),
                waits=([(st_, flash_done_st[-1])] if hp == 0 and half == 0 else ()),
                inc=ow, inc_by=16)
    ow_rdy = (ow, P_.cur(ow))
    oe = "oe" + tag
    oe_b = P_.curb(oe, "v")
    for tt in range(NTT):
        for mb in range(2):
            gi = tt * 2 + mb
            bank = P_.pbank[6 + gi % 2]
            for hp in range(8):
                mw = []
                if hp == 0:
                    mw = [ow_rdy, (sde, P_.cur(sde))]
                    if gi >= 2:
                        mw.append((oe, oe_b + gi - 1))
                P_.op("t", lambda e, bank=bank, hp=hp, tt=tt, mb=mb: e.matmul(
                    bank[:], attn[:, hp * 2048 + tt * 128: hp * 2048 + tt * 128 + 128],
                    arena2[:, hp * 1024 + mb * 512: hp * 1024 + mb * 512 + 512],
                    start=(hp == 0), stop=False), waits=mw, inc=st_)
            P_.op("t", lambda e, bank=bank, mb=mb: e.matmul(
                bank[:], onesbf[:], obvb_sb[0:1, mb * 512:(mb + 1) * 512],
                start=False, stop=True), inc=st_)
            xsl = x_sb[:, tt * 1024 + mb * 512: tt * 1024 + mb * 512 + 512]
            P_.op("v", lambda e, xsl=xsl, bank=bank: e.tensor_add(
                xsl, xsl, bank[:]), waits=[(st_, P_.cur(st_))], inc=oe)
    import os as _os
    if False:
        P_.op("sp", lambda e: e.dma_start(out=io["dbg_attn"], in_=attn[:]),
              waits=[(oe, P_.cur(oe))], inc="dbg" + tag, inc_by=16)
        P_.op("sp", lambda e: e.dma_start(out=io["dbg_qfk"], in_=lat_sb[0:64, 0:2048]),
              inc="dbg" + tag, inc_by=16)
        P_.op("g", lambda e: e.dma_start(out=io["dbg_kag"], in_=kag_o.ap()),
              inc="dbgg" + tag, inc_by=16)
        P_.op("v", lambda e: e.engine_nop(),
              waits=[(oe, P_.cur(oe)), ("dbg" + tag, P_.cur("dbg" + tag)),
                     ("dbgg" + tag, P_.cur("dbgg" + tag))], inc="done" + tag)
        return "done" + tag
    P_.op("v", lambda e: e.engine_nop(),
          waits=[(oe, P_.cur(oe)), (sv, P_.cur(sv)), (ss, P_.cur(ss))],
          inc="done" + tag)
    return "done" + tag


def build_l1b_launch():
    pr = P("l1b")
    nc = pr.nc
    x_d = nc.dram_tensor("x_in", [TPC, D], F32, kind="ExternalInput").ap()
    consts_d = nc.dram_tensor("consts_in", [128, NCONST], F32, kind="ExternalInput").ap()
    ident_d = nc.dram_tensor("ident_in", [128, 128], F32, kind="ExternalInput").ap()
    io = {}
    for nm, shp in [("cattn2d", [1024, 3072]), ("cproj2d", [1024, 1024]),
                    ("cprojb_row", [1, 1024]), ("hmask32", [128, 2]),
                    ("row32A", [1, 128]), ("row32B", [1, 128]),
                    ("rowOnes", [1, 128]), ("cmask16", [128, 16 * 512]),
                    ("blkw", [128, 256]), ("l2oT2", [64, 128]),
                    ("bias_cols", [128, 32]), ("ropeT", [128, 4096])]:
        io[nm] = nc.dram_tensor(nm, shp, F32, kind="ExternalInput").ap()
    io["kag_i"] = nc.dram_tensor("kag_i", [513, 2048], BF16, kind="Internal")
    io["vag_i"] = nc.dram_tensor("vag_i", [2048, 528], BF16, kind="Internal")
    io["kag_o"] = nc.dram_tensor("kag_o", [2052, 2048], BF16, kind="Internal")
    io["vag_o"] = nc.dram_tensor("vag_o", [8192, 528], BF16, kind="Internal")
    xo_d = nc.dram_tensor("x_out", [TPC, D], F32, kind="ExternalOutput").ap()
    import os as _os
    if False:
        io["dbg_attn"] = nc.dram_tensor("dbg_attn", [128, 16384], BF16, kind="ExternalOutput").ap()
        io["dbg_qfk"] = nc.dram_tensor("dbg_qfk", [64, 2048], BF16, kind="ExternalOutput").ap()
        io["dbg_kag"] = nc.dram_tensor("dbg_kag", [2052, 2048], BF16, kind="ExternalOutput").ap()
    pr.alloc_pbanks(8)
    x_sb, consts, ident = emit_load_basics(pr, x_d, consts_d, ident_d)
    dsem = emit_l1b(pr, "M", x_sb, consts, ident, io)
    emit_store_x(pr, x_sb, xo_d, waits=[(dsem, 1)])
    return pr.finish(), pr.n_ops


def rope_tables(Tn=8192, L=32):
    inv = 1.0 / 10000.0 ** (np.arange(0, L, 2, dtype=np.float32) / L)
    t = np.arange(Tn, dtype=np.float32)
    fr = np.outer(t, inv)
    cos = np.repeat(np.cos(fr), 2, axis=-1)[:, :L]
    sin = np.repeat(np.sin(fr), 2, axis=-1)[:, :L]
    return cos, sin


def l1b_host_inputs(inp, core):
    f32 = np.float32
    caw = np.asarray(inp["L1_attn_b_c_attn_w"], f32)
    cab = np.asarray(inp["L1_attn_b_c_attn_b"], f32)
    q2l = np.asarray(inp["L1_attn_b_q2l_w"], f32)
    q2lb = np.asarray(inp["L1_attn_b_q2l_b"], f32)
    kv2l = np.asarray(inp["L1_attn_b_kv2l_w"], f32)
    kv2lb = np.asarray(inp["L1_attn_b_kv2l_b"], f32)
    l2o = np.asarray(inp["L1_attn_b_l2o_w"], f32)
    l2ob = np.asarray(inp["L1_attn_b_l2o_b"], f32)
    wqk = np.asarray(inp["L1_attn_b_wqk_w"], f32)
    wqkb = np.asarray(inp["L1_attn_b_wqk_b"], f32)
    cpw = np.asarray(inp["L1_attn_b_cproj_w"], f32)
    cpb = np.asarray(inp["L1_attn_b_cproj_b"], f32)
    hm = np.zeros((128, 2), f32)
    hm[0:32, 0] = 1.0
    hm[32:64, 1] = 1.0
    r32A = np.zeros((1, 128), f32); r32A[0, 0:32] = 1.0
    r32B = np.zeros((1, 128), f32); r32B[0, 32:64] = 1.0
    rowOnes = np.ones((1, 128), f32)
    j = core % 4
    jj = np.arange(128)[:, None]
    ii = np.arange(512)[None, :]
    cmv = np.zeros((128, 16 * 512), f32)
    for w in range(16):
        cmv[:, w * 512:(w + 1) * 512] = (128 * w + jj <= 512 * j + ii)
    blkw = np.zeros((128, 256), f32)
    q2lT = q2l.T
    kv2lT = kv2l.T
    blkw[0:64, 0:32] = q2lT
    blkw[64:128, 32:64] = q2lT
    blkw[0:64, 64:96] = kv2lT
    blkw[64:128, 96:128] = kv2lT
    R = np.zeros((32, 32), f32)
    for i_ in range(16):
        R[2 * i_, 2 * i_ + 1] = -1.0
        R[2 * i_ + 1, 2 * i_] = 1.0
    RT = R.T
    blkw[0:32, 128:160] = RT
    blkw[32:64, 160:192] = RT
    wqkT = wqk.T
    blkw[0:32, 192:224] = wqkT
    blkw[32:64, 224:256] = wqkT
    l2oT2 = np.zeros((64, 128), f32)
    l2oT = l2o.T
    l2oT2[0:32, 0:64] = l2oT
    l2oT2[32:64, 64:128] = l2oT
    bias_cols = np.zeros((128, 32), f32)
    for p in range(8):
        bias_cols[:, p] = cab[p * 128:(p + 1) * 128]
        bias_cols[:, 8 + p] = cab[1024 + p * 128: 1024 + (p + 1) * 128]
        bias_cols[:, 16 + p] = cab[2048 + p * 128: 2048 + (p + 1) * 128]
    bias_cols[0:32, 24] = q2lb; bias_cols[32:64, 24] = q2lb
    bias_cols[0:32, 25] = kv2lb; bias_cols[32:64, 25] = kv2lb
    bias_cols[0:32, 26] = wqkb; bias_cols[32:64, 26] = wqkb
    bias_cols[0:64, 27] = l2ob; bias_cols[64:128, 27] = l2ob
    cos, sin = rope_tables()
    pos = np.concatenate([np.arange(a, b) for a, b in seq_segments(core)])
    cosl = cos[pos].T
    sinl = sin[pos].T
    ropeT = np.zeros((128, 4096), f32)
    for r in range(4):
        ropeT[r * 32:(r + 1) * 32, 0:2048] = cosl
        ropeT[r * 32:(r + 1) * 32, 2048:4096] = sinl
    return {
        "consts_in": host_consts(),
        "ident_in": np.eye(128, dtype=f32),
        "cattn2d": np.ascontiguousarray(caw.T),
        "cproj2d": np.ascontiguousarray(cpw.T),
        "cprojb_row": cpb.reshape(1, 1024),
        "hmask32": hm, "row32A": r32A, "row32B": r32B, "rowOnes": rowOnes,
        "cmask16": cmv, "blkw": blkw, "l2oT2": l2oT2, "bias_cols": bias_cols,
        "ropeT": ropeT,
    }



POOL = 202208

OVERLAY = {
    "x": 16512,
    "xn_fm": 82048,
    "attn": 114816,
    "arena1": 147584,
    "arena2": 163968,
    "arena3": 180352,
    "xn2": 197248,
    "cols": 201344,
    "consts": 201408,
    "ident": 201440,
    "onesbf": 201952,
    "xn2b": 197248,
    "identb": 228288,
    "h_fm": 114816,
    "w1buf": 147584,
    "hbuf": 147584,
    "w2buf": 163968,
    "sm_sb": 163968,
    "g_sb": 172160,
    "gbf": 180352,
    "win_sb": 184448,
    "wout_sb": 185472,
    "bout_sb": 187520,
    "onesrow": 189568,
    "ffn_small": 189824,
    "kfm": 180352,
    "vfm": 188544,
    "etile": 196736,
    "agc": 180352,
    "q_hp": POOL + 0,
    "pt_sb": POOL + 4096,
    "sq": POOL + 8192,
    "rows_sb": POOL + 10240,
    "den_sb": POOL + 11776,
    "mxrow": POOL + 15872,
    "qw_sb": POOL + 16896,
    "obvb_sb": POOL + 18944,
    "kg_sb": POOL + 20992,
    "qkb_sb": POOL + 21248,
    "bcols": POOL + 21376,
    "gjc": POOL + 21664,
    "iirow": POOL + 21920,
    "qpos": POOL + 23968,
    "lat_sb": POOL + 16896,
    "blk_sb": POOL + 25088,
    "l2o_sb": POOL + 25600,
    "bias_sb": POOL + 25856,
    "hmask_sb": POOL + 25984,
    "negmc": POOL + 26016,
    "gx": POOL + 0,
    "selwT": POOL + 4096,
    "b2_sb": POOL + 12288,
    "b1_sb": POOL + 16384,
    "gcols": POOL + 17408,
    "selw": POOL + 17536,
    "gate8": POOL + 18048,
    "eq8": POOL + 18080,
    "wred_sb": POOL + 18112,
    "wgq": POOL + 18624,
    "redT_sb": POOL + 18880,
    "vtile": POOL + 0,
    "numb": POOL + 512,
    "ctxb": POOL + 4672,
    "lnwb": POOL + 6720,
    "borow": POOL + 7488,
    "cols2": POOL + 9536,
}


def build_fused_launch(nstages=8, first=0):
    pr = P("fused")
    pr.overlay = OVERLAY
    nc = pr.nc
    x_d = nc.dram_tensor("x_in", [TPC, D], F32, kind="ExternalInput").ap()
    consts_d = nc.dram_tensor("consts_in", [128, NCONST], F32, kind="ExternalInput").ap()
    ident_d = nc.dram_tensor("ident_in", [128, 128], F32, kind="ExternalInput").ap()

    def eio(pre, ext, internal):
        d = {}
        for nm, shp in ext:
            d[nm] = nc.dram_tensor(pre + nm, shp, F32, kind="ExternalInput").ap()
        for nm, shp, dt in internal:
            d[nm] = nc.dram_tensor(pre + nm, shp, dt, kind="Internal")
        return d

    ioa = eio("a_", [("qw2d", [1024, 1024]), ("kw2d", [1024, 1024]),
                     ("vw2d", [1024, 1024]), ("ow2d", [1024, 1024]),
                     ("qkb", [128, 16]), ("vb_row", [1, 1024]),
                     ("ob_row", [1, 1024]), ("hmask", [128, 2]),
                     ("rowA", [1, 128]), ("rowB", [1, 128]),
                     ("rowOnes", [1, 128]), ("cmask16", [128, 16 * 512])],
               [("kag_i", [1025, 2048], BF16), ("vag_i", [2048, 1040], BF16),
                ("kag_o", [4100, 2048], BF16), ("vag_o", [8192, 1040], BF16)])
    iom = eio("m_", [("w1", [8, 1024, 4096]), ("w2", [8, 4096, 1024]),
                     ("wred", [128, 128]), ("wg", [8, 16]),
                     ("b1t", [128, 256]), ("b2", [8, 1024])], [])
    iob = eio("b_", [("qw2d", [1024, 1024]), ("kvw2d", [1024, 512]),
                     ("wproj2d", [1024, 1024]), ("bproj_row", [1, 1024]),
                     ("hmask", [128, 2]), ("rowA", [1, 128]), ("rowB", [1, 128]),
                     ("rowOnes", [1, 128]), ("gjcols", [128, 64]),
                     ("iirow", [1, 512]), ("qpos", [1, 4]),
                     ("alibiT", [16, 128, 2944])],
               [("kag_i", [257, 2048], BF16), ("vag_i", [512, 1040], BF16),
                ("kag_o", [1028, 2048], BF16), ("vag_o", [2048, 1040], BF16)])
    ioe = eio("e_", [("lnwb", [128, 192]), ("l1a_wq", [6, 1024, 1024]),
                     ("l1a_wk", [6, 1024, 1024]), ("l1a_wv", [6, 1024, 1024]),
                     ("l1a_wo", [6, 1024, 1024]), ("l1a_w1", [6, 1024, 4096]),
                     ("l1a_w2", [6, 4096, 1024]), ("l1a_bo", [6, 1, 1024])],
               [("l1a_ag_i", [128, 1040], F32), ("l1a_ag_o", [512, 1040], F32),
                ("l1a_xsave", [128, NTT * 1024], F32)])
    iof = eio("f_", [("win", [128, NDC * 64]), ("binp", [64, 1]),
                     ("w1p", [64, 16]), ("b1p", [64, 16]), ("w2p", [64, 16]),
                     ("b2p", [64, 1]), ("woutp", [64, 1024]),
                     ("boutp", [1, 1024])], [])
    iog = eio("g_", [("cattn2d", [1024, 3072]), ("cproj2d", [1024, 1024]),
                     ("cprojb_row", [1, 1024]), ("hmask32", [128, 2]),
                     ("row32A", [1, 128]), ("row32B", [1, 128]),
                     ("rowOnes", [1, 128]), ("cmask16", [128, 16 * 512]),
                     ("blkw", [128, 256]), ("l2oT2", [64, 128]),
                     ("bias_cols", [128, 32]), ("ropeT", [128, 4096])],
               [("kag_i", [513, 2048], BF16), ("vag_i", [2048, 528], BF16),
                ("kag_o", [2052, 2048], BF16), ("vag_o", [8192, 528], BF16)])
    xo_d = nc.dram_tensor("x_out", [TPC, D], F32, kind="ExternalOutput").ap()

    pr.alloc_pbanks(8)
    pr.sbuf("mxrow", [1, 256], F32)
    x_sb, consts, ident = emit_load_basics(pr, x_d, consts_d, ident_d)

    state = {"fw": [], "saved": {}}

    def run(key, emitfn, reuse=None):
        pr.mark(key)
        saved = pr.sem_pool_begin(state["saved"].get(reuse))
        state["saved"][key] = saved
        d = emitfn(state["fw"])
        if isinstance(d, tuple):
            dn, dv = d
        else:
            dn, dv = d, pr.cur(d)
        state["fw"] = [(pr._pool_map.get(dn, dn), dv)]

    stages = [
        ("l0a", None, lambda f: emit_l0a(pr, "Q", x_sb, consts, ident,
                                                ioa, first_waits=f)),
        ("moe", None, lambda f: emit_moe(pr, "A", x_sb, consts, ident,
                                         iom["w1"], iom["w2"], iom["wred"],
                                         iom["wg"], iom["b1t"], iom["b2"],
                                         first_waits=f)),
        ("l0b", None, lambda f: emit_l0b(pr, "W", x_sb, consts, ident,
                                                iob, first_waits=f)),
        ("moe2", "moe", lambda f: emit_moe(pr, "A", x_sb, consts, ident,
                                           iom["w1"], iom["w2"], iom["wred"],
                                           iom["wg"], iom["b1t"], iom["b2"],
                                           first_waits=f)),
        ("l1a", None, lambda f: emit_l1a(pr, "L", x_sb, consts, ident,
                                                ioe, first_waits=f)),
        ("ffn", None, lambda f: emit_l1ffn(pr, "F", x_sb, consts, ident,
                                           iof["win"], iof["binp"], iof["w1p"],
                                           iof["b1p"], iof["w2p"], iof["b2p"],
                                           iof["woutp"], iof["boutp"],
                                           first_waits=f)),
        ("l1b", None, lambda f: emit_l1b(pr, "M", x_sb, consts, ident,
                                                iog, first_waits=f)),
        ("ffn2", "ffn", lambda f: emit_l1ffn(pr, "F", x_sb, consts, ident,
                                             iof["win"], iof["binp"],
                                             iof["w1p"], iof["b1p"],
                                             iof["w2p"], iof["b2p"],
                                             iof["woutp"], iof["boutp"],
                                             first_waits=f)),
    ]
    for key, reuse, fn in stages[first:nstages]:
        run(key, fn, reuse=reuse)
    pr._pool_on = False
    pr.mark("store")
    emit_store_x(pr, x_sb, xo_d, waits=state["fw"])
    nsems = len(pr.sems)
    assert nsems <= 100, f"too many sems: {nsems}"
    return pr.finish(), pr.n_ops, nsems


def fused_host_inputs(inp, core):
    h = {"consts_in": host_consts(), "ident_in": np.eye(128, dtype=np.float32)}
    for pre, d in [("a_", l0a_host_inputs(inp, core)),
                   ("m_", moe_host_inputs(inp)),
                   ("b_", l0b_host_inputs(inp, core)),
                   ("e_", l1a_host_inputs(inp, core)),
                   ("f_", l1ffn_host_inputs(inp)),
                   ("g_", l1b_host_inputs(inp, core))]:
        for k, v in d.items():
            if k in ("consts_in", "ident_in", "consts", "ident"):
                continue
            h[pre + k] = v
    return h


_PROG = []


def kernel(**inputs):
    if not _PROG:
        _PROG.append(build_fused_launch()[0])
    nc = _PROG[0]
    x = np.asarray(inputs["L0_attn_a_x"], np.float32).reshape(B * S, D)
    xs = shard_rows(x)
    im = [dict(fused_host_inputs(inputs, c), x_in=xs[c]) for c in range(NCORES)]
    res = run_bass_kernel_spmd(nc, im, core_ids=list(range(NCORES)))
    out = [np.asarray(res.results[c]["x_out"]) for c in range(NCORES)]
    return unshard_rows(out).reshape(B, S, D)
